# version 31
# baseline (speedup 1.0000x reference)
import math
import contextlib
import numpy as np

import concourse.bass as bass
import concourse.mybir as mybir
from concourse.bass_utils import run_bass_kernel_spmd

F32 = mybir.dt.float32
BF16 = mybir.dt.bfloat16
I32 = mybir.dt.int32
AF = mybir.ActivationFunctionType
ALU = mybir.AluOpType

NCORES = 8
S = 4096
D = 1024
CH = 8
TILE = 512
NT = S // TILE
NH = 16
EPS_LN = 1e-5
EPS_ATT = 1e-6
LN16 = math.log(16.0)
C2PI = 1.0 / (2.0 * math.pi)
SQ2M = math.sqrt(2.0 / 256.0)
GSC = [math.sqrt(2.0 * g) for g in (0.5, 1.0, 2.0)]

ACT_SET_LNEXP = 6
ACT_SET_TRIG = 9


class Eng:
    def __init__(self, real):
        self.real = real
        self.last = None
        self._ctx = None
        self._name = None

    def __getattr__(self, name):
        if self.real is None:
            def dummy(*a, **k):
                self.last = "dummy"
            return dummy
        f = getattr(self.real, name)

        def wrap(*a, **k):
            self.last = f(*a, **k)
            return self.last
        return wrap


class Ctx:
    ENGS = ("q", "g", "t", "v", "s", "p")
    STREAMS = ("su", "gsu", "wc0", "wc1", "gc0", "gc1", "gs0", "gs1", "xl",
               "x1sp", "ysp", "yl", "wl", "qksp", "vsp", "kl", "vl", "x1l",
               "ql", "x1l0", "x1l1", "ql0", "ql1", "ol", "rt") \
        + tuple(f"wlc{i}" for i in range(8)) \
        + tuple(f"wlr{i}" for i in range(6)) \
        + tuple(f"wlp{i}" for i in range(8))

    def __init__(self, em, engines, sems):
        self.em = em
        self.e = engines
        self.sems = sems
        self.cnt = {n: 0 for n in self.ENGS + self.STREAMS}
        self.lastw = {}
        for n, e in engines.items():
            e._ctx = self
            e._name = n

    def dma(self, eng, stream, **kw):
        self.cnt[stream] += 16
        e = self.e[eng]
        if e.real is not None:
            ins = e.real.dma_start(**kw)
            ins.then_inc(self.sems[stream], 16)

    def mark(self, name, inc=1):
        eng = self.e[name]
        if eng.last is None:
            return
        self.cnt[name] += inc
        if eng.real is not None:
            eng.last.then_inc(self.sems[name], inc)
        eng.last = None

    def wait(self, waiter, producer, val=None):
        if val is None:
            val = self.cnt[producer]
        if val <= 0:
            return
        key = (waiter, producer)
        if self.lastw.get(key, 0) >= val:
            return
        self.lastw[key] = val
        em_name = "g" if waiter == "p" else waiter
        if self.em == em_name:
            self.e[waiter].real.wait_ge(self.sems[producer], val)

    def snap(self, name):
        return self.cnt[name]


def build_program(upto=3, debug_dump=False):
    nc = bass.Bass("TRN2", target_bir_lowering=False, debug=False)
    dt_ = nc.dram_tensor

    x_in = dt_("x", [CH, 128, S], F32, kind="ExternalInput").ap()
    conv_k = dt_("conv_k", [3, 3, 256, 1024], F32, kind="ExternalInput").ap()
    conv_b = dt_("conv_b", [3, 1024], F32, kind="ExternalInput").ap()
    rff_w = dt_("rff_w", [3, 3, 1024, 256], F32, kind="ExternalInput").ap()
    rff_b = dt_("rff_b", [3, 3, 256], F32, kind="ExternalInput").ap()
    proj_w = dt_("proj_w", [3, 768, 1024], F32, kind="ExternalInput").ap()
    proj_b = dt_("proj_b", [3, 1024], F32, kind="ExternalInput").ap()
    omega = dt_("omega", [64, 256], F32, kind="ExternalInput").ap()
    landm = dt_("landmarks", [64, 64], F32, kind="ExternalInput").ap()
    out_w = dt_("out_w", [1024, 1024], F32, kind="ExternalInput").ap()
    out_b = dt_("out_b", [1024], F32, kind="ExternalInput").ap()
    f_win = dt_("ffn_win", [1024, 64], F32, kind="ExternalInput").ap()
    f_bin = dt_("ffn_bin", [64], F32, kind="ExternalInput").ap()
    f_w1 = dt_("ffn_w1", [64, 16], F32, kind="ExternalInput").ap()
    f_b1 = dt_("ffn_b1", [64, 16], F32, kind="ExternalInput").ap()
    f_w2 = dt_("ffn_w2", [64, 16], F32, kind="ExternalInput").ap()
    f_b2 = dt_("ffn_b2", [64], F32, kind="ExternalInput").ap()
    f_wout = dt_("ffn_wout", [64, 1024], F32, kind="ExternalInput").ap()
    f_bout = dt_("ffn_bout", [1024], F32, kind="ExternalInput").ap()
    rope_c = dt_("rope_c", [128, S], F32, kind="ExternalInput").ap()
    rope_s = dt_("rope_s", [128, S], F32, kind="ExternalInput").ap()
    pmat = dt_("pmat", [128, 128], F32, kind="ExternalInput").ap()
    ident_in = dt_("ident", [128, 128], F32, kind="ExternalInput").ap()

    out = dt_("out", [CH, 128, S], F32, kind="ExternalOutput").ap()
    kvdbg = dt_("kvdbg", [128, 2, NH, 65], BF16, kind="ExternalOutput").ap() if debug_dump else None
    khdbg = dt_("khdbg", [128, CH, 512], BF16, kind="ExternalOutput").ap() if debug_dump else None
    ofdbg = dt_("ofdbg", [128, CH, 512], BF16, kind="ExternalOutput").ap() if debug_dump else None
    x2dbg = dt_("x2dbg", [128, CH, 512], F32, kind="ExternalOutput").ap() if debug_dump else None
    owdbg = dt_("owdbg", [128, 8, 8, 128], BF16, kind="ExternalOutput").ap() if debug_dump else None
    x1bdbg = dt_("x1bdbg", [128, CH, 512], F32, kind="ExternalOutput").ap() if debug_dump else None
    lndbg = dt_("lndbg", [1, 4, 512], F32, kind="ExternalOutput").ap() if debug_dump else None

    x1s = dt_("x1s", [CH, 128, S], F32, kind="Internal").ap()
    wbd = dt_("wbd", [3, 128, 144, 128], BF16, kind="Internal").ap()
    ys = dt_("ys", [CH, 128, S + 2], BF16, kind="Internal").ap()
    qs = dt_("qs", [CH, 128, S], BF16, kind="Internal").ap()

    al = lambda n, sh, dty: nc.alloc_sbuf_tensor(n, sh, dty).ap()

    winb = al("winb", [128, 8, 64], BF16)
    woutd = al("woutd", [128, 8, 128], BF16)
    om_z = al("om_z", [128, 2, 256], BF16)
    cm_z = al("cm_z", [128, 2, 256], BF16)
    lm_bd = al("lm_bd", [128, 128], BF16)
    cm_bd = al("cm_bd", [128, 128], BF16)
    onesm = al("onesm", [128, 128], BF16)
    l2d = al("l2d", [128, 64], BF16)
    pmat_b = al("pmat_b", [128, 128], BF16)
    ident_b = al("ident_b", [128, 128], BF16)
    ones_row = al("ones_row", [1, 128], BF16)
    ones_pad = al("ones_pad", [128, 128], BF16)
    ones512 = al("ones512", [128, 512], BF16)
    negln16 = al("negln16", [128, 1], F32)
    epsl = al("epsl", [128, 1], F32)
    nybias = al("nybias", [128, 1], F32)
    w1p = al("w1p", [128, 16], F32)
    b1p = al("b1p", [128, 16], F32)
    w2p = al("w2p", [128, 16], F32)
    b2c = al("b2c", [128, 1], F32)
    binc = al("binc", [128, 1], F32)
    winsum = al("winsum", [1, 64], BF16)
    bout_row = al("bout_row", [128, 1024], BF16)
    ob_row = al("ob_row", [128, 1024], BF16)
    pbv_row = al("pbv_row", [128, 1024], BF16)
    pbv_f = al("pbv_f", [1, 1024], F32)
    cbb = al("cbb", [128, 3, 8], F32)
    pbb = al("pbb", [128, 3, 8], F32)
    rbbr = al("rbbr", [128, 3, 6], F32)
    rbb = al("rbb", [128, 3, 6], F32)
    lm_f = al("lm_f", [128, 64], F32)
    lm_T = al("lm_T", [128, 64], F32)
    lsq_sb = al("lsq_sb", [128, 64], F32)
    kvbfT = al("kvbfT", [65, 256], BF16)
    kvp = al("kvp", [128, 2, NH, 65], BF16)
    stat_m = al("stat_m", [1, 512], F32)
    stat_e2 = al("stat_e2", [1, 512], F32)
    stat_m2 = al("stat_m2", [1, 512], F32)
    stat_var = al("stat_var", [1, 512], F32)
    stat_ln = al("stat_ln", [1, 512], F32)
    arow_f = al("arow_f", [1, 512], F32)
    arow_b = al("arow_b", [1, 512], BF16)
    b0row_b = al("b0row_b", [1, 512], BF16)
    dcol = al("dcol", [128, 8], F32)
    rcol = al("rcol", [128, 8], F32)
    zpad = al("zpad", [128, 8], BF16)

    ARN = 166 * 1024 + 2048
    arena = nc.alloc_sbuf_tensor("arena", [128, ARN], mybir.dt.uint8).ap()

    def carve(off, dty, shape):
        n = int(np.prod(shape))
        sz = {F32: 4, BF16: 2, I32: 4}[dty]
        sl = arena[:, off:off + n * sz].bitcast(dty)
        if len(shape) == 1:
            return sl
        spec = " ".join(f"d{i}" for i in range(len(shape)))
        kw = {f"d{i}": shape[i] for i in range(len(shape))}
        return sl.rearrange(f"p ({spec}) -> p {spec}", **kw)

    P0_xt = carve(0, F32, [CH, 512])
    P0_x1 = carve(16384, F32, [CH, 512])
    P0_ytmp = carve(32768, F32, [CH, 512])
    P0_xbf = carve(49152, BF16, [CH, 512])
    P0_xsq = carve(57344, BF16, [CH, 512])
    P0_abc = carve(65536, BF16, [512])
    P0_bbc = carve(66560, BF16, [512])
    P0_s = carve(67584, F32, [256])
    P0_h = carve(68608, BF16, [16, 256])
    P0_pp = carve(76800, BF16, [16, 256])
    P0_tr1 = carve(84992, F32, [8, 256])
    P0_tr2 = carve(93184, F32, [4, 256])
    P0_tr3 = carve(97280, F32, [2, 256])
    P0_g = carve(99328, BF16, [256])
    P0_y = carve(99840, BF16, [CH, 512])
    P0_apk = carve(108032, BF16, [256])

    def p0set(base):
        xbf = carve(base + 16384, BF16, [CH, 512])
        return dict(
            xt=carve(base + 0, F32, [CH, 512]),
            xbf=xbf,
            y=xbf,
            xsq=carve(base + 24576, BF16, [CH, 512]),
            s=carve(base + 32768, F32, [256]),
            h=carve(base + 33792, BF16, [16, 256]),
            apk=carve(base + 41984, BF16, [256]),
            abc=carve(base + 42496, BF16, [512]),
            bbc=carve(base + 43520, BF16, [512]),
            fr=carve(base + 44544, F32, [6, 512]),
            br=carve(base + 56832, BF16, [4, 512]),
        )
    P0S = [p0set(0), p0set(60928)]
    P0_pp2 = carve(121856, BF16, [16, 256])
    P0_tr1b = carve(130048, F32, [8, 256])
    P0_tr2b = carve(138240, F32, [4, 256])
    P0_tr3b = carve(142336, F32, [2, 256])
    P0_g2 = carve(144384, BF16, [256])
    P0_xts = [P0S[0]["xt"], P0S[1]["xt"], carve(144896, F32, [CH, 512])]

    A1_y = carve(0, BF16, [CH, 514])
    A1_h2 = [carve(8224, BF16, [CH, 512]), carve(147648, BF16, [CH, 512])]
    A1_z2 = [carve(16416, BF16, [6, 512]), carve(155840, BF16, [6, 512])]
    A1_preq = carve(22560, BF16, [CH, 512])
    A1_prek = carve(30752, BF16, [CH, 512])
    A1_qsb = carve(38944, BF16, [CH, 512])
    A1_ksb = carve(47136, BF16, [CH, 512])
    A1_vtm = carve(55328, BF16, [4, NH, 65])
    A1_ct = carve(63680, BF16, [512])
    A1_st = carve(64704, BF16, [512])
    A1_rot = carve(65728, BF16, [CH, 512])
    A1_u6 = carve(73920, F32, [3, 512])
    A1_n6 = carve(80064, I32, [3, 512])
    A1_n6f = carve(80064, F32, [3, 512])
    A1_mg6 = carve(86208, BF16, [3, 512])
    A1_ml6 = carve(89280, BF16, [3, 512])
    A1_wb = carve(110784, BF16, [144, 128])
    A1_ct2 = [A1_ct, carve(161984, BF16, [512])]
    A1_st2 = [A1_st, carve(163008, BF16, [512])]
    A2_ksq = carve(65728, BF16, [CH, 512])
    A2_khsq = A2_ksq
    A2_khat = carve(147648, BF16, [CH, 512])
    kvacc = carve(92352, F32, [NH, 256])[0:65]
    A2_kp = carve(108736, BF16, [4, 256])
    A2_kp2 = carve(166976, BF16, [4, 256])


    B_x1 = carve(0, F32, [CH, 512])
    B_qsb = carve(16384, BF16, [CH, 512])
    B_qsq = carve(24576, BF16, [CH, 512])
    B_qp = carve(32768, BF16, [2, 512])
    B_otm = carve(34816, BF16, [4, 128])
    B_ofm = carve(35840, BF16, [CH, 512])
    B_x2 = carve(44032, F32, [CH, 512])
    B_xbf = carve(60416, BF16, [CH, 512])
    B_xsq = carve(68608, BF16, [CH, 512])
    B_s = carve(76800, F32, [256])
    B_h = carve(77824, BF16, [16, 256])
    B_pp = carve(86016, BF16, [16, 256])
    B_tr1 = carve(94208, F32, [8, 256])
    B_tr2 = carve(102400, F32, [4, 256])
    B_tr3 = carve(106496, F32, [2, 256])
    B_g = carve(108544, BF16, [256])
    B_apk = carve(109056, BF16, [256])
    owb = carve(109632, BF16, [8, 8, 128])
    B_x1b = carve(126016, F32, [CH, 512])
    B_qsbb = carve(142400, BF16, [CH, 512])
    B_x2b = carve(150592, F32, [CH, 512])
    B_otmb = carve(166976, BF16, [4, 128])
    B_qp2 = carve(168000, BF16, [2, 512])
    kvbf2 = carve(170048, BF16, [256])

    pb = [nc.alloc_psum_tensor(f"pb{i}", [128, 512], F32).ap() for i in range(8)]
    pbh = [p.bitcast(BF16) for p in pb]

    es = contextlib.ExitStack()
    sems = {n: es.enter_context(nc.semaphore(f"sem_{n}")) for n in Ctx.ENGS + Ctx.STREAMS}

    def walk(ctx: Ctx):
        Q, G, T, V, Sc, P = (ctx.e[n] for n in ("q", "g", "t", "v", "s", "p"))
        mark, wait = ctx.mark, ctx.wait
        prod, cons = {}, {}

        def t_wait_bank(i):
            if i in cons:
                wait("t", cons[i][0], cons[i][1])

        def bank_done(i):
            prod[i] = ("t", ctx.cnt["t"])

        def use_bank(i, eng):
            if i in prod:
                wait(eng, prod[i][0], prod[i][1])

        def bank_consumed(i, eng):
            cons[i] = (eng, ctx.cnt[eng])

        def vfence():
            mark("v")
            wait("v", "v")

        def sfence():
            mark("s")
            wait("s", "s")

        def load_table(set_id):
            if ctx.em == "s":
                Sc.real.add_instruction(mybir.InstLoadActFuncSet(
                    name=nc.get_next_instruction_name(), act_func_set_id=set_id,
                    ins=[], outs=[]))

        load_table(ACT_SET_LNEXP)

        V.memset(cm_z.rearrange("p a b -> p (a b)"), 0.0)
        V.memset(cm_bd[:], 0.0)
        vfence()
        V.memset(cm_z[0:64, 0, :], -0.5)
        V.memset(cm_z[64:128, 1, :], -0.5)
        V.memset(cm_bd[0:64, 0:64], -1.0 / 64.0)
        V.memset(cm_bd[64:128, 64:128], -1.0 / 64.0)
        V.memset(onesm[:], 1.0)
        V.memset(om_z[64:128, 0, :], 0.0)
        V.memset(om_z[0:64, 1, :], 0.0)
        V.memset(ones_row[:], 1.0)
        V.memset(ones_pad[:], 0.0)
        V.memset(ones512[:], 0.0)
        V.memset(bout_row[:], 0.0)
        V.memset(ob_row[:], 0.0)
        V.memset(pbv_row[:], 0.0)
        for _k in (0, 1):
            V.memset(P0S[_k]["br"].rearrange("p a b -> p (a b)"), 0.0)
        vfence()
        V.memset(ones_pad[0:1, :], 1.0)
        V.memset(ones512[0:1, :], 1.0)
        V.memset(epsl[:], EPS_LN)
        V.memset(negln16[:], -LN16)
        V.memset(zpad[:], 0.0)
        mark("v")

        D_ = ctx.dma
        D_("q", "su", out=lm_f[0:64, :], in_=landm)
        D_("q", "su", out=lm_f[64:128, :], in_=landm)
        with nc.allow_non_contiguous_dma(reason="tiny transposed landmark load"):
            D_("q", "su", out=lm_T[0:64, :], in_=landm.transpose([1, 0]))
            D_("q", "su", out=lm_T[64:128, :], in_=landm.transpose([1, 0]))
        with nc.allow_non_contiguous_dma(reason="tiny bias loads"):
            D_("q", "su", out=cbb[:], in_=conv_b.rearrange("i (c p) -> p i c", p=128))
            D_("q", "su", out=pbb[:], in_=proj_b.rearrange("i (c p) -> p i c", p=128))
            D_("q", "su", out=rbbr[:], in_=rff_b.rearrange("i g (mm p) -> p i (g mm)", p=128))
        D_("q", "su", out=w1p[0:64, :], in_=f_w1)
        D_("q", "su", out=w1p[64:128, :], in_=f_w1)
        D_("q", "su", out=b1p[0:64, :], in_=f_b1)
        D_("q", "su", out=b1p[64:128, :], in_=f_b1)
        D_("q", "su", out=w2p[0:64, :], in_=f_w2)
        D_("q", "su", out=w2p[64:128, :], in_=f_w2)
        D_("q", "su", out=b2c[0:64, 0], in_=f_b2)
        D_("q", "su", out=b2c[64:128, 0], in_=f_b2)
        D_("q", "su", out=binc[0:64, 0], in_=f_bin)
        D_("q", "su", out=binc[64:128, 0], in_=f_bin)

        D_("g", "gsu", out=winb[:], in_=f_win.rearrange("(c p) m -> p c m", p=128))
        D_("g", "gsu", out=woutd[0:64, :, :], in_=f_wout.rearrange("m (c j) -> m c j", j=128))
        D_("g", "gsu", out=woutd[64:128, :, :], in_=f_wout.rearrange("m (c j) -> m c j", j=128))
        D_("g", "gsu", out=om_z[0:64, 0, :], in_=omega)
        D_("g", "gsu", out=om_z[64:128, 1, :], in_=omega)
        D_("g", "gsu", out=pmat_b[:], in_=pmat)
        D_("g", "gsu", out=ident_b[:], in_=ident_in)
        wait("g", "v")
        D_("g", "gsu", out=bout_row[0:1, :], in_=f_bout[None, :])
        D_("g", "gsu", out=ob_row[0:1, :], in_=out_b[None, :])
        for prj in range(3):
            for oc in range(CH):
                g4 = oc // 2
                c0 = g4 * 256 + (oc % 2) * 128
                D_("g", "wc0",
                   out=wbd[prj][:, oc:48 + oc:8, :].rearrange("p (w a) j -> p w a j", w=3),
                   in_=conv_k[prj, :, :, c0:c0 + 128].rearrange("w (a p) j -> p w a j", p=128))
            for rc in range(6):
                gam, mm = rc // 2, rc % 2
                D_("g", "wc0",
                   out=wbd[prj][:, 48 + gam * 16 + mm:48 + gam * 16 + 16 + mm:2, :],
                   in_=rff_w[prj, gam, :, mm * 128:(mm + 1) * 128].rearrange("(a p) j -> p a j", p=128))
            if prj < 2:
                for cc in range(CH):
                    D_("g", "wc0",
                       out=wbd[prj][:, 96 + cc:137 + cc:8, :],
                       in_=proj_w[prj, :, cc * 128:(cc + 1) * 128].rearrange("(f p) j -> p f j", p=128))
            else:
                for cg in range(2):
                    D_("g", "wc0",
                       out=wbd[prj][:, 96:144, :].rearrange("p (f c) j -> p f c j", f=6)[:, :, cg * 4:(cg + 1) * 4, :],
                       in_=proj_w[prj, :, cg * 512:(cg + 1) * 512].rearrange("(f p) (c j) -> p f c j", p=128, j=128))
        D_("q", "su", out=pbv_f[0:1, :], in_=proj_b[2:3, :])

        wait("v", "su")
        vfence()
        V.tensor_scalar_mul(pbv_row[0:1, :], pbv_f[0:1, :], 1.0 / SQ2M)
        V.scalar_tensor_tensor(out=b1p[:], in0=w1p[:], scalar=binc[:, 0:1], in1=b1p[:],
                               op0=ALU.mult, op1=ALU.add)
        V.tensor_scalar(out=rbb[:], in0=rbbr[:], scalar1=-C2PI, scalar2=0.25,
                        op0=ALU.mult, op1=ALU.add)
        V.tensor_mul(lsq_sb[:], lm_f[:], lm_f[:])
        vfence()
        V.reduce_sum(nybias[:, 0:1], lsq_sb[:], axis=mybir.AxisListType.X)
        vfence()
        V.tensor_scalar_mul(nybias[:, 0:1], nybias[:, 0:1], -1.0 / 64.0)
        mark("v")

        V.tensor_scalar_mul(l2d[:], lm_T[:], 2.0 / 64.0)
        V.memset(lm_bd[:], 0.0)
        vfence()
        V.tensor_copy(lm_bd[0:64, 0:64], l2d[0:64, :])
        V.tensor_copy(lm_bd[64:128, 64:128], l2d[64:128, :])
        mark("v")
        wait("t", "gsu")
        wait("t", "v")
        t_wait_bank(1)
        for c in range(CH):
            T.matmul(pb[1][0:64, 0:64], onesm[:, 0:64], winb[:, c, :],
                     start=(c == 0), stop=(c == CH - 1))
        mark("t")
        bank_done(1)
        use_bank(1, "s")
        wait("s", "t")
        Sc.copy(winsum[:], pb[1][0:1, 0:64])
        mark("s")
        bank_consumed(1, "s")

        wait("q", "v")
        with nc.allow_non_contiguous_dma(reason="edge pad columns"):
            D_("q", "ysp", out=ys[:, :, 0:1].transpose([1, 0, 2]),
               in_=zpad.rearrange("p (c e) -> p c e", e=1))
            D_("q", "ysp", out=ys[:, :, S + 1:S + 2].transpose([1, 0, 2]),
               in_=zpad.rearrange("p (c e) -> p c e", e=1))

        def ln_stats(xbf, xsq, arow_bf_dst, b0row_bf_dst, bank_m=0, bank_s=1):
            t_wait_bank(bank_m)
            t_wait_bank(bank_s)
            for c in range(CH):
                T.matmul(pb[bank_m][:, :], onesm[:, :], xbf[:, c, :],
                         start=(c == 0), stop=(c == CH - 1))
            for c in range(CH):
                T.matmul(pb[bank_s][:, :], onesm[:, :], xsq[:, c, :],
                         start=(c == 0), stop=(c == CH - 1))
            mark("t")
            bank_done(bank_m)
            bank_done(bank_s)
            use_bank(bank_m, "v")
            use_bank(bank_s, "v")
            wait("v", "t")
            V.tensor_scalar_mul(stat_m[:], pb[bank_m][0:1, :], 1.0 / float(D))
            V.tensor_scalar(out=stat_e2[:], in0=pb[bank_s][0:1, :], scalar1=1.0 / float(D),
                            scalar2=EPS_LN, op0=ALU.mult, op1=ALU.add)
            vfence()
            V.tensor_mul(stat_m2[:], stat_m[:], stat_m[:])
            vfence()
            V.tensor_sub(stat_var[:], stat_e2[:], stat_m2[:])
            mark("v")
            bank_consumed(bank_m, "v")
            bank_consumed(bank_s, "v")
            wait("s", "v")
            Sc.activation(stat_ln[:], stat_var[:], AF.Ln)
            sfence()
            Sc.activation(arow_f[:], stat_ln[:], AF.Exp, scale=-0.5)
            sfence()
            Sc.activation(arow_bf_dst[:], arow_f[:], AF.Copy)
            mark("s")
            wait("v", "s")
            V.scalar_tensor_tensor(out=b0row_bf_dst[:], in0=stat_m[:], scalar=-1.0,
                                   in1=arow_f[:], op0=ALU.mult, op1=ALU.mult)
            mark("v")

        def ffn(xbf, arow_bf, b0row_bf, xin_f32, xout_f32,
                sb_s, sb_h, sb_pp, tr1, tr2, tr3, sb_g, sb_apk,
                bank_s=2, bank_a=3, bank_o0=4, bank_o1=5):
            t_wait_bank(bank_s)
            t_wait_bank(bank_a)
            for half in range(2):
                cols = slice(half * 256, half * 256 + 256)
                po = slice(half * 64, half * 64 + 64)
                for c in range(CH):
                    T.matmul(pb[bank_s][po, 0:256], winb[:, c, :], xbf[:, c, cols],
                             start=(c == 0), stop=False)
                T.matmul(pb[bank_s][po, 0:256], winsum[:], b0row_bf[0:1, cols],
                         start=False, stop=True)
                T.matmul(pb[bank_a][po, 0:256], ones_row[0:1, 0:64], arow_bf[0:1, cols],
                         start=True, stop=True)
            mark("t")
            bank_done(bank_s)
            bank_done(bank_a)
            use_bank(bank_a, "s")
            wait("s", "t")
            Sc.copy(sb_apk, pb[bank_a][:, 0:256])
            mark("s")
            bank_consumed(bank_a, "s")
            use_bank(bank_s, "v")
            wait("v", "s")
            V.tensor_mul(sb_s, pb[bank_s][:, 0:256], sb_apk)
            mark("v")
            bank_consumed(bank_s, "v")
            wait("s", "v")
            for k in range(16):
                Sc.activation(sb_h[:, k, :], sb_s, AF.Relu,
                              bias=b1p[:, k:k + 1], scale=w1p[:, k:k + 1])
            mark("s")
            wait("v", "s")
            V.tensor_scalar(out=sb_pp[:, 0, :], in0=sb_h[:, 0, :], scalar1=w2p[:, 0:1],
                            scalar2=b2c[:, 0:1], op0=ALU.mult, op1=ALU.add)
            for k in range(1, 16):
                V.tensor_scalar_mul(sb_pp[:, k, :], sb_h[:, k, :], w2p[:, k:k + 1])
            vfence()
            for j in range(8):
                V.tensor_add(tr1[:, j, :], sb_pp[:, 2 * j, :], sb_pp[:, 2 * j + 1, :])
            vfence()
            for j in range(4):
                V.tensor_add(tr2[:, j, :], tr1[:, 2 * j, :], tr1[:, 2 * j + 1, :])
            vfence()
            for j in range(2):
                V.tensor_add(tr3[:, j, :], tr2[:, 2 * j, :], tr2[:, 2 * j + 1, :])
            vfence()
            V.tensor_add(sb_g, tr3[:, 0, :], tr3[:, 1, :])
            mark("v")
            wait("t", "v")
            for c in range(CH):
                bank = bank_o0 if (c % 2 == 0) else bank_o1
                t_wait_bank(bank)
                T.matmul(pb[bank][:, 0:256], woutd[0:64, c, :], sb_g[0:64, :],
                         start=True, stop=False)
                T.matmul(pb[bank][:, 0:256], bout_row[:, c * 128:(c + 1) * 128],
                         ones512[:, 0:256], start=False, stop=True)
                T.matmul(pb[bank][:, 256:512], woutd[64:128, c, :], sb_g[64:128, :],
                         start=True, stop=False)
                T.matmul(pb[bank][:, 256:512], bout_row[:, c * 128:(c + 1) * 128],
                         ones512[:, 0:256], start=False, stop=True)
                mark("t")
                bank_done(bank)
                use_bank(bank, "v")
                wait("v", "t")
                V.scalar_tensor_tensor(out=xout_f32[:, c, :], in0=pb[bank][:, :], scalar=0.5,
                                       in1=xin_f32[:, c, :], op0=ALU.mult, op1=ALU.add)
                mark("v")
                bank_consumed(bank, "v")

        def p0_stats_vs(banks, ra, rb):
            wait("v", "t")
            for k in (0, 1):
                bm, bs = banks[k]
                fr = P0S[k]["fr"]
                V.tensor_scalar_mul(fr[0:1, 0, :], pb[bm][0:1, :], 1.0 / float(D))
                V.tensor_scalar(out=fr[0:1, 1, :], in0=pb[bs][0:1, :],
                                scalar1=1.0 / float(D), scalar2=EPS_LN,
                                op0=ALU.mult, op1=ALU.add)
            vfence()
            for k in (0, 1):
                fr = P0S[k]["fr"]
                V.tensor_mul(fr[0:1, 2, :], fr[0:1, 0, :], fr[0:1, 0, :])
            vfence()
            for k in (0, 1):
                fr = P0S[k]["fr"]
                V.tensor_sub(fr[0:1, 3, :], fr[0:1, 1, :], fr[0:1, 2, :])
            mark("v")
            for k in (0, 1):
                bank_consumed(banks[k][0], "v")
                bank_consumed(banks[k][1], "v")
            wait("s", "v")
            for k in (0, 1):
                fr = P0S[k]["fr"]
                Sc.activation(fr[0:1, 4, :], fr[0:1, 3, :], AF.Ln)
            sfence()
            for k in (0, 1):
                fr = P0S[k]["fr"]
                Sc.activation(fr[0:1, 5, :], fr[0:1, 4, :], AF.Exp, scale=-0.5)
            sfence()
            for k in (0, 1):
                Sc.activation(P0S[k]["br"][0:1, ra, :], P0S[k]["fr"][0:1, 5, :], AF.Copy)
            mark("s")
            wait("v", "s")
            for k in (0, 1):
                V.scalar_tensor_tensor(out=P0S[k]["br"][0:1, rb, :],
                                       in0=P0S[k]["fr"][0:1, 0, :], scalar=-1.0,
                                       in1=P0S[k]["fr"][0:1, 5, :],
                                       op0=ALU.mult, op1=ALU.mult)
            mark("v")

        def p0_stats_t(banks):
            wait("t", "s")
            for k in (0, 1):
                bm, bs = banks[k]
                st_ = P0S[k]
                t_wait_bank(bm)
                t_wait_bank(bs)
                for c in range(CH):
                    T.matmul(pb[bm][:, :], onesm[:, :], st_["xbf"][:, c, :],
                             start=(c == 0), stop=(c == CH - 1))
                for c in range(CH):
                    T.matmul(pb[bs][:, :], onesm[:, :], st_["xsq"][:, c, :],
                             start=(c == 0), stop=(c == CH - 1))
                mark("t")
                bank_done(bm)
                bank_done(bs)
                use_bank(bm, "v")
                use_bank(bs, "v")

        def p0_cast(k, xt_):
            st_ = P0S[k]
            Sc.activation(st_["xbf"].rearrange("p c t -> p (c t)"),
                          xt_.rearrange("p c t -> p (c t)"), AF.Copy)
            Sc.activation(st_["xsq"].rearrange("p c t -> p (c t)"),
                          xt_.rearrange("p c t -> p (c t)"), AF.Square)

        NP = NT // 2
        for pr in range(NP):
            tA, tB = (2 * pr) * TILE, (2 * pr + 1) * TILE
            tt_of = {0: tA, 1: tB}
            slot_of = {0: (2 * pr) % 3, 1: (2 * pr + 1) % 3}
            if pr == 0:
                for k in (0, 1):
                    D_("q", "xl", out=P0_xts[k],
                       in_=x_in[:, :, tt_of[k]:tt_of[k] + TILE].transpose([1, 0, 2]))
            wait("s", "xl")
            wait("s", "t")
            wait("s", "ysp")
            for k in (0, 1):
                p0_cast(k, P0_xts[slot_of[k]])
            mark("s")
            p0_stats_t({0: (0, 1), 1: (2, 3)})
            p0_stats_vs({0: (0, 1), 1: (2, 3)}, 0, 1)
            if 2 * pr + 2 < NT:
                wait("q", "v")
                wait("q", "x1sp")
                tn = (2 * pr + 2) * TILE
                D_("q", "xl", out=P0_xts[(2 * pr + 2) % 3],
                   in_=x_in[:, :, tn:tn + TILE].transpose([1, 0, 2]))
            fbank = {0: (4, 5, 6), 1: (0, 1, 2)}
            for k in (0, 1):
                s0_, s1_, ba_ = fbank[k]
                st_ = P0S[k]
                t_wait_bank(s0_)
                t_wait_bank(s1_)
                t_wait_bank(ba_)
                for half, sb_ in ((0, s0_), (1, s1_)):
                    cols = slice(half * 256, half * 256 + 256)
                    po = slice(half * 64, half * 64 + 64)
                    for c in range(CH):
                        T.matmul(pb[sb_][po, 0:256], winb[:, c, :], st_["xbf"][:, c, cols],
                                 start=(c == 0), stop=False)
                mark("t")
            wait("t", "v")
            wait("t", "s")
            for k in (0, 1):
                s0_, s1_, ba_ = fbank[k]
                st_ = P0S[k]
                for half, sb_ in ((0, s0_), (1, s1_)):
                    cols = slice(half * 256, half * 256 + 256)
                    po = slice(half * 64, half * 64 + 64)
                    T.matmul(pb[sb_][po, 0:256], winsum[:], st_["br"][0:1, 1, cols],
                             start=False, stop=True)
                    T.matmul(pb[ba_][po, 0:256], ones_row[0:1, 0:64], st_["br"][0:1, 0, cols],
                             start=True, stop=True)
                mark("t")
                bank_done(s0_)
                bank_done(s1_)
                bank_done(ba_)
            for k in (0, 1):
                s0_, s1_, ba_ = fbank[k]
                st_ = P0S[k]
                use_bank(ba_, "s")
                wait("s", "t")
                Sc.copy(st_["apk"], pb[ba_][:, 0:256])
                mark("s")
                bank_consumed(ba_, "s")
                use_bank(s0_, "v")
                use_bank(s1_, "v")
                wait("v", "s")
                V.tensor_mul(st_["s"][0:64, :], pb[s0_][0:64, 0:256], st_["apk"][0:64, :])
                V.tensor_mul(st_["s"][64:128, :], pb[s1_][64:128, 0:256], st_["apk"][64:128, :])
                mark("v")
                bank_consumed(s0_, "v")
                bank_consumed(s1_, "v")
                wait("s", "v")
                for kk in range(16):
                    Sc.activation(st_["h"][:, kk, :], st_["s"], AF.Relu,
                                  bias=b1p[:, kk:kk + 1], scale=w1p[:, kk:kk + 1])
                mark("s")
            obank = {0: (3, 7), 1: (4, 5)}
            for k in (0, 1):
                st_ = P0S[k]
                wait("v", "s")
                V.tensor_scalar(out=P0_pp2[:, 0, :], in0=st_["h"][:, 0, :],
                                scalar1=w2p[:, 0:1], scalar2=b2c[:, 0:1],
                                op0=ALU.mult, op1=ALU.add)
                for kk in range(1, 16):
                    V.tensor_scalar_mul(P0_pp2[:, kk, :], st_["h"][:, kk, :], w2p[:, kk:kk + 1])
                vfence()
                for j in range(8):
                    V.tensor_add(P0_tr1b[:, j, :], P0_pp2[:, 2 * j, :], P0_pp2[:, 2 * j + 1, :])
                vfence()
                for j in range(4):
                    V.tensor_add(P0_tr2b[:, j, :], P0_tr1b[:, 2 * j, :], P0_tr1b[:, 2 * j + 1, :])
                vfence()
                for j in range(2):
                    V.tensor_add(P0_tr3b[:, j, :], P0_tr2b[:, 2 * j, :], P0_tr2b[:, 2 * j + 1, :])
                vfence()
                V.tensor_add(P0_g2, P0_tr3b[:, 0, :], P0_tr3b[:, 1, :])
                mark("v")
                wait("t", "v")
                b0_, b1_ = obank[k]
                for c in range(CH):
                    bank = b0_ if c % 2 == 0 else b1_
                    t_wait_bank(bank)
                    T.matmul(pb[bank][:, 0:256], woutd[0:64, c, :], P0_g2[0:64, :],
                             start=True, stop=False)
                    T.matmul(pb[bank][:, 0:256], bout_row[:, c * 128:(c + 1) * 128],
                             ones512[:, 0:256], start=False, stop=True)
                    T.matmul(pb[bank][:, 256:512], woutd[64:128, c, :], P0_g2[64:128, :],
                             start=True, stop=False)
                    T.matmul(pb[bank][:, 256:512], bout_row[:, c * 128:(c + 1) * 128],
                             ones512[:, 0:256], start=False, stop=True)
                    mark("t")
                    bank_done(bank)
                    use_bank(bank, "v")
                    wait("v", "t")
                    V.scalar_tensor_tensor(out=P0_xts[slot_of[k]][:, c, :], in0=pb[bank][:, :],
                                           scalar=0.5, in1=P0_xts[slot_of[k]][:, c, :],
                                           op0=ALU.mult, op1=ALU.add)
                    mark("v")
                    bank_consumed(bank, "v")
                wait("q", "v")
                D_("q", "x1sp", out=x1s[:, :, tt_of[k]:tt_of[k] + TILE].transpose([1, 0, 2]),
                   in_=P0_xts[slot_of[k]])
                wait("s", "v")
                p0_cast(k, P0_xts[slot_of[k]])
                mark("s")
            p0_stats_t({0: (2, 3), 1: (0, 1)})
            p0_stats_vs({0: (2, 3), 1: (0, 1)}, 2, 3)
            ybank = {0: (6, 7), 1: (2, 3)}
            wait("t", "v")
            wait("t", "s")
            for k in (0, 1):
                ba_, bb_ = ybank[k]
                st_ = P0S[k]
                t_wait_bank(ba_)
                t_wait_bank(bb_)
                T.matmul(pb[ba_][:, :], ones_pad[:, :], st_["br"][:, 2, :], start=True, stop=True)
                T.matmul(pb[bb_][:, :], ones_pad[:, :], st_["br"][:, 3, :], start=True, stop=True)
                mark("t")
                bank_done(ba_)
                bank_done(bb_)
                use_bank(ba_, "s")
                use_bank(bb_, "s")
            wait("s", "t")
            wait("s", "v")
            wait("s", "p")
            for k in (0, 1):
                ba_, bb_ = ybank[k]
                st_ = P0S[k]
                Sc.copy(st_["abc"], pb[ba_][:, :])
                Sc.copy(st_["bbc"], pb[bb_][:, :])
                mark("s")
                bank_consumed(ba_, "s")
                bank_consumed(bb_, "s")
            for k in (0, 1):
                st_ = P0S[k]
                wait("v", "s")
                for c in range(CH):
                    V.tensor_mul(st_["y"][:, c, :], P0_xts[slot_of[k]][:, c, :], st_["abc"])
                mark("v")
                wait("p", "v")
                for c in range(CH):
                    P.tensor_add(st_["y"][:, c, :], st_["y"][:, c, :], st_["bbc"])
                mark("p")
                wait("q", "p")
                D_("q", "ysp", out=ys[:, :, tt_of[k] + 1:tt_of[k] + 1 + TILE].transpose([1, 0, 2]),
                   in_=st_["y"])
                if k == 0 and 2 * pr + 3 < NT:
                    tn = (2 * pr + 3) * TILE
                    wait("q", "v")
                    wait("q", "x1sp")
                    D_("q", "xl", out=P0_xts[(2 * pr + 3) % 3],
                       in_=x_in[:, :, tn:tn + TILE].transpose([1, 0, 2]))

        if upto < 1:
            wait("q", "x1sp")
            for i in range(NT):
                t0 = i * TILE
                D_("q", "ol", out=out[:, :, t0:t0 + TILE], in_=x1s[:, :, t0:t0 + TILE])
            if ctx.em == "q":
                Q.real.wait_ge(sems["ol"], ctx.cnt["ol"])
            return

        load_table(ACT_SET_TRIG)
        wait("v", "x1sp")
        wait("v", "ysp")
        V.memset(A1_vtm[:, :, :, 64:65], 1.0)
        mark("v")

        wb_free = {"conv": 0, "rff": 0, "proj": 0}
        wl_ready = {}

        wait("g", "wc0")

        def a1_wl_convrff(prj):
            wait("g", "t", wb_free["conv"])
            D_("g", "wlc0", out=A1_wb[:, 0:48, :], in_=wbd[prj][:, 0:48, :])
            for oc in range(CH):
                wl_ready[("conv", prj, oc)] = ctx.snap("wlc0")
            wait("g", "t", wb_free["rff"])
            D_("g", "wlr0", out=A1_wb[:, 48:96, :], in_=wbd[prj][:, 48:96, :])
            for rc in range(6):
                wl_ready[("rff", prj, rc)] = ctx.snap("wlr0")
        def a1_wl_proj(prj):
            wait("g", "t", wb_free["proj"])
            D_("g", "wlp0", out=A1_wb[:, 96:144, :], in_=wbd[prj][:, 96:144, :])
            for cc in range(CH):
                wl_ready[("proj", prj, cc)] = ctx.snap("wlp0")

        def a1_tile_loads(i):
            t0 = i * TILE
            wait("q", "t")
            wait("q", "ysp")
            wait("q", "x1sp")
            wait("q", "qksp")
            D_("q", "yl", out=A1_y, in_=ys[:, :, t0:t0 + TILE + 2].transpose([1, 0, 2]))
            wait("g", "v")
            D_("g", "rt", out=A1_ct2[i % 2], in_=rope_c[:, t0:t0 + TILE])
            D_("g", "rt", out=A1_st2[i % 2], in_=rope_s[:, t0:t0 + TILE])

        a1_tile_loads(0)
        a1_wl_convrff(0)
        a1_wl_proj(0)
        h_snap = {}
        sin_snap = {}

        def a1_conv(i, prj):
            A1_h = A1_h2[prj % 2]
            for oc in range(CH):
                bank = oc % 2
                grp = oc // 2
                t_wait_bank(bank)
                wait("t", "wlc0", wl_ready[("conv", prj, oc)])
                wait("t", "yl", yl_gate[i])
                first = True
                for w in range(3):
                    for a in range(2):
                        widx = (w * 2 + a) * 8 + oc
                        T.matmul(pb[bank][:, :], A1_wb[:, widx, :],
                                 A1_y[:, grp * 2 + a, w:w + TILE],
                                 start=first, stop=(w == 2 and a == 1))
                        first = False
                mark("t")
                bank_done(bank)
                use_bank(bank, "s")
                wait("s", "t")
                wait("s", "v")
                Sc.activation(A1_h[:, oc, :], pb[bank][:, :], AF.Identity,
                              bias=cbb[:, prj, oc:oc + 1])
                mark("s")
                bank_consumed(bank, "s")
            wb_free["conv"] = ctx.snap("t")
            h_snap[prj] = ctx.snap("s")

        def a1_rff(prj):
            A1_h = A1_h2[prj % 2]
            for rc in range(6):
                bank = 2 + rc % 2
                gam = rc // 2
                t_wait_bank(bank)
                wait("t", "wlr0", wl_ready[("rff", prj, rc)])
                wait("t", "s", h_snap[prj])
                for a in range(CH):
                    widx = 48 + gam * 16 + a * 2 + (rc % 2)
                    T.matmul(pb[bank][:, :], A1_wb[:, widx, :],
                             A1_h[:, a, :], start=(a == 0), stop=(a == CH - 1))
                mark("t")
                bank_done(bank)
                use_bank(bank, "v")
                wait("v", "t")
                if rc in (0, 3):
                    wait("v", "s")
                V.tensor_scalar(out=A1_u6[:, rc % 3, :], in0=pb[bank][:, :],
                                scalar1=-GSC[gam] * C2PI,
                                scalar2=rbb[:, prj, rc:rc + 1],
                                op0=ALU.mult, op1=ALU.add)
                mark("v")
                bank_consumed(bank, "v")
                if rc == 2 or rc == 5:
                    a1_vchain_half(prj, rc // 3)
            wb_free["rff"] = ctx.snap("t")

        def a1_vchain_half(prj, half):
            A1_z = A1_z2[prj % 2]
            u6f = A1_u6.rearrange("p a b -> p (a b)")
            n6f = A1_n6f.rearrange("p a b -> p (a b)")
            mg6 = A1_mg6.rearrange("p a b -> p (a b)")
            ml6 = A1_ml6.rearrange("p a b -> p (a b)")
            vfence()
            V.tensor_copy(A1_n6.rearrange("p a b -> p (a b)"), u6f)
            vfence()
            V.tensor_copy(n6f, A1_n6.rearrange("p a b -> p (a b)"))
            vfence()
            V.tensor_sub(u6f, u6f, n6f)
            vfence()
            V.tensor_single_scalar(out=mg6, in_=u6f, scalar=0.5, op=ALU.is_gt)
            V.tensor_single_scalar(out=ml6, in_=u6f, scalar=-0.5, op=ALU.is_lt)
            vfence()
            V.tensor_sub(u6f, u6f, mg6)
            vfence()
            V.tensor_add(u6f, u6f, ml6)
            mark("v")
            wait("s", "v")
            Sc.activation(A1_z[:, half * 3:half * 3 + 3, :].rearrange("p a b -> p (a b)"),
                          u6f, AF.Sin, scale=float(2 * math.pi))
            mark("s")
            sin_snap[prj] = ctx.snap("s")

        def a1_proj(i, prj):
            A1_z = A1_z2[prj % 2]
            if prj < 2:
                pre = A1_preq if prj == 0 else A1_prek
                for cc in range(CH):
                    bank = 4 + cc % 2
                    t_wait_bank(bank)
                    wait("t", "wlp0", wl_ready[("proj", prj, cc)])
                    wait("t", "s", sin_snap[prj])
                    for f in range(6):
                        T.matmul(pb[bank][:, :], A1_wb[:, 96 + f * 8 + cc, :],
                                 A1_z[:, f, :], start=(f == 0), stop=(f == 5))
                    mark("t")
                    bank_done(bank)
                    use_bank(bank, "s")
                    wait("s", "t")
                    Sc.activation(pre[:, cc, :], pb[bank][:, :], AF.Identity,
                                  scale=SQ2M, bias=pbb[:, prj, cc:cc + 1])
                    mark("s")
                    bank_consumed(bank, "s")
            else:
                for sub in range(4):
                    for cg in range(2):
                        bank = 4 + (sub * 2 + cg) % 2
                        t_wait_bank(bank)
                        wait("t", "wlp0", wl_ready[("proj", prj, cg)])
                        wait("t", "s", sin_snap[prj])
                        for f in range(6):
                            i0 = 96 + f * 8 + cg * 4
                            T.matmul(pb[bank][:, :],
                                     A1_z[:, f, sub * 128:(sub + 1) * 128],
                                     A1_wb[:, i0:i0 + 4, :].rearrange("p n j -> p (n j)"),
                                     start=(f == 0), stop=False)
                        T.matmul(pb[bank][:, :], ones_pad[:, :],
                                 pbv_row[:, cg * 512:(cg + 1) * 512],
                                 start=False, stop=True)
                        mark("t")
                        bank_done(bank)
                        use_bank(bank, "s")
                        wait("s", "t")
                        Sc.activation(A1_vtm[:, sub, cg * 8:(cg + 1) * 8, 0:64],
                                      pb[bank][:, :].rearrange("p (h d) -> p h d", d=64),
                                      AF.Identity, scale=SQ2M)
                        mark("s")
                        bank_consumed(bank, "s")
            wb_free["proj"] = ctx.snap("t")

        yl_gate = {}
        yl_gate[0] = ctx.snap("yl")
        for i in range(NT):
            t0 = i * TILE
            if i == 0:
                a1_conv(i, 0)
            for prj in range(3):
                a1_rff(prj)
                if prj < 2 or i + 1 < NT:
                    a1_wl_convrff(prj + 1 if prj < 2 else 0)
                if prj < 2:
                    a1_conv(i, prj + 1)
                    if prj == 1 and i + 1 < NT:
                        a1_tile_loads(i + 1)
                        yl_gate[i + 1] = ctx.snap("yl")
                a1_proj(i, prj)
                if prj < 2 or i + 1 < NT:
                    a1_wl_proj(prj + 1 if prj < 2 else 0)

            for which in range(2):
                pre = A1_preq if which == 0 else A1_prek
                dst = A1_qsb if which == 0 else A1_ksb
                rtmp = A1_h2[1]
                wait("s", "v")
                for cc in range(CH):
                    bank = 6 + cc % 2
                    t_wait_bank(bank)
                    wait("t", "s")
                    T.matmul(pb[bank][:, :], pmat_b[:], pre[:, cc, :], start=True, stop=True)
                    mark("t")
                    bank_done(bank)
                    use_bank(bank, "s")
                    wait("s", "t")
                    Sc.copy(A1_rot[:, cc, :], pb[bank][:, :])
                    mark("s")
                    bank_consumed(bank, "s")
                wait("v", "s")
                wait("v", "rt")
                wait("v", "qksp")
                for cc in range(CH):
                    V.tensor_mul(rtmp[:, cc, :], pre[:, cc, :], A1_ct2[i % 2])
                for cc in range(CH):
                    V.tensor_mul(A1_rot[:, cc, :], A1_rot[:, cc, :], A1_st2[i % 2])
                vfence()
                for cc in range(CH):
                    V.tensor_add(dst[:, cc, :], rtmp[:, cc, :], A1_rot[:, cc, :])
                mark("v")
                if which == 0:
                    wait("q", "v")
                    D_("q", "qksp", out=qs[:, :, t0:t0 + TILE].transpose([1, 0, 2]),
                       in_=dst)
            load_table(ACT_SET_LNEXP)
            vfence()
            V.tensor_mul(A2_ksq.rearrange("p c t -> p (c t)"),
                         A1_ksb.rearrange("p c t -> p (c t)"),
                         A1_ksb.rearrange("p c t -> p (c t)"))
            mark("v")
            wait("t", "v")
            for cc in range(CH):
                bank = cc % 2
                t_wait_bank(bank)
                T.matmul(pb[bank][:, :], lm_bd[:, :],
                         A1_ksb[:, cc, :], start=True, stop=False)
                T.matmul(pb[bank][:, :], cm_bd[:, :],
                         A2_ksq[:, cc, :], start=False, stop=True)
                mark("t")
                bank_done(bank)
                use_bank(bank, "s")
                wait("s", "t")
                wait("s", "v")
                Sc.activation(A2_khat[:, cc, :], pb[bank][:, :],
                              AF.Exp, bias=nybias[:, 0:1])
                mark("s")
                bank_consumed(bank, "s")
            wait("v", "s")
            V.tensor_mul(A2_khsq.rearrange("p c t -> p (c t)"),
                         A2_khat.rearrange("p c t -> p (c t)"),
                         A2_khat.rearrange("p c t -> p (c t)"))
            mark("v")
            if debug_dump and i == NT - 1:
                wait("q", "s")
                D_("q", "ol", out=khdbg, in_=A2_khat[:])
            if i + 1 < NT:
                a1_conv(i + 1, 0)
            wait("t", "v")
            kp_d = [A2_kp, A2_kp2]
            s_snap = {}

            def kv_emit(hh):
                kbank = 4 + hh % 2
                t_wait_bank(kbank)
                wait("t", "s", s_snap[hh])
                for sub in range(4):
                    T.matmul(pb[kbank][0:65, 0:256], A1_vtm[:, sub, hh, :],
                             kp_d[hh % 2][:, sub, :], start=(sub == 0), stop=(sub == 3))
                mark("t")
                bank_done(kbank)
                use_bank(kbank, "v")
                wait("v", "t")
                if i == 0:
                    V.tensor_copy(kvacc[:, hh, :], pb[kbank][0:65, 0:256])
                else:
                    V.tensor_add(kvacc[:, hh, :], kvacc[:, hh, :], pb[kbank][0:65, 0:256])
                mark("v")
                bank_consumed(kbank, "v")
                if i == NT - 1:
                    kb = kvbfT if hh % 2 == 0 else kvbf2
                    wait("v", "t")
                    vfence()
                    V.tensor_copy(kb[0:65, :], kvacc[:, hh, :])
                    mark("v")
                    t_wait_bank(6)
                    wait("t", "v")
                    T.transpose(pbh[6][0:128, 0:65], kb[0:65, 0:128], ident_b[0:65, 0:65])
                    T.transpose(pbh[6][0:128, 66:131], kb[0:65, 128:256], ident_b[0:65, 0:65])
                    mark("t")
                    bank_done(6)
                    use_bank(6, "s")
                    wait("s", "t")
                    Sc.copy(kvp[:, 0, hh, :], pbh[6][0:128, 0:65])
                    Sc.copy(kvp[:, 1, hh, :], pbh[6][0:128, 66:131])
                    mark("s")
                    bank_consumed(6, "s")

            for h in range(NH):
                base, cc = (h % 2) * 64, h // 2
                for sub in range(4):
                    bank = 2 + sub % 2
                    t_wait_bank(bank)
                    T.matmul(pb[bank][:, 0:256],
                             A2_khat[:, cc, sub * 128:(sub + 1) * 128],
                             om_z[:, h % 2, :], start=True, stop=False)
                    T.matmul(pb[bank][:, 0:256],
                             A2_khsq[:, cc, sub * 128:(sub + 1) * 128],
                             cm_z[:, h % 2, :], start=False, stop=True)
                    mark("t")
                    bank_done(bank)
                    use_bank(bank, "s")
                    wait("s", "t")
                    Sc.activation(kp_d[h % 2][:, sub, :], pb[bank][:, 0:256], AF.Exp,
                                  bias=negln16[:, 0:1])
                    mark("s")
                    bank_consumed(bank, "s")
                s_snap[h] = ctx.snap("s")
                if h > 0:
                    kv_emit(h - 1)
            kv_emit(NH - 1)
            if i + 1 < NT:
                load_table(ACT_SET_TRIG)

        if upto < 2:
            wait("q", "x1sp")
            wait("q", "qksp")
            for i in range(NT):
                t0 = i * TILE
                D_("q", "ol", out=out[:, :, t0:t0 + TILE], in_=x1s[:, :, t0:t0 + TILE])
            if ctx.em == "q":
                Q.real.wait_ge(sems["ol"], ctx.cnt["ol"])
            return


        if upto < 3:
            wait("q", "x1sp")
            for i in range(NT):
                t0 = i * TILE
                D_("q", "ol", out=out[:, :, t0:t0 + TILE], in_=x1s[:, :, t0:t0 + TILE])
            if ctx.em == "q":
                Q.real.wait_ge(sems["ol"], ctx.cnt["ol"])
            return


        if debug_dump:
            wait("q", "s")
            D_("q", "ol", out=kvdbg, in_=kvp[:])
        load_table(ACT_SET_LNEXP)
        wait("g", "t")
        wait("g", "v")
        wait("g", "s")
        for _ws in range(8):
            wait("g", f"wlc{_ws}")
            wait("g", f"wlp{_ws}")
        for _ws in range(6):
            wait("g", f"wlr{_ws}")
        D_("g", "gsu", out=owb[:],
           in_=out_w.rearrange("(dc p) (cc j) -> p dc cc j", p=128, j=128))
        Bx1d = [B_x1, B_x1b]
        Bqsd = [B_qsb, B_qsbb]
        Bx2d = [B_x2, B_x2b]
        Botm = [B_otm, B_otmb]

        def b_loads(i):
            t0 = i * TILE
            if i == 0:
                wait("q", "t")
                wait("q", "v")
                wait("q", "s")
            wait("q", "x1sp")
            wait("q", "qksp")
            wait("q", "ol")
            D_("q", f"x1l{i % 2}", out=Bx1d[i % 2],
               in_=x1s[:, :, t0:t0 + TILE].transpose([1, 0, 2]))
            D_("q", f"ql{i % 2}", out=Bqsd[i % 2],
               in_=qs[:, :, t0:t0 + TILE].transpose([1, 0, 2]))

        def t_cast(j):
            x2b = Bx2d[j % 2]
            wait("p", "v")
            wait("p", "t")
            P.tensor_copy(B_xbf.rearrange("p c t -> p (c t)"),
                          x2b.rearrange("p c t -> p (c t)"))
            P.tensor_mul(B_xsq.rearrange("p c t -> p (c t)"),
                         x2b.rearrange("p c t -> p (c t)"),
                         x2b.rearrange("p c t -> p (c t)"))
            mark("p")

        def t_stats_T(j):
            wait("t", "p")
            t_wait_bank(6)
            t_wait_bank(7)
            for c in range(CH):
                T.matmul(pb[6][:, :], onesm[:, :], B_xbf[:, c, :],
                         start=(c == 0), stop=(c == CH - 1))
            for c in range(CH):
                T.matmul(pb[7][:, :], onesm[:, :], B_xsq[:, c, :],
                         start=(c == 0), stop=(c == CH - 1))
            mark("t")
            bank_done(6)
            bank_done(7)
            use_bank(6, "v")
            use_bank(7, "v")

        def t_stats_VS(j):
            use_bank(6, "s")
            use_bank(7, "s")
            wait("s", "t")
            Sc.activation(stat_m[:], pb[6][0:1, :], AF.Copy, scale=1.0 / float(D))
            Sc.activation(stat_e2[:], pb[7][0:1, :], AF.Copy, scale=1.0 / float(D),
                          bias=float(EPS_LN))
            sfence()
            Sc.activation(stat_m2[:], stat_m[:], AF.Square)
            mark("s")
            bank_consumed(6, "s")
            bank_consumed(7, "s")
            wait("v", "s")
            V.tensor_sub(stat_var[:], stat_e2[:], stat_m2[:])
            mark("v")
            wait("s", "v")
            Sc.activation(stat_ln[:], stat_var[:], AF.Ln)
            sfence()
            Sc.activation(arow_f[:], stat_ln[:], AF.Exp, scale=-0.5)
            sfence()
            Sc.activation(arow_b[:], arow_f[:], AF.Copy)
            mark("s")
            wait("v", "s")
            V.scalar_tensor_tensor(out=b0row_b[:], in0=stat_m[:], scalar=-1.0,
                                   in1=arow_f[:], op0=ALU.mult, op1=ALU.mult)
            mark("v")

        def t_ffn_T(j):
            t_wait_bank(6)
            t_wait_bank(7)
            wait("t", "v")
            wait("t", "s")
            for half in range(2):
                cols = slice(half * 256, half * 256 + 256)
                po = slice(half * 64, half * 64 + 64)
                for c in range(CH):
                    T.matmul(pb[6][po, 0:256], winb[:, c, :], B_xbf[:, c, cols],
                             start=(c == 0), stop=False)
                T.matmul(pb[6][po, 0:256], winsum[:], b0row_b[0:1, cols],
                         start=False, stop=True)
                T.matmul(pb[7][po, 0:256], ones_row[0:1, 0:64], arow_b[0:1, cols],
                         start=True, stop=True)
            mark("t")
            bank_done(6)
            bank_done(7)

        def t_ffn_SV(j):
            use_bank(7, "s")
            wait("s", "t")
            Sc.copy(B_apk, pb[7][:, 0:256])
            mark("s")
            bank_consumed(7, "s")
            use_bank(6, "v")
            wait("v", "s")
            V.tensor_mul(B_s, pb[6][:, 0:256], B_apk)
            mark("v")
            bank_consumed(6, "v")
            wait("s", "v")
            for kk in range(16):
                Sc.activation(B_h[:, kk, :], B_s, AF.Relu,
                              bias=b1p[:, kk:kk + 1], scale=w1p[:, kk:kk + 1])
            mark("s")

        def t_tree(j):
            wait("v", "s")
            V.tensor_scalar(out=B_pp[:, 0, :], in0=B_h[:, 0, :], scalar1=w2p[:, 0:1],
                            scalar2=b2c[:, 0:1], op0=ALU.mult, op1=ALU.add)
            for kk in range(1, 16):
                V.tensor_scalar_mul(B_pp[:, kk, :], B_h[:, kk, :], w2p[:, kk:kk + 1])
            vfence()
            for jj in range(8):
                V.tensor_add(B_tr1[:, jj, :], B_pp[:, 2 * jj, :], B_pp[:, 2 * jj + 1, :])
            vfence()
            for jj in range(4):
                V.tensor_add(B_tr2[:, jj, :], B_tr1[:, 2 * jj, :], B_tr1[:, 2 * jj + 1, :])
            vfence()
            for jj in range(2):
                V.tensor_add(B_tr3[:, jj, :], B_tr2[:, 2 * jj, :], B_tr2[:, 2 * jj + 1, :])
            vfence()
            V.tensor_add(B_g, B_tr3[:, 0, :], B_tr3[:, 1, :])
            mark("v")

        def t_out(j):
            x1b, x2b = Bx1d[j % 2], Bx2d[j % 2]
            wait("t", "v")
            for c in range(CH):
                bank = 6 + c % 2
                t_wait_bank(bank)
                T.matmul(pb[bank][:, 0:256], woutd[0:64, c, :], B_g[0:64, :],
                         start=True, stop=False)
                T.matmul(pb[bank][:, 0:256], bout_row[:, c * 128:(c + 1) * 128],
                         ones512[:, 0:256], start=False, stop=True)
                T.matmul(pb[bank][:, 256:512], woutd[64:128, c, :], B_g[64:128, :],
                         start=True, stop=False)
                T.matmul(pb[bank][:, 256:512], bout_row[:, c * 128:(c + 1) * 128],
                         ones512[:, 0:256], start=False, stop=True)
                mark("t")
                bank_done(bank)
                use_bank(bank, "v")
                wait("v", "t")
                V.scalar_tensor_tensor(out=x1b[:, c, :], in0=pb[bank][:, :],
                                       scalar=0.5, in1=x2b[:, c, :],
                                       op0=ALU.mult, op1=ALU.add)
                mark("v")
                bank_consumed(bank, "v")
            wait("q", "v")
            t0 = j * TILE
            D_("q", "ol", out=out[:, :, t0:t0 + TILE].transpose([1, 0, 2]), in_=x1b)

        def b_transpose_pair(p):
            obank = 2 + p % 2
            cc = p
            t_wait_bank(obank)
            wait("t", "v")
            for sub in range(4):
                T.transpose(pbh[obank][0:128, sub * 128:(sub + 1) * 128],
                            Botm[p % 2][:, sub, :], ident_b[:, :])
            mark("t")
            bank_done(obank)
            use_bank(obank, "s")
            wait("s", "t")
            Sc.copy(B_ofm[:, cc, :], pbh[obank][:, 0:512])
            mark("s")
            bank_consumed(obank, "s")

        def b_attn(i):
            x1b, qsb, x2b = Bx1d[i % 2], Bqsd[i % 2], Bx2d[i % 2]
            tj = i - 1
            if tj >= 0:
                t_cast(tj)
            wait("v", f"ql{i % 2}")
            wait("v", "t")
            V.tensor_mul(B_qsq.rearrange("p c t -> p (c t)"),
                         qsb.rearrange("p c t -> p (c t)"),
                         qsb.rearrange("p c t -> p (c t)"))
            mark("v")
            wait("t", "v")
            qp_d = [B_qp, B_qp2]
            qsnap = {}

            def b_post(hh):
                nbank = 4 + hh % 2
                t_wait_bank(nbank)
                wait("t", "s", qsnap[hh])
                for sub in range(4):
                    for ec in range(2):
                        T.matmul(pb[nbank][:, sub * 65:sub * 65 + 65],
                                 qp_d[hh % 2][:, ec, sub * 128:(sub + 1) * 128],
                                 kvp[:, ec, hh, :], start=(ec == 0), stop=(ec == 1))
                mark("t")
                bank_done(nbank)
                if hh % 2 == 1:
                    if hh >= 3:
                        b_transpose_pair((hh - 1) // 2 - 1)
                    use_bank(4, "v")
                    use_bank(5, "v")
                    wait("v", "t")
                    for hx in (hh - 1, hh):
                        nb = 4 + hx % 2
                        for sub in range(4):
                            cc8 = (hx % 2) * 4 + sub
                            V.tensor_single_scalar(out=dcol[:, cc8:cc8 + 1],
                                                   in_=pb[nb][:, sub * 65 + 64:sub * 65 + 65],
                                                   scalar=EPS_ATT, op=ALU.add)
                    vfence()
                    V.reciprocal(rcol[:, 0:8], dcol[:, 0:8])
                    vfence()
                    for hx in (hh - 1, hh):
                        nb = 4 + hx % 2
                        bh = (hx % 2) * 64
                        for sub in range(4):
                            cc8 = (hx % 2) * 4 + sub
                            V.tensor_scalar_mul(Botm[(hx // 2) % 2][:, sub, bh:bh + 64],
                                                pb[nb][:, sub * 65:sub * 65 + 64],
                                                rcol[:, cc8:cc8 + 1])
                    mark("v")
                    bank_consumed(4, "v")
                    bank_consumed(5, "v")

            for h in range(NH):
                base, cc = (h % 2) * 64, h // 2
                for ec in range(2):
                    bank = (0 if h % 2 == 0 else 2) + ec
                    t_wait_bank(bank)
                    T.matmul(pb[bank][:, :], om_z[:, h % 2, ec * 128:(ec + 1) * 128],
                             qsb[:, cc, :], start=True, stop=False)
                    T.matmul(pb[bank][:, :], cm_z[:, h % 2, 0:128],
                             B_qsq[:, cc, :], start=False, stop=True)
                    mark("t")
                    bank_done(bank)
                    use_bank(bank, "s")
                    wait("s", "t")
                    Sc.activation(qp_d[h % 2][:, ec, :], pb[bank][:, :], AF.Exp,
                                  bias=negln16[:, 0:1])
                    mark("s")
                    bank_consumed(bank, "s")
                qsnap[h] = ctx.snap("s")
                if h > 0:
                    b_post(h - 1)
                if tj >= 0:
                    if h == 4:
                        t_stats_T(tj)
                    elif h == 6:
                        t_stats_VS(tj)
                    elif h == 9:
                        t_ffn_T(tj)
                    elif h == 11:
                        t_ffn_SV(tj)
            b_post(NH - 1)
            b_transpose_pair(NH // 2 - 1)
            if tj >= 0:
                t_tree(tj)
            wait("t", "s")
            wait("t", "gsu")
            for c in range(CH):
                bank = 4 + c % 2
                t_wait_bank(bank)
                for dc in range(CH):
                    T.matmul(pb[bank][:, :], owb[:, dc, c, :], B_ofm[:, dc, :],
                             start=(dc == 0), stop=False)
                T.matmul(pb[bank][:, :], ob_row[:, c * 128:(c + 1) * 128], ones512[:, :],
                         start=False, stop=True)
                mark("t")
                bank_done(bank)
                use_bank(bank, "v")
                wait("v", "t")
                wait("v", f"x1l{i % 2}")
                wait("v", "s")
                V.scalar_tensor_tensor(out=x2b[:, c, :], in0=pb[bank][:, :], scalar=1.0,
                                       in1=x1b[:, c, :], op0=ALU.mult, op1=ALU.add)
                mark("v")
                bank_consumed(bank, "v")
            if tj >= 0:
                t_out(tj)

        b_loads(0)
        for i in range(NT):
            b_attn(i)
            if i + 1 < NT:
                b_loads(i + 1)
        j = NT - 1
        t_cast(j)
        t_stats_T(j)
        t_stats_VS(j)
        t_ffn_T(j)
        t_ffn_SV(j)
        t_tree(j)
        t_out(j)

        if ctx.em == "q":
            Q.real.wait_ge(sems["ol"], ctx.cnt["ol"])

    with nc.Block() as block:
        def make_body(name):
            def body(eng):
                engines = {}
                for n in Ctx.ENGS:
                    real = eng if (n == name or (name == "g" and n == "p")) else None
                    engines[n] = Eng(real)
                walk(Ctx(name, engines, sems))
            return body
        block.sync(make_body("q"))
        block.gpsimd(make_body("g"))
        block.tensor(make_body("t"))
        block.vector(make_body("v"))
        block.scalar(make_body("s"))
    return nc


_NC_CACHE = None


def kernel(**inputs):
    global _NC_CACHE
    xs = np.ascontiguousarray(np.asarray(inputs["L0_attn_x"], dtype=np.float32))

    t = np.arange(S, dtype=np.float64)
    inv = 1.0 / (10000.0 ** (np.arange(0, 64, 2, dtype=np.float64) / 64.0))
    f = t[:, None] * inv[None, :]
    sin64 = np.sin(f)
    cos64 = np.cos(f)
    cos_t = np.empty((128, S), dtype=np.float32)
    sin_t = np.empty((128, S), dtype=np.float32)
    for r in range(128):
        cos_t[r] = cos64[:, (r % 64) % 32]
        sin_t[r] = sin64[:, (r % 64) % 32]
    P64 = np.zeros((64, 64), dtype=np.float32)
    for j in range(32):
        P64[j + 32, j] = -1.0
        P64[j, j + 32] = 1.0
    pmat = np.zeros((128, 128), dtype=np.float32)
    pmat[0:64, 0:64] = P64
    pmat[64:128, 64:128] = P64
    ident = np.eye(128, dtype=np.float32)

    common = {
        "conv_k": inputs["L0_attn_conv_k"], "conv_b": inputs["L0_attn_conv_b"],
        "rff_w": inputs["L0_attn_rff_w"], "rff_b": inputs["L0_attn_rff_b"],
        "proj_w": inputs["L0_attn_proj_w"], "proj_b": inputs["L0_attn_proj_b"],
        "omega": inputs["L0_attn_omega"], "landmarks": inputs["L0_attn_landmarks"],
        "out_w": inputs["L0_attn_out_w"], "out_b": inputs["L0_attn_out_b"],
        "ffn_win": inputs["L0_ffn_Win"], "ffn_bin": inputs["L0_ffn_bin_"],
        "ffn_w1": inputs["L0_ffn_W1"], "ffn_b1": inputs["L0_ffn_b1"],
        "ffn_w2": inputs["L0_ffn_W2"], "ffn_b2": inputs["L0_ffn_b2"],
        "ffn_wout": inputs["L0_ffn_Wout"], "ffn_bout": inputs["L0_ffn_bout"],
        "rope_c": cos_t, "rope_s": sin_t, "pmat": pmat, "ident": ident,
    }
    common = {k: np.ascontiguousarray(np.asarray(v, dtype=np.float32)) for k, v in common.items()}

    if _NC_CACHE is None:
        _NC_CACHE = build_program()
    nc = _NC_CACHE

    in_maps = []
    for b in range(NCORES):
        m = dict(common)
        m["x"] = np.ascontiguousarray(xs[b].T.reshape(CH, 128, S))
        in_maps.append(m)
    res = run_bass_kernel_spmd(nc, in_maps, core_ids=list(range(NCORES)))
    outs = []
    for b in range(NCORES):
        o = np.asarray(res.results[b]["out"])
        outs.append(o.reshape(D, S).T.astype(np.float32))
    return np.stack(outs, axis=0)


# version 32
# speedup vs baseline: 1.0011x; 1.0011x over previous
import math
import contextlib
import numpy as np

import concourse.bass as bass
import concourse.mybir as mybir
from concourse.bass_utils import run_bass_kernel_spmd

F32 = mybir.dt.float32
BF16 = mybir.dt.bfloat16
I32 = mybir.dt.int32
AF = mybir.ActivationFunctionType
ALU = mybir.AluOpType

NCORES = 8
S = 4096
D = 1024
CH = 8
TILE = 512
NT = S // TILE
NH = 16
EPS_LN = 1e-5
EPS_ATT = 1e-6
LN16 = math.log(16.0)
C2PI = 1.0 / (2.0 * math.pi)
SQ2M = math.sqrt(2.0 / 256.0)
GSC = [math.sqrt(2.0 * g) for g in (0.5, 1.0, 2.0)]

ACT_SET_LNEXP = 6
ACT_SET_TRIG = 9


class Eng:
    def __init__(self, real):
        self.real = real
        self.last = None
        self._ctx = None
        self._name = None

    def __getattr__(self, name):
        if self.real is None:
            def dummy(*a, **k):
                self.last = "dummy"
            return dummy
        f = getattr(self.real, name)

        def wrap(*a, **k):
            self.last = f(*a, **k)
            return self.last
        return wrap


class Ctx:
    ENGS = ("q", "g", "t", "v", "s", "p")
    STREAMS = ("su", "gsu", "wc0", "wc1", "gc0", "gc1", "gs0", "gs1", "xl",
               "x1sp", "ysp", "yl", "wl", "qksp", "vsp", "kl", "vl", "x1l",
               "ql", "x1l0", "x1l1", "ql0", "ql1", "ol", "rt") \
        + tuple(f"wlc{i}" for i in range(8)) \
        + tuple(f"wlr{i}" for i in range(6)) \
        + tuple(f"wlp{i}" for i in range(8))

    def __init__(self, em, engines, sems):
        self.em = em
        self.e = engines
        self.sems = sems
        self.cnt = {n: 0 for n in self.ENGS + self.STREAMS}
        self.lastw = {}
        for n, e in engines.items():
            e._ctx = self
            e._name = n

    def dma(self, eng, stream, **kw):
        self.cnt[stream] += 16
        e = self.e[eng]
        if e.real is not None:
            ins = e.real.dma_start(**kw)
            ins.then_inc(self.sems[stream], 16)

    def mark(self, name, inc=1):
        eng = self.e[name]
        if eng.last is None:
            return
        self.cnt[name] += inc
        if eng.real is not None:
            eng.last.then_inc(self.sems[name], inc)
        eng.last = None

    def wait(self, waiter, producer, val=None):
        if val is None:
            val = self.cnt[producer]
        if val <= 0:
            return
        key = (waiter, producer)
        if self.lastw.get(key, 0) >= val:
            return
        self.lastw[key] = val
        em_name = "g" if waiter == "p" else waiter
        if self.em == em_name:
            self.e[waiter].real.wait_ge(self.sems[producer], val)

    def snap(self, name):
        return self.cnt[name]


def build_program(upto=3, debug_dump=False):
    nc = bass.Bass("TRN2", target_bir_lowering=False, debug=False)
    dt_ = nc.dram_tensor

    x_in = dt_("x", [CH, 128, S], F32, kind="ExternalInput").ap()
    conv_k = dt_("conv_k", [3, 3, 256, 1024], F32, kind="ExternalInput").ap()
    conv_b = dt_("conv_b", [3, 1024], F32, kind="ExternalInput").ap()
    rff_w = dt_("rff_w", [3, 3, 1024, 256], F32, kind="ExternalInput").ap()
    rff_b = dt_("rff_b", [3, 3, 256], F32, kind="ExternalInput").ap()
    proj_w = dt_("proj_w", [3, 768, 1024], F32, kind="ExternalInput").ap()
    proj_b = dt_("proj_b", [3, 1024], F32, kind="ExternalInput").ap()
    omega = dt_("omega", [64, 256], F32, kind="ExternalInput").ap()
    landm = dt_("landmarks", [64, 64], F32, kind="ExternalInput").ap()
    out_w = dt_("out_w", [1024, 1024], F32, kind="ExternalInput").ap()
    out_b = dt_("out_b", [1024], F32, kind="ExternalInput").ap()
    f_win = dt_("ffn_win", [1024, 64], F32, kind="ExternalInput").ap()
    f_bin = dt_("ffn_bin", [64], F32, kind="ExternalInput").ap()
    f_w1 = dt_("ffn_w1", [64, 16], F32, kind="ExternalInput").ap()
    f_b1 = dt_("ffn_b1", [64, 16], F32, kind="ExternalInput").ap()
    f_w2 = dt_("ffn_w2", [64, 16], F32, kind="ExternalInput").ap()
    f_b2 = dt_("ffn_b2", [64], F32, kind="ExternalInput").ap()
    f_wout = dt_("ffn_wout", [64, 1024], F32, kind="ExternalInput").ap()
    f_bout = dt_("ffn_bout", [1024], F32, kind="ExternalInput").ap()
    rope_c = dt_("rope_c", [128, S], F32, kind="ExternalInput").ap()
    rope_s = dt_("rope_s", [128, S], F32, kind="ExternalInput").ap()
    pmat = dt_("pmat", [128, 128], F32, kind="ExternalInput").ap()
    ident_in = dt_("ident", [128, 128], F32, kind="ExternalInput").ap()

    out = dt_("out", [CH, 128, S], F32, kind="ExternalOutput").ap()
    kvdbg = dt_("kvdbg", [128, 2, NH, 65], BF16, kind="ExternalOutput").ap() if debug_dump else None
    khdbg = dt_("khdbg", [128, CH, 512], BF16, kind="ExternalOutput").ap() if debug_dump else None
    ofdbg = dt_("ofdbg", [128, CH, 512], BF16, kind="ExternalOutput").ap() if debug_dump else None
    x2dbg = dt_("x2dbg", [128, CH, 512], F32, kind="ExternalOutput").ap() if debug_dump else None
    owdbg = dt_("owdbg", [128, 8, 8, 128], BF16, kind="ExternalOutput").ap() if debug_dump else None
    x1bdbg = dt_("x1bdbg", [128, CH, 512], F32, kind="ExternalOutput").ap() if debug_dump else None
    lndbg = dt_("lndbg", [1, 4, 512], F32, kind="ExternalOutput").ap() if debug_dump else None

    x1s = dt_("x1s", [CH, 128, S], F32, kind="Internal").ap()
    wbd = dt_("wbd", [3, 128, 144, 128], BF16, kind="Internal").ap()
    rcb = dt_("rcb", [128, S], BF16, kind="Internal").ap()
    rsb = dt_("rsb", [128, S], BF16, kind="Internal").ap()
    ys = dt_("ys", [CH, 128, S + 2], BF16, kind="Internal").ap()
    qs = dt_("qs", [CH, 128, S], BF16, kind="Internal").ap()

    al = lambda n, sh, dty: nc.alloc_sbuf_tensor(n, sh, dty).ap()

    winb = al("winb", [128, 8, 64], BF16)
    woutd = al("woutd", [128, 8, 128], BF16)
    om_z = al("om_z", [128, 2, 256], BF16)
    cm_z = al("cm_z", [128, 2, 256], BF16)
    lm_bd = al("lm_bd", [128, 128], BF16)
    cm_bd = al("cm_bd", [128, 128], BF16)
    onesm = al("onesm", [128, 128], BF16)
    l2d = al("l2d", [128, 64], BF16)
    pmat_b = al("pmat_b", [128, 128], BF16)
    ident_b = al("ident_b", [128, 128], BF16)
    ones_row = al("ones_row", [1, 128], BF16)
    ones_pad = al("ones_pad", [128, 128], BF16)
    ones512 = al("ones512", [128, 512], BF16)
    negln16 = al("negln16", [128, 1], F32)
    epsl = al("epsl", [128, 1], F32)
    nybias = al("nybias", [128, 1], F32)
    w1p = al("w1p", [128, 16], F32)
    b1p = al("b1p", [128, 16], F32)
    w2p = al("w2p", [128, 16], F32)
    b2c = al("b2c", [128, 1], F32)
    binc = al("binc", [128, 1], F32)
    winsum = al("winsum", [1, 64], BF16)
    bout_row = al("bout_row", [128, 1024], BF16)
    ob_row = al("ob_row", [128, 1024], BF16)
    pbv_row = al("pbv_row", [128, 1024], BF16)
    pbv_f = al("pbv_f", [1, 1024], F32)
    cbb = al("cbb", [128, 3, 8], F32)
    pbb = al("pbb", [128, 3, 8], F32)
    rbbr = al("rbbr", [128, 3, 6], F32)
    rbb = al("rbb", [128, 3, 6], F32)
    lm_f = al("lm_f", [128, 64], F32)
    lm_T = al("lm_T", [128, 64], F32)
    lsq_sb = al("lsq_sb", [128, 64], F32)
    kvbfT = al("kvbfT", [65, 256], BF16)
    kvp = al("kvp", [128, 2, NH, 65], BF16)
    stat_m = al("stat_m", [1, 512], F32)
    stat_e2 = al("stat_e2", [1, 512], F32)
    stat_m2 = al("stat_m2", [1, 512], F32)
    stat_var = al("stat_var", [1, 512], F32)
    stat_ln = al("stat_ln", [1, 512], F32)
    arow_f = al("arow_f", [1, 512], F32)
    arow_b = al("arow_b", [1, 512], BF16)
    b0row_b = al("b0row_b", [1, 512], BF16)
    dcol = al("dcol", [128, 8], F32)
    rcol = al("rcol", [128, 8], F32)
    zpad = al("zpad", [128, 8], BF16)

    ARN = 166 * 1024 + 2048
    arena = nc.alloc_sbuf_tensor("arena", [128, ARN], mybir.dt.uint8).ap()

    def carve(off, dty, shape):
        n = int(np.prod(shape))
        sz = {F32: 4, BF16: 2, I32: 4}[dty]
        sl = arena[:, off:off + n * sz].bitcast(dty)
        if len(shape) == 1:
            return sl
        spec = " ".join(f"d{i}" for i in range(len(shape)))
        kw = {f"d{i}": shape[i] for i in range(len(shape))}
        return sl.rearrange(f"p ({spec}) -> p {spec}", **kw)

    P0_xt = carve(0, F32, [CH, 512])
    P0_x1 = carve(16384, F32, [CH, 512])
    P0_ytmp = carve(32768, F32, [CH, 512])
    P0_xbf = carve(49152, BF16, [CH, 512])
    P0_xsq = carve(57344, BF16, [CH, 512])
    P0_abc = carve(65536, BF16, [512])
    P0_bbc = carve(66560, BF16, [512])
    P0_s = carve(67584, F32, [256])
    P0_h = carve(68608, BF16, [16, 256])
    P0_pp = carve(76800, BF16, [16, 256])
    P0_tr1 = carve(84992, F32, [8, 256])
    P0_tr2 = carve(93184, F32, [4, 256])
    P0_tr3 = carve(97280, F32, [2, 256])
    P0_g = carve(99328, BF16, [256])
    P0_y = carve(99840, BF16, [CH, 512])
    P0_apk = carve(108032, BF16, [256])

    def p0set(base):
        xbf = carve(base + 16384, BF16, [CH, 512])
        return dict(
            xt=carve(base + 0, F32, [CH, 512]),
            xbf=xbf,
            y=xbf,
            xsq=carve(base + 24576, BF16, [CH, 512]),
            s=carve(base + 32768, F32, [256]),
            h=carve(base + 33792, BF16, [16, 256]),
            apk=carve(base + 41984, BF16, [256]),
            abc=carve(base + 42496, BF16, [512]),
            bbc=carve(base + 43520, BF16, [512]),
            fr=carve(base + 44544, F32, [6, 512]),
            br=carve(base + 56832, BF16, [4, 512]),
        )
    P0S = [p0set(0), p0set(60928)]
    P0_pp2 = carve(121856, BF16, [16, 256])
    P0_tr1b = carve(130048, F32, [8, 256])
    P0_tr2b = carve(138240, F32, [4, 256])
    P0_tr3b = carve(142336, F32, [2, 256])
    P0_g2 = carve(144384, BF16, [256])
    P0_xts = [P0S[0]["xt"], P0S[1]["xt"], carve(144896, F32, [CH, 512])]

    A1_y = carve(0, BF16, [CH, 514])
    A1_h2 = [carve(8224, BF16, [CH, 512]), carve(147648, BF16, [CH, 512])]
    A1_z2 = [carve(16416, BF16, [6, 512]), carve(155840, BF16, [6, 512])]
    A1_preq = carve(22560, BF16, [CH, 512])
    A1_prek = carve(30752, BF16, [CH, 512])
    A1_qsb = carve(38944, BF16, [CH, 512])
    A1_ksb = carve(47136, BF16, [CH, 512])
    A1_vtm = carve(55328, BF16, [4, NH, 65])
    A1_ct = carve(63680, BF16, [512])
    A1_st = carve(64704, BF16, [512])
    A1_rot = carve(65728, BF16, [CH, 512])
    A1_u6 = carve(73920, F32, [3, 512])
    A1_n6 = carve(80064, I32, [3, 512])
    A1_n6f = carve(80064, F32, [3, 512])
    A1_mg6 = carve(86208, BF16, [3, 512])
    A1_ml6 = carve(89280, BF16, [3, 512])
    A1_wb = carve(110784, BF16, [144, 128])
    A1_ct2 = [A1_ct, carve(161984, BF16, [512])]
    A1_st2 = [A1_st, carve(163008, BF16, [512])]
    A2_ksq = carve(65728, BF16, [CH, 512])
    A2_khsq = A2_ksq
    A2_khat = carve(147648, BF16, [CH, 512])
    kvacc = carve(92352, F32, [NH, 256])[0:65]
    A2_kp = carve(108736, BF16, [4, 256])
    A2_kp2 = carve(166976, BF16, [4, 256])


    B_x1 = carve(0, F32, [CH, 512])
    B_qsb = carve(16384, BF16, [CH, 512])
    B_qsq = carve(24576, BF16, [CH, 512])
    B_qp = carve(32768, BF16, [2, 512])
    B_otm = carve(34816, BF16, [4, 128])
    B_ofm = carve(35840, BF16, [CH, 512])
    B_x2 = carve(44032, F32, [CH, 512])
    B_xbf = carve(60416, BF16, [CH, 512])
    B_xsq = carve(68608, BF16, [CH, 512])
    B_s = carve(76800, F32, [256])
    B_h = carve(77824, BF16, [16, 256])
    B_pp = carve(86016, BF16, [16, 256])
    B_tr1 = carve(94208, F32, [8, 256])
    B_tr2 = carve(102400, F32, [4, 256])
    B_tr3 = carve(106496, F32, [2, 256])
    B_g = carve(108544, BF16, [256])
    B_apk = carve(109056, BF16, [256])
    owb = carve(109632, BF16, [8, 8, 128])
    B_x1b = carve(126016, F32, [CH, 512])
    B_qsbb = carve(142400, BF16, [CH, 512])
    B_x2b = carve(150592, F32, [CH, 512])
    B_otmb = carve(166976, BF16, [4, 128])
    B_qp2 = carve(168000, BF16, [2, 512])

    pb = [nc.alloc_psum_tensor(f"pb{i}", [128, 512], F32).ap() for i in range(8)]
    pbh = [p.bitcast(BF16) for p in pb]

    es = contextlib.ExitStack()
    sems = {n: es.enter_context(nc.semaphore(f"sem_{n}")) for n in Ctx.ENGS + Ctx.STREAMS}

    def walk(ctx: Ctx):
        Q, G, T, V, Sc, P = (ctx.e[n] for n in ("q", "g", "t", "v", "s", "p"))
        mark, wait = ctx.mark, ctx.wait
        prod, cons = {}, {}

        def t_wait_bank(i):
            if i in cons:
                wait("t", cons[i][0], cons[i][1])

        def bank_done(i):
            prod[i] = ("t", ctx.cnt["t"])

        def use_bank(i, eng):
            if i in prod:
                wait(eng, prod[i][0], prod[i][1])

        def bank_consumed(i, eng):
            cons[i] = (eng, ctx.cnt[eng])

        def vfence():
            mark("v")
            wait("v", "v")

        def sfence():
            mark("s")
            wait("s", "s")

        def load_table(set_id):
            if ctx.em == "s":
                Sc.real.add_instruction(mybir.InstLoadActFuncSet(
                    name=nc.get_next_instruction_name(), act_func_set_id=set_id,
                    ins=[], outs=[]))

        load_table(ACT_SET_LNEXP)

        V.memset(cm_z.rearrange("p a b -> p (a b)"), 0.0)
        V.memset(cm_bd[:], 0.0)
        vfence()
        V.memset(cm_z[0:64, 0, :], -0.5)
        V.memset(cm_z[64:128, 1, :], -0.5)
        V.memset(cm_bd[0:64, 0:64], -1.0 / 64.0)
        V.memset(cm_bd[64:128, 64:128], -1.0 / 64.0)
        V.memset(onesm[:], 1.0)
        V.memset(om_z[64:128, 0, :], 0.0)
        V.memset(om_z[0:64, 1, :], 0.0)
        V.memset(ones_row[:], 1.0)
        V.memset(ones_pad[:], 0.0)
        V.memset(ones512[:], 0.0)
        V.memset(bout_row[:], 0.0)
        V.memset(ob_row[:], 0.0)
        V.memset(pbv_row[:], 0.0)
        for _k in (0, 1):
            V.memset(P0S[_k]["br"].rearrange("p a b -> p (a b)"), 0.0)
        vfence()
        V.memset(ones_pad[0:1, :], 1.0)
        V.memset(ones512[0:1, :], 1.0)
        V.memset(epsl[:], EPS_LN)
        V.memset(negln16[:], -LN16)
        V.memset(zpad[:], 0.0)
        mark("v")

        D_ = ctx.dma
        D_("q", "su", out=lm_f[0:64, :], in_=landm)
        D_("q", "su", out=lm_f[64:128, :], in_=landm)
        with nc.allow_non_contiguous_dma(reason="tiny transposed landmark load"):
            D_("q", "su", out=lm_T[0:64, :], in_=landm.transpose([1, 0]))
            D_("q", "su", out=lm_T[64:128, :], in_=landm.transpose([1, 0]))
        with nc.allow_non_contiguous_dma(reason="tiny bias loads"):
            D_("q", "su", out=cbb[:], in_=conv_b.rearrange("i (c p) -> p i c", p=128))
            D_("q", "su", out=pbb[:], in_=proj_b.rearrange("i (c p) -> p i c", p=128))
            D_("q", "su", out=rbbr[:], in_=rff_b.rearrange("i g (mm p) -> p i (g mm)", p=128))
        D_("q", "su", out=w1p[0:64, :], in_=f_w1)
        D_("q", "su", out=w1p[64:128, :], in_=f_w1)
        D_("q", "su", out=b1p[0:64, :], in_=f_b1)
        D_("q", "su", out=b1p[64:128, :], in_=f_b1)
        D_("q", "su", out=w2p[0:64, :], in_=f_w2)
        D_("q", "su", out=w2p[64:128, :], in_=f_w2)
        D_("q", "su", out=b2c[0:64, 0], in_=f_b2)
        D_("q", "su", out=b2c[64:128, 0], in_=f_b2)
        D_("q", "su", out=binc[0:64, 0], in_=f_bin)
        D_("q", "su", out=binc[64:128, 0], in_=f_bin)

        D_("g", "gsu", out=winb[:], in_=f_win.rearrange("(c p) m -> p c m", p=128))
        D_("g", "gsu", out=woutd[0:64, :, :], in_=f_wout.rearrange("m (c j) -> m c j", j=128))
        D_("g", "gsu", out=woutd[64:128, :, :], in_=f_wout.rearrange("m (c j) -> m c j", j=128))
        D_("g", "gsu", out=om_z[0:64, 0, :], in_=omega)
        D_("g", "gsu", out=om_z[64:128, 1, :], in_=omega)
        D_("g", "gsu", out=pmat_b[:], in_=pmat)
        D_("g", "gsu", out=ident_b[:], in_=ident_in)
        wait("g", "v")
        D_("g", "gsu", out=bout_row[0:1, :], in_=f_bout[None, :])
        D_("g", "gsu", out=ob_row[0:1, :], in_=out_b[None, :])
        D_("g", "wc0", out=rcb, in_=rope_c)
        D_("g", "wc0", out=rsb, in_=rope_s)
        for prj in range(3):
            for oc in range(CH):
                g4 = oc // 2
                c0 = g4 * 256 + (oc % 2) * 128
                D_("g", "wc0",
                   out=wbd[prj][:, oc:48 + oc:8, :].rearrange("p (w a) j -> p w a j", w=3),
                   in_=conv_k[prj, :, :, c0:c0 + 128].rearrange("w (a p) j -> p w a j", p=128))
            for rc in range(6):
                gam, mm = rc // 2, rc % 2
                D_("g", "wc0",
                   out=wbd[prj][:, 48 + gam * 16 + mm:48 + gam * 16 + 16 + mm:2, :],
                   in_=rff_w[prj, gam, :, mm * 128:(mm + 1) * 128].rearrange("(a p) j -> p a j", p=128))
            if prj < 2:
                for cc in range(CH):
                    D_("g", "wc0",
                       out=wbd[prj][:, 96 + cc:137 + cc:8, :],
                       in_=proj_w[prj, :, cc * 128:(cc + 1) * 128].rearrange("(f p) j -> p f j", p=128))
            else:
                for cg in range(2):
                    D_("g", "wc0",
                       out=wbd[prj][:, 96:144, :].rearrange("p (f c) j -> p f c j", f=6)[:, :, cg * 4:(cg + 1) * 4, :],
                       in_=proj_w[prj, :, cg * 512:(cg + 1) * 512].rearrange("(f p) (c j) -> p f c j", p=128, j=128))
        D_("q", "su", out=pbv_f[0:1, :], in_=proj_b[2:3, :])

        wait("v", "su")
        vfence()
        V.tensor_scalar_mul(pbv_row[0:1, :], pbv_f[0:1, :], 1.0 / SQ2M)
        V.scalar_tensor_tensor(out=b1p[:], in0=w1p[:], scalar=binc[:, 0:1], in1=b1p[:],
                               op0=ALU.mult, op1=ALU.add)
        V.tensor_scalar(out=rbb[:], in0=rbbr[:], scalar1=-C2PI, scalar2=0.25,
                        op0=ALU.mult, op1=ALU.add)
        V.tensor_mul(lsq_sb[:], lm_f[:], lm_f[:])
        vfence()
        V.reduce_sum(nybias[:, 0:1], lsq_sb[:], axis=mybir.AxisListType.X)
        vfence()
        V.tensor_scalar_mul(nybias[:, 0:1], nybias[:, 0:1], -1.0 / 64.0)
        mark("v")

        V.tensor_scalar_mul(l2d[:], lm_T[:], 2.0 / 64.0)
        V.memset(lm_bd[:], 0.0)
        vfence()
        V.tensor_copy(lm_bd[0:64, 0:64], l2d[0:64, :])
        V.tensor_copy(lm_bd[64:128, 64:128], l2d[64:128, :])
        mark("v")
        wait("t", "gsu")
        wait("t", "v")
        t_wait_bank(1)
        for c in range(CH):
            T.matmul(pb[1][0:64, 0:64], onesm[:, 0:64], winb[:, c, :],
                     start=(c == 0), stop=(c == CH - 1))
        mark("t")
        bank_done(1)
        use_bank(1, "s")
        wait("s", "t")
        Sc.copy(winsum[:], pb[1][0:1, 0:64])
        mark("s")
        bank_consumed(1, "s")

        wait("q", "v")
        with nc.allow_non_contiguous_dma(reason="edge pad columns"):
            D_("q", "ysp", out=ys[:, :, 0:1].transpose([1, 0, 2]),
               in_=zpad.rearrange("p (c e) -> p c e", e=1))
            D_("q", "ysp", out=ys[:, :, S + 1:S + 2].transpose([1, 0, 2]),
               in_=zpad.rearrange("p (c e) -> p c e", e=1))

        def ln_stats(xbf, xsq, arow_bf_dst, b0row_bf_dst, bank_m=0, bank_s=1):
            t_wait_bank(bank_m)
            t_wait_bank(bank_s)
            for c in range(CH):
                T.matmul(pb[bank_m][:, :], onesm[:, :], xbf[:, c, :],
                         start=(c == 0), stop=(c == CH - 1))
            for c in range(CH):
                T.matmul(pb[bank_s][:, :], onesm[:, :], xsq[:, c, :],
                         start=(c == 0), stop=(c == CH - 1))
            mark("t")
            bank_done(bank_m)
            bank_done(bank_s)
            use_bank(bank_m, "v")
            use_bank(bank_s, "v")
            wait("v", "t")
            V.tensor_scalar_mul(stat_m[:], pb[bank_m][0:1, :], 1.0 / float(D))
            V.tensor_scalar(out=stat_e2[:], in0=pb[bank_s][0:1, :], scalar1=1.0 / float(D),
                            scalar2=EPS_LN, op0=ALU.mult, op1=ALU.add)
            vfence()
            V.tensor_mul(stat_m2[:], stat_m[:], stat_m[:])
            vfence()
            V.tensor_sub(stat_var[:], stat_e2[:], stat_m2[:])
            mark("v")
            bank_consumed(bank_m, "v")
            bank_consumed(bank_s, "v")
            wait("s", "v")
            Sc.activation(stat_ln[:], stat_var[:], AF.Ln)
            sfence()
            Sc.activation(arow_f[:], stat_ln[:], AF.Exp, scale=-0.5)
            sfence()
            Sc.activation(arow_bf_dst[:], arow_f[:], AF.Copy)
            mark("s")
            wait("v", "s")
            V.scalar_tensor_tensor(out=b0row_bf_dst[:], in0=stat_m[:], scalar=-1.0,
                                   in1=arow_f[:], op0=ALU.mult, op1=ALU.mult)
            mark("v")

        def ffn(xbf, arow_bf, b0row_bf, xin_f32, xout_f32,
                sb_s, sb_h, sb_pp, tr1, tr2, tr3, sb_g, sb_apk,
                bank_s=2, bank_a=3, bank_o0=4, bank_o1=5):
            t_wait_bank(bank_s)
            t_wait_bank(bank_a)
            for half in range(2):
                cols = slice(half * 256, half * 256 + 256)
                po = slice(half * 64, half * 64 + 64)
                for c in range(CH):
                    T.matmul(pb[bank_s][po, 0:256], winb[:, c, :], xbf[:, c, cols],
                             start=(c == 0), stop=False)
                T.matmul(pb[bank_s][po, 0:256], winsum[:], b0row_bf[0:1, cols],
                         start=False, stop=True)
                T.matmul(pb[bank_a][po, 0:256], ones_row[0:1, 0:64], arow_bf[0:1, cols],
                         start=True, stop=True)
            mark("t")
            bank_done(bank_s)
            bank_done(bank_a)
            use_bank(bank_a, "s")
            wait("s", "t")
            Sc.copy(sb_apk, pb[bank_a][:, 0:256])
            mark("s")
            bank_consumed(bank_a, "s")
            use_bank(bank_s, "v")
            wait("v", "s")
            V.tensor_mul(sb_s, pb[bank_s][:, 0:256], sb_apk)
            mark("v")
            bank_consumed(bank_s, "v")
            wait("s", "v")
            for k in range(16):
                Sc.activation(sb_h[:, k, :], sb_s, AF.Relu,
                              bias=b1p[:, k:k + 1], scale=w1p[:, k:k + 1])
            mark("s")
            wait("v", "s")
            V.tensor_scalar(out=sb_pp[:, 0, :], in0=sb_h[:, 0, :], scalar1=w2p[:, 0:1],
                            scalar2=b2c[:, 0:1], op0=ALU.mult, op1=ALU.add)
            for k in range(1, 16):
                V.tensor_scalar_mul(sb_pp[:, k, :], sb_h[:, k, :], w2p[:, k:k + 1])
            vfence()
            for j in range(8):
                V.tensor_add(tr1[:, j, :], sb_pp[:, 2 * j, :], sb_pp[:, 2 * j + 1, :])
            vfence()
            for j in range(4):
                V.tensor_add(tr2[:, j, :], tr1[:, 2 * j, :], tr1[:, 2 * j + 1, :])
            vfence()
            for j in range(2):
                V.tensor_add(tr3[:, j, :], tr2[:, 2 * j, :], tr2[:, 2 * j + 1, :])
            vfence()
            V.tensor_add(sb_g, tr3[:, 0, :], tr3[:, 1, :])
            mark("v")
            wait("t", "v")
            for c in range(CH):
                bank = bank_o0 if (c % 2 == 0) else bank_o1
                t_wait_bank(bank)
                T.matmul(pb[bank][:, 0:256], woutd[0:64, c, :], sb_g[0:64, :],
                         start=True, stop=False)
                T.matmul(pb[bank][:, 0:256], bout_row[:, c * 128:(c + 1) * 128],
                         ones512[:, 0:256], start=False, stop=True)
                T.matmul(pb[bank][:, 256:512], woutd[64:128, c, :], sb_g[64:128, :],
                         start=True, stop=False)
                T.matmul(pb[bank][:, 256:512], bout_row[:, c * 128:(c + 1) * 128],
                         ones512[:, 0:256], start=False, stop=True)
                mark("t")
                bank_done(bank)
                use_bank(bank, "v")
                wait("v", "t")
                V.scalar_tensor_tensor(out=xout_f32[:, c, :], in0=pb[bank][:, :], scalar=0.5,
                                       in1=xin_f32[:, c, :], op0=ALU.mult, op1=ALU.add)
                mark("v")
                bank_consumed(bank, "v")

        def p0_stats_vs(banks, ra, rb):
            wait("v", "t")
            for k in (0, 1):
                bm, bs = banks[k]
                fr = P0S[k]["fr"]
                V.tensor_scalar_mul(fr[0:1, 0, :], pb[bm][0:1, :], 1.0 / float(D))
                V.tensor_scalar(out=fr[0:1, 1, :], in0=pb[bs][0:1, :],
                                scalar1=1.0 / float(D), scalar2=EPS_LN,
                                op0=ALU.mult, op1=ALU.add)
            vfence()
            for k in (0, 1):
                fr = P0S[k]["fr"]
                V.tensor_mul(fr[0:1, 2, :], fr[0:1, 0, :], fr[0:1, 0, :])
            vfence()
            for k in (0, 1):
                fr = P0S[k]["fr"]
                V.tensor_sub(fr[0:1, 3, :], fr[0:1, 1, :], fr[0:1, 2, :])
            mark("v")
            for k in (0, 1):
                bank_consumed(banks[k][0], "v")
                bank_consumed(banks[k][1], "v")
            wait("s", "v")
            for k in (0, 1):
                fr = P0S[k]["fr"]
                Sc.activation(fr[0:1, 4, :], fr[0:1, 3, :], AF.Ln)
            sfence()
            for k in (0, 1):
                fr = P0S[k]["fr"]
                Sc.activation(fr[0:1, 5, :], fr[0:1, 4, :], AF.Exp, scale=-0.5)
            sfence()
            for k in (0, 1):
                Sc.activation(P0S[k]["br"][0:1, ra, :], P0S[k]["fr"][0:1, 5, :], AF.Copy)
            mark("s")
            wait("v", "s")
            for k in (0, 1):
                V.scalar_tensor_tensor(out=P0S[k]["br"][0:1, rb, :],
                                       in0=P0S[k]["fr"][0:1, 0, :], scalar=-1.0,
                                       in1=P0S[k]["fr"][0:1, 5, :],
                                       op0=ALU.mult, op1=ALU.mult)
            mark("v")

        def p0_stats_t(banks):
            wait("t", "s")
            for k in (0, 1):
                bm, bs = banks[k]
                st_ = P0S[k]
                t_wait_bank(bm)
                t_wait_bank(bs)
                for c in range(CH):
                    T.matmul(pb[bm][:, :], onesm[:, :], st_["xbf"][:, c, :],
                             start=(c == 0), stop=(c == CH - 1))
                for c in range(CH):
                    T.matmul(pb[bs][:, :], onesm[:, :], st_["xsq"][:, c, :],
                             start=(c == 0), stop=(c == CH - 1))
                mark("t")
                bank_done(bm)
                bank_done(bs)
                use_bank(bm, "v")
                use_bank(bs, "v")

        def p0_cast(k, xt_):
            st_ = P0S[k]
            Sc.activation(st_["xbf"].rearrange("p c t -> p (c t)"),
                          xt_.rearrange("p c t -> p (c t)"), AF.Copy)
            Sc.activation(st_["xsq"].rearrange("p c t -> p (c t)"),
                          xt_.rearrange("p c t -> p (c t)"), AF.Square)

        NP = NT // 2
        for pr in range(NP):
            tA, tB = (2 * pr) * TILE, (2 * pr + 1) * TILE
            tt_of = {0: tA, 1: tB}
            slot_of = {0: (2 * pr) % 3, 1: (2 * pr + 1) % 3}
            if pr == 0:
                for k in (0, 1):
                    D_("q", "xl", out=P0_xts[k],
                       in_=x_in[:, :, tt_of[k]:tt_of[k] + TILE].transpose([1, 0, 2]))
            wait("s", "xl")
            wait("s", "t")
            wait("s", "ysp")
            for k in (0, 1):
                p0_cast(k, P0_xts[slot_of[k]])
            mark("s")
            p0_stats_t({0: (0, 1), 1: (2, 3)})
            p0_stats_vs({0: (0, 1), 1: (2, 3)}, 0, 1)
            if 2 * pr + 2 < NT:
                wait("q", "v")
                wait("q", "x1sp")
                tn = (2 * pr + 2) * TILE
                D_("q", "xl", out=P0_xts[(2 * pr + 2) % 3],
                   in_=x_in[:, :, tn:tn + TILE].transpose([1, 0, 2]))
            fbank = {0: (4, 5, 6), 1: (0, 1, 2)}
            for k in (0, 1):
                s0_, s1_, ba_ = fbank[k]
                st_ = P0S[k]
                t_wait_bank(s0_)
                t_wait_bank(s1_)
                t_wait_bank(ba_)
                for half, sb_ in ((0, s0_), (1, s1_)):
                    cols = slice(half * 256, half * 256 + 256)
                    po = slice(half * 64, half * 64 + 64)
                    for c in range(CH):
                        T.matmul(pb[sb_][po, 0:256], winb[:, c, :], st_["xbf"][:, c, cols],
                                 start=(c == 0), stop=False)
                mark("t")
            wait("t", "v")
            wait("t", "s")
            for k in (0, 1):
                s0_, s1_, ba_ = fbank[k]
                st_ = P0S[k]
                for half, sb_ in ((0, s0_), (1, s1_)):
                    cols = slice(half * 256, half * 256 + 256)
                    po = slice(half * 64, half * 64 + 64)
                    T.matmul(pb[sb_][po, 0:256], winsum[:], st_["br"][0:1, 1, cols],
                             start=False, stop=True)
                    T.matmul(pb[ba_][po, 0:256], ones_row[0:1, 0:64], st_["br"][0:1, 0, cols],
                             start=True, stop=True)
                mark("t")
                bank_done(s0_)
                bank_done(s1_)
                bank_done(ba_)
            for k in (0, 1):
                s0_, s1_, ba_ = fbank[k]
                st_ = P0S[k]
                use_bank(ba_, "s")
                wait("s", "t")
                Sc.copy(st_["apk"], pb[ba_][:, 0:256])
                mark("s")
                bank_consumed(ba_, "s")
                use_bank(s0_, "v")
                use_bank(s1_, "v")
                wait("v", "s")
                V.tensor_mul(st_["s"][0:64, :], pb[s0_][0:64, 0:256], st_["apk"][0:64, :])
                V.tensor_mul(st_["s"][64:128, :], pb[s1_][64:128, 0:256], st_["apk"][64:128, :])
                mark("v")
                bank_consumed(s0_, "v")
                bank_consumed(s1_, "v")
                wait("s", "v")
                for kk in range(16):
                    Sc.activation(st_["h"][:, kk, :], st_["s"], AF.Relu,
                                  bias=b1p[:, kk:kk + 1], scale=w1p[:, kk:kk + 1])
                mark("s")
            obank = {0: (3, 7), 1: (4, 5)}
            for k in (0, 1):
                st_ = P0S[k]
                wait("v", "s")
                V.tensor_scalar(out=P0_pp2[:, 0, :], in0=st_["h"][:, 0, :],
                                scalar1=w2p[:, 0:1], scalar2=b2c[:, 0:1],
                                op0=ALU.mult, op1=ALU.add)
                for kk in range(1, 16):
                    V.tensor_scalar_mul(P0_pp2[:, kk, :], st_["h"][:, kk, :], w2p[:, kk:kk + 1])
                vfence()
                for j in range(8):
                    V.tensor_add(P0_tr1b[:, j, :], P0_pp2[:, 2 * j, :], P0_pp2[:, 2 * j + 1, :])
                vfence()
                for j in range(4):
                    V.tensor_add(P0_tr2b[:, j, :], P0_tr1b[:, 2 * j, :], P0_tr1b[:, 2 * j + 1, :])
                vfence()
                for j in range(2):
                    V.tensor_add(P0_tr3b[:, j, :], P0_tr2b[:, 2 * j, :], P0_tr2b[:, 2 * j + 1, :])
                vfence()
                V.tensor_add(P0_g2, P0_tr3b[:, 0, :], P0_tr3b[:, 1, :])
                mark("v")
                wait("t", "v")
                b0_, b1_ = obank[k]
                for c in range(CH):
                    bank = b0_ if c % 2 == 0 else b1_
                    t_wait_bank(bank)
                    T.matmul(pb[bank][:, 0:256], woutd[0:64, c, :], P0_g2[0:64, :],
                             start=True, stop=False)
                    T.matmul(pb[bank][:, 0:256], bout_row[:, c * 128:(c + 1) * 128],
                             ones512[:, 0:256], start=False, stop=True)
                    T.matmul(pb[bank][:, 256:512], woutd[64:128, c, :], P0_g2[64:128, :],
                             start=True, stop=False)
                    T.matmul(pb[bank][:, 256:512], bout_row[:, c * 128:(c + 1) * 128],
                             ones512[:, 0:256], start=False, stop=True)
                    mark("t")
                    bank_done(bank)
                    use_bank(bank, "v")
                    wait("v", "t")
                    V.scalar_tensor_tensor(out=P0_xts[slot_of[k]][:, c, :], in0=pb[bank][:, :],
                                           scalar=0.5, in1=P0_xts[slot_of[k]][:, c, :],
                                           op0=ALU.mult, op1=ALU.add)
                    mark("v")
                    bank_consumed(bank, "v")
                wait("q", "v")
                D_("q", "x1sp", out=x1s[:, :, tt_of[k]:tt_of[k] + TILE].transpose([1, 0, 2]),
                   in_=P0_xts[slot_of[k]])
                wait("s", "v")
                p0_cast(k, P0_xts[slot_of[k]])
                mark("s")
            p0_stats_t({0: (2, 3), 1: (0, 1)})
            p0_stats_vs({0: (2, 3), 1: (0, 1)}, 2, 3)
            ybank = {0: (6, 7), 1: (2, 3)}
            wait("t", "v")
            wait("t", "s")
            for k in (0, 1):
                ba_, bb_ = ybank[k]
                st_ = P0S[k]
                t_wait_bank(ba_)
                t_wait_bank(bb_)
                T.matmul(pb[ba_][:, :], ones_pad[:, :], st_["br"][:, 2, :], start=True, stop=True)
                T.matmul(pb[bb_][:, :], ones_pad[:, :], st_["br"][:, 3, :], start=True, stop=True)
                mark("t")
                bank_done(ba_)
                bank_done(bb_)
                use_bank(ba_, "s")
                use_bank(bb_, "s")
            wait("s", "t")
            wait("s", "v")
            wait("s", "p")
            for k in (0, 1):
                ba_, bb_ = ybank[k]
                st_ = P0S[k]
                Sc.copy(st_["abc"], pb[ba_][:, :])
                Sc.copy(st_["bbc"], pb[bb_][:, :])
                mark("s")
                bank_consumed(ba_, "s")
                bank_consumed(bb_, "s")
            for k in (0, 1):
                st_ = P0S[k]
                wait("v", "s")
                for c in range(CH):
                    V.tensor_mul(st_["y"][:, c, :], P0_xts[slot_of[k]][:, c, :], st_["abc"])
                mark("v")
                wait("p", "v")
                for c in range(CH):
                    P.tensor_add(st_["y"][:, c, :], st_["y"][:, c, :], st_["bbc"])
                mark("p")
                wait("q", "p")
                D_("q", "ysp", out=ys[:, :, tt_of[k] + 1:tt_of[k] + 1 + TILE].transpose([1, 0, 2]),
                   in_=st_["y"])
                if k == 0 and 2 * pr + 3 < NT:
                    tn = (2 * pr + 3) * TILE
                    wait("q", "v")
                    wait("q", "x1sp")
                    D_("q", "xl", out=P0_xts[(2 * pr + 3) % 3],
                       in_=x_in[:, :, tn:tn + TILE].transpose([1, 0, 2]))

        if upto < 1:
            wait("q", "x1sp")
            for i in range(NT):
                t0 = i * TILE
                D_("q", "ol", out=out[:, :, t0:t0 + TILE], in_=x1s[:, :, t0:t0 + TILE])
            if ctx.em == "q":
                Q.real.wait_ge(sems["ol"], ctx.cnt["ol"])
            return

        load_table(ACT_SET_TRIG)
        wait("v", "x1sp")
        wait("v", "ysp")
        V.memset(A1_vtm[:, :, :, 64:65], 1.0)
        mark("v")

        wb_free = {"conv": 0, "rff": 0, "proj": 0}
        wl_ready = {}

        wait("g", "wc0")

        def a1_wl_convrff(prj):
            wait("g", "t", wb_free["conv"])
            D_("g", "wlc0", out=A1_wb[:, 0:48, :], in_=wbd[prj][:, 0:48, :])
            for oc in range(CH):
                wl_ready[("conv", prj, oc)] = ctx.snap("wlc0")
            wait("g", "t", wb_free["rff"])
            D_("g", "wlr0", out=A1_wb[:, 48:96, :], in_=wbd[prj][:, 48:96, :])
            for rc in range(6):
                wl_ready[("rff", prj, rc)] = ctx.snap("wlr0")
        def a1_wl_proj(prj):
            wait("g", "t", wb_free["proj"])
            D_("g", "wlp0", out=A1_wb[:, 96:144, :], in_=wbd[prj][:, 96:144, :])
            for cc in range(CH):
                wl_ready[("proj", prj, cc)] = ctx.snap("wlp0")

        def a1_tile_loads(i):
            t0 = i * TILE
            wait("q", "t")
            wait("q", "ysp")
            wait("q", "x1sp")
            wait("q", "qksp")
            D_("q", "yl", out=A1_y, in_=ys[:, :, t0:t0 + TILE + 2].transpose([1, 0, 2]))
            wait("g", "v")
            wait("g", "wc0")
            D_("g", "rt", out=A1_ct2[i % 2], in_=rcb[:, t0:t0 + TILE])
            D_("g", "rt", out=A1_st2[i % 2], in_=rsb[:, t0:t0 + TILE])

        a1_tile_loads(0)
        a1_wl_convrff(0)
        a1_wl_proj(0)
        h_snap = {}
        sin_snap = {}

        def a1_conv(i, prj):
            A1_h = A1_h2[prj % 2]
            for oc in range(CH):
                bank = oc % 2
                grp = oc // 2
                t_wait_bank(bank)
                wait("t", "wlc0", wl_ready[("conv", prj, oc)])
                wait("t", "yl", yl_gate[i])
                first = True
                for w in range(3):
                    for a in range(2):
                        widx = (w * 2 + a) * 8 + oc
                        T.matmul(pb[bank][:, :], A1_wb[:, widx, :],
                                 A1_y[:, grp * 2 + a, w:w + TILE],
                                 start=first, stop=(w == 2 and a == 1))
                        first = False
                mark("t")
                bank_done(bank)
                use_bank(bank, "s")
                wait("s", "t")
                wait("s", "v")
                Sc.activation(A1_h[:, oc, :], pb[bank][:, :], AF.Identity,
                              bias=cbb[:, prj, oc:oc + 1])
                mark("s")
                bank_consumed(bank, "s")
            wb_free["conv"] = ctx.snap("t")
            h_snap[prj] = ctx.snap("s")

        def a1_rff(prj):
            A1_h = A1_h2[prj % 2]
            for rc in range(6):
                bank = 2 + rc % 2
                gam = rc // 2
                t_wait_bank(bank)
                wait("t", "wlr0", wl_ready[("rff", prj, rc)])
                wait("t", "s", h_snap[prj])
                for a in range(CH):
                    widx = 48 + gam * 16 + a * 2 + (rc % 2)
                    T.matmul(pb[bank][:, :], A1_wb[:, widx, :],
                             A1_h[:, a, :], start=(a == 0), stop=(a == CH - 1))
                mark("t")
                bank_done(bank)
                use_bank(bank, "v")
                wait("v", "t")
                if rc in (0, 3):
                    wait("v", "s")
                V.tensor_scalar(out=A1_u6[:, rc % 3, :], in0=pb[bank][:, :],
                                scalar1=-GSC[gam] * C2PI,
                                scalar2=rbb[:, prj, rc:rc + 1],
                                op0=ALU.mult, op1=ALU.add)
                mark("v")
                bank_consumed(bank, "v")
                if rc == 2 or rc == 5:
                    a1_vchain_half(prj, rc // 3)
            wb_free["rff"] = ctx.snap("t")

        def a1_vchain_half(prj, half):
            A1_z = A1_z2[prj % 2]
            u6f = A1_u6.rearrange("p a b -> p (a b)")
            n6f = A1_n6f.rearrange("p a b -> p (a b)")
            mg6 = A1_mg6.rearrange("p a b -> p (a b)")
            ml6 = A1_ml6.rearrange("p a b -> p (a b)")
            vfence()
            V.tensor_copy(A1_n6.rearrange("p a b -> p (a b)"), u6f)
            vfence()
            V.tensor_copy(n6f, A1_n6.rearrange("p a b -> p (a b)"))
            vfence()
            V.tensor_sub(u6f, u6f, n6f)
            vfence()
            V.tensor_single_scalar(out=mg6, in_=u6f, scalar=0.5, op=ALU.is_gt)
            V.tensor_single_scalar(out=ml6, in_=u6f, scalar=-0.5, op=ALU.is_lt)
            vfence()
            V.tensor_sub(u6f, u6f, mg6)
            vfence()
            V.tensor_add(u6f, u6f, ml6)
            mark("v")
            wait("s", "v")
            Sc.activation(A1_z[:, half * 3:half * 3 + 3, :].rearrange("p a b -> p (a b)"),
                          u6f, AF.Sin, scale=float(2 * math.pi))
            mark("s")
            sin_snap[prj] = ctx.snap("s")

        def a1_proj(i, prj):
            A1_z = A1_z2[prj % 2]
            if prj < 2:
                pre = A1_preq if prj == 0 else A1_prek
                for cc in range(CH):
                    bank = 4 + cc % 2
                    t_wait_bank(bank)
                    wait("t", "wlp0", wl_ready[("proj", prj, cc)])
                    wait("t", "s", sin_snap[prj])
                    for f in range(6):
                        T.matmul(pb[bank][:, :], A1_wb[:, 96 + f * 8 + cc, :],
                                 A1_z[:, f, :], start=(f == 0), stop=(f == 5))
                    mark("t")
                    bank_done(bank)
                    use_bank(bank, "s")
                    wait("s", "t")
                    Sc.activation(pre[:, cc, :], pb[bank][:, :], AF.Identity,
                                  scale=SQ2M, bias=pbb[:, prj, cc:cc + 1])
                    mark("s")
                    bank_consumed(bank, "s")
            else:
                for sub in range(4):
                    for cg in range(2):
                        bank = 4 + (sub * 2 + cg) % 2
                        t_wait_bank(bank)
                        wait("t", "wlp0", wl_ready[("proj", prj, cg)])
                        wait("t", "s", sin_snap[prj])
                        for f in range(6):
                            i0 = 96 + f * 8 + cg * 4
                            T.matmul(pb[bank][:, :],
                                     A1_z[:, f, sub * 128:(sub + 1) * 128],
                                     A1_wb[:, i0:i0 + 4, :].rearrange("p n j -> p (n j)"),
                                     start=(f == 0), stop=False)
                        T.matmul(pb[bank][:, :], ones_pad[:, :],
                                 pbv_row[:, cg * 512:(cg + 1) * 512],
                                 start=False, stop=True)
                        mark("t")
                        bank_done(bank)
                        use_bank(bank, "s")
                        wait("s", "t")
                        Sc.activation(A1_vtm[:, sub, cg * 8:(cg + 1) * 8, 0:64],
                                      pb[bank][:, :].rearrange("p (h d) -> p h d", d=64),
                                      AF.Identity, scale=SQ2M)
                        mark("s")
                        bank_consumed(bank, "s")
            wb_free["proj"] = ctx.snap("t")

        yl_gate = {}
        yl_gate[0] = ctx.snap("yl")
        for i in range(NT):
            t0 = i * TILE
            if i == 0:
                a1_conv(i, 0)
            for prj in range(3):
                a1_rff(prj)
                if prj < 2 or i + 1 < NT:
                    a1_wl_convrff(prj + 1 if prj < 2 else 0)
                if prj < 2:
                    a1_conv(i, prj + 1)
                    if prj == 1 and i + 1 < NT:
                        a1_tile_loads(i + 1)
                        yl_gate[i + 1] = ctx.snap("yl")
                a1_proj(i, prj)
                if prj < 2 or i + 1 < NT:
                    a1_wl_proj(prj + 1 if prj < 2 else 0)

            for which in range(2):
                pre = A1_preq if which == 0 else A1_prek
                dst = A1_qsb if which == 0 else A1_ksb
                rtmp = A1_h2[1]
                wait("s", "v")
                for cc in range(CH):
                    bank = 6 + cc % 2
                    t_wait_bank(bank)
                    wait("t", "s")
                    T.matmul(pb[bank][:, :], pmat_b[:], pre[:, cc, :], start=True, stop=True)
                    mark("t")
                    bank_done(bank)
                    use_bank(bank, "s")
                    wait("s", "t")
                    Sc.copy(A1_rot[:, cc, :], pb[bank][:, :])
                    mark("s")
                    bank_consumed(bank, "s")
                wait("v", "s")
                wait("v", "rt")
                wait("v", "qksp")
                for cc in range(CH):
                    V.tensor_mul(rtmp[:, cc, :], pre[:, cc, :], A1_ct2[i % 2])
                for cc in range(CH):
                    V.tensor_mul(A1_rot[:, cc, :], A1_rot[:, cc, :], A1_st2[i % 2])
                vfence()
                for cc in range(CH):
                    V.tensor_add(dst[:, cc, :], rtmp[:, cc, :], A1_rot[:, cc, :])
                mark("v")
                if which == 0:
                    wait("q", "v")
                    D_("q", "qksp", out=qs[:, :, t0:t0 + TILE].transpose([1, 0, 2]),
                       in_=dst)
            load_table(ACT_SET_LNEXP)
            vfence()
            V.tensor_mul(A2_ksq.rearrange("p c t -> p (c t)"),
                         A1_ksb.rearrange("p c t -> p (c t)"),
                         A1_ksb.rearrange("p c t -> p (c t)"))
            mark("v")
            wait("t", "v")
            for cc in range(CH):
                bank = cc % 2
                t_wait_bank(bank)
                T.matmul(pb[bank][:, :], lm_bd[:, :],
                         A1_ksb[:, cc, :], start=True, stop=False)
                T.matmul(pb[bank][:, :], cm_bd[:, :],
                         A2_ksq[:, cc, :], start=False, stop=True)
                mark("t")
                bank_done(bank)
                use_bank(bank, "s")
                wait("s", "t")
                wait("s", "v")
                Sc.activation(A2_khat[:, cc, :], pb[bank][:, :],
                              AF.Exp, bias=nybias[:, 0:1])
                mark("s")
                bank_consumed(bank, "s")
            wait("v", "s")
            V.tensor_mul(A2_khsq.rearrange("p c t -> p (c t)"),
                         A2_khat.rearrange("p c t -> p (c t)"),
                         A2_khat.rearrange("p c t -> p (c t)"))
            mark("v")
            if debug_dump and i == NT - 1:
                wait("q", "s")
                D_("q", "ol", out=khdbg, in_=A2_khat[:])
            if i + 1 < NT:
                a1_conv(i + 1, 0)
            wait("t", "v")
            kp_d = [A2_kp, A2_kp2]
            s_snap = {}

            def kv_emit(hh):
                kbank = 4 + hh % 2
                t_wait_bank(kbank)
                wait("t", "s", s_snap[hh])
                for sub in range(4):
                    T.matmul(pb[kbank][0:65, 0:256], A1_vtm[:, sub, hh, :],
                             kp_d[hh % 2][:, sub, :], start=(sub == 0), stop=(sub == 3))
                mark("t")
                bank_done(kbank)
                use_bank(kbank, "v")
                wait("v", "t")
                if i == 0:
                    V.tensor_copy(kvacc[:, hh, :], pb[kbank][0:65, 0:256])
                else:
                    V.tensor_add(kvacc[:, hh, :], kvacc[:, hh, :], pb[kbank][0:65, 0:256])
                mark("v")
                bank_consumed(kbank, "v")

            for h in range(NH):
                base, cc = (h % 2) * 64, h // 2
                for sub in range(4):
                    bank = 2 + sub % 2
                    t_wait_bank(bank)
                    T.matmul(pb[bank][:, 0:256],
                             A2_khat[:, cc, sub * 128:(sub + 1) * 128],
                             om_z[:, h % 2, :], start=True, stop=False)
                    T.matmul(pb[bank][:, 0:256],
                             A2_khsq[:, cc, sub * 128:(sub + 1) * 128],
                             cm_z[:, h % 2, :], start=False, stop=True)
                    mark("t")
                    bank_done(bank)
                    use_bank(bank, "s")
                    wait("s", "t")
                    Sc.activation(kp_d[h % 2][:, sub, :], pb[bank][:, 0:256], AF.Exp,
                                  bias=negln16[:, 0:1])
                    mark("s")
                    bank_consumed(bank, "s")
                s_snap[h] = ctx.snap("s")
                if h > 0:
                    kv_emit(h - 1)
            kv_emit(NH - 1)
            if i + 1 < NT:
                load_table(ACT_SET_TRIG)

        if upto < 2:
            wait("q", "x1sp")
            wait("q", "qksp")
            for i in range(NT):
                t0 = i * TILE
                D_("q", "ol", out=out[:, :, t0:t0 + TILE], in_=x1s[:, :, t0:t0 + TILE])
            if ctx.em == "q":
                Q.real.wait_ge(sems["ol"], ctx.cnt["ol"])
            return

        for h in range(NH):
            wait("v", "t")
            wait("v", "s")
            if h == 0:
                vfence()
            V.tensor_copy(kvbfT[:], kvacc[:, h, :])
            mark("v")
            wait("t", "v")
            t_wait_bank(6)
            T.transpose(pbh[6][0:128, 0:65], kvbfT[0:65, 0:128], ident_b[0:65, 0:65])
            T.transpose(pbh[6][0:128, 66:131], kvbfT[0:65, 128:256], ident_b[0:65, 0:65])
            mark("t")
            bank_done(6)
            use_bank(6, "s")
            wait("s", "t")
            Sc.copy(kvp[:, 0, h, :], pbh[6][0:128, 0:65])
            Sc.copy(kvp[:, 1, h, :], pbh[6][0:128, 66:131])
            mark("s")
            bank_consumed(6, "s")

        if upto < 3:
            wait("q", "x1sp")
            for i in range(NT):
                t0 = i * TILE
                D_("q", "ol", out=out[:, :, t0:t0 + TILE], in_=x1s[:, :, t0:t0 + TILE])
            if ctx.em == "q":
                Q.real.wait_ge(sems["ol"], ctx.cnt["ol"])
            return


        if debug_dump:
            wait("q", "s")
            D_("q", "ol", out=kvdbg, in_=kvp[:])
        load_table(ACT_SET_LNEXP)
        wait("g", "t")
        wait("g", "v")
        wait("g", "s")
        for _ws in range(8):
            wait("g", f"wlc{_ws}")
            wait("g", f"wlp{_ws}")
        for _ws in range(6):
            wait("g", f"wlr{_ws}")
        D_("g", "gsu", out=owb[:],
           in_=out_w.rearrange("(dc p) (cc j) -> p dc cc j", p=128, j=128))
        Bx1d = [B_x1, B_x1b]
        Bqsd = [B_qsb, B_qsbb]
        Bx2d = [B_x2, B_x2b]
        Botm = [B_otm, B_otmb]

        def b_loads(i):
            t0 = i * TILE
            if i == 0:
                wait("q", "t")
                wait("q", "v")
                wait("q", "s")
            wait("q", "x1sp")
            wait("q", "qksp")
            wait("q", "ol")
            D_("q", f"x1l{i % 2}", out=Bx1d[i % 2],
               in_=x1s[:, :, t0:t0 + TILE].transpose([1, 0, 2]))
            D_("q", f"ql{i % 2}", out=Bqsd[i % 2],
               in_=qs[:, :, t0:t0 + TILE].transpose([1, 0, 2]))

        def t_cast(j):
            x2b = Bx2d[j % 2]
            wait("p", "v")
            wait("p", "t")
            P.tensor_copy(B_xbf.rearrange("p c t -> p (c t)"),
                          x2b.rearrange("p c t -> p (c t)"))
            P.tensor_mul(B_xsq.rearrange("p c t -> p (c t)"),
                         x2b.rearrange("p c t -> p (c t)"),
                         x2b.rearrange("p c t -> p (c t)"))
            mark("p")

        def t_stats_T(j):
            wait("t", "p")
            t_wait_bank(6)
            t_wait_bank(7)
            for c in range(CH):
                T.matmul(pb[6][:, :], onesm[:, :], B_xbf[:, c, :],
                         start=(c == 0), stop=(c == CH - 1))
            for c in range(CH):
                T.matmul(pb[7][:, :], onesm[:, :], B_xsq[:, c, :],
                         start=(c == 0), stop=(c == CH - 1))
            mark("t")
            bank_done(6)
            bank_done(7)
            use_bank(6, "v")
            use_bank(7, "v")

        def t_stats_VS(j):
            use_bank(6, "s")
            use_bank(7, "s")
            wait("s", "t")
            Sc.activation(stat_m[:], pb[6][0:1, :], AF.Copy, scale=1.0 / float(D))
            Sc.activation(stat_e2[:], pb[7][0:1, :], AF.Copy, scale=1.0 / float(D),
                          bias=float(EPS_LN))
            sfence()
            Sc.activation(stat_m2[:], stat_m[:], AF.Square)
            mark("s")
            bank_consumed(6, "s")
            bank_consumed(7, "s")
            wait("v", "s")
            V.tensor_sub(stat_var[:], stat_e2[:], stat_m2[:])
            mark("v")
            wait("s", "v")
            Sc.activation(stat_ln[:], stat_var[:], AF.Ln)
            sfence()
            Sc.activation(arow_f[:], stat_ln[:], AF.Exp, scale=-0.5)
            sfence()
            Sc.activation(arow_b[:], arow_f[:], AF.Copy)
            mark("s")
            wait("v", "s")
            V.scalar_tensor_tensor(out=b0row_b[:], in0=stat_m[:], scalar=-1.0,
                                   in1=arow_f[:], op0=ALU.mult, op1=ALU.mult)
            mark("v")

        def t_ffn_T(j):
            t_wait_bank(6)
            t_wait_bank(7)
            wait("t", "v")
            wait("t", "s")
            for half in range(2):
                cols = slice(half * 256, half * 256 + 256)
                po = slice(half * 64, half * 64 + 64)
                for c in range(CH):
                    T.matmul(pb[6][po, 0:256], winb[:, c, :], B_xbf[:, c, cols],
                             start=(c == 0), stop=False)
                T.matmul(pb[6][po, 0:256], winsum[:], b0row_b[0:1, cols],
                         start=False, stop=True)
                T.matmul(pb[7][po, 0:256], ones_row[0:1, 0:64], arow_b[0:1, cols],
                         start=True, stop=True)
            mark("t")
            bank_done(6)
            bank_done(7)

        def t_ffn_SV(j):
            use_bank(7, "s")
            wait("s", "t")
            Sc.copy(B_apk, pb[7][:, 0:256])
            mark("s")
            bank_consumed(7, "s")
            use_bank(6, "v")
            wait("v", "s")
            V.tensor_mul(B_s, pb[6][:, 0:256], B_apk)
            mark("v")
            bank_consumed(6, "v")
            wait("s", "v")
            for kk in range(16):
                Sc.activation(B_h[:, kk, :], B_s, AF.Relu,
                              bias=b1p[:, kk:kk + 1], scale=w1p[:, kk:kk + 1])
            mark("s")

        def t_tree(j):
            wait("v", "s")
            V.tensor_scalar(out=B_pp[:, 0, :], in0=B_h[:, 0, :], scalar1=w2p[:, 0:1],
                            scalar2=b2c[:, 0:1], op0=ALU.mult, op1=ALU.add)
            for kk in range(1, 16):
                V.tensor_scalar_mul(B_pp[:, kk, :], B_h[:, kk, :], w2p[:, kk:kk + 1])
            vfence()
            for jj in range(8):
                V.tensor_add(B_tr1[:, jj, :], B_pp[:, 2 * jj, :], B_pp[:, 2 * jj + 1, :])
            vfence()
            for jj in range(4):
                V.tensor_add(B_tr2[:, jj, :], B_tr1[:, 2 * jj, :], B_tr1[:, 2 * jj + 1, :])
            vfence()
            for jj in range(2):
                V.tensor_add(B_tr3[:, jj, :], B_tr2[:, 2 * jj, :], B_tr2[:, 2 * jj + 1, :])
            vfence()
            V.tensor_add(B_g, B_tr3[:, 0, :], B_tr3[:, 1, :])
            mark("v")

        def t_out(j):
            x1b, x2b = Bx1d[j % 2], Bx2d[j % 2]
            wait("t", "v")
            for c in range(CH):
                bank = 6 + c % 2
                t_wait_bank(bank)
                T.matmul(pb[bank][:, 0:256], woutd[0:64, c, :], B_g[0:64, :],
                         start=True, stop=False)
                T.matmul(pb[bank][:, 0:256], bout_row[:, c * 128:(c + 1) * 128],
                         ones512[:, 0:256], start=False, stop=True)
                T.matmul(pb[bank][:, 256:512], woutd[64:128, c, :], B_g[64:128, :],
                         start=True, stop=False)
                T.matmul(pb[bank][:, 256:512], bout_row[:, c * 128:(c + 1) * 128],
                         ones512[:, 0:256], start=False, stop=True)
                mark("t")
                bank_done(bank)
                use_bank(bank, "v")
                wait("v", "t")
                V.scalar_tensor_tensor(out=x1b[:, c, :], in0=pb[bank][:, :],
                                       scalar=0.5, in1=x2b[:, c, :],
                                       op0=ALU.mult, op1=ALU.add)
                mark("v")
                bank_consumed(bank, "v")
            wait("q", "v")
            t0 = j * TILE
            D_("q", "ol", out=out[:, :, t0:t0 + TILE].transpose([1, 0, 2]), in_=x1b)

        def b_transpose_pair(p):
            obank = 2 + p % 2
            cc = p
            t_wait_bank(obank)
            wait("t", "v")
            for sub in range(4):
                T.transpose(pbh[obank][0:128, sub * 128:(sub + 1) * 128],
                            Botm[p % 2][:, sub, :], ident_b[:, :])
            mark("t")
            bank_done(obank)
            use_bank(obank, "s")
            wait("s", "t")
            Sc.copy(B_ofm[:, cc, :], pbh[obank][:, 0:512])
            mark("s")
            bank_consumed(obank, "s")

        def b_attn(i):
            x1b, qsb, x2b = Bx1d[i % 2], Bqsd[i % 2], Bx2d[i % 2]
            tj = i - 1
            if tj >= 0:
                t_cast(tj)
            wait("v", f"ql{i % 2}")
            wait("v", "t")
            V.tensor_mul(B_qsq.rearrange("p c t -> p (c t)"),
                         qsb.rearrange("p c t -> p (c t)"),
                         qsb.rearrange("p c t -> p (c t)"))
            mark("v")
            wait("t", "v")
            qp_d = [B_qp, B_qp2]
            qsnap = {}

            def b_post(hh):
                nbank = 4 + hh % 2
                t_wait_bank(nbank)
                wait("t", "s", qsnap[hh])
                for sub in range(4):
                    for ec in range(2):
                        T.matmul(pb[nbank][:, sub * 65:sub * 65 + 65],
                                 qp_d[hh % 2][:, ec, sub * 128:(sub + 1) * 128],
                                 kvp[:, ec, hh, :], start=(ec == 0), stop=(ec == 1))
                mark("t")
                bank_done(nbank)
                if hh % 2 == 1:
                    if hh >= 3:
                        b_transpose_pair((hh - 1) // 2 - 1)
                    use_bank(4, "v")
                    use_bank(5, "v")
                    wait("v", "t")
                    for hx in (hh - 1, hh):
                        nb = 4 + hx % 2
                        for sub in range(4):
                            cc8 = (hx % 2) * 4 + sub
                            V.tensor_single_scalar(out=dcol[:, cc8:cc8 + 1],
                                                   in_=pb[nb][:, sub * 65 + 64:sub * 65 + 65],
                                                   scalar=EPS_ATT, op=ALU.add)
                    vfence()
                    V.reciprocal(rcol[:, 0:8], dcol[:, 0:8])
                    vfence()
                    for hx in (hh - 1, hh):
                        nb = 4 + hx % 2
                        bh = (hx % 2) * 64
                        for sub in range(4):
                            cc8 = (hx % 2) * 4 + sub
                            V.tensor_scalar_mul(Botm[(hx // 2) % 2][:, sub, bh:bh + 64],
                                                pb[nb][:, sub * 65:sub * 65 + 64],
                                                rcol[:, cc8:cc8 + 1])
                    mark("v")
                    bank_consumed(4, "v")
                    bank_consumed(5, "v")

            for h in range(NH):
                base, cc = (h % 2) * 64, h // 2
                for ec in range(2):
                    bank = (0 if h % 2 == 0 else 2) + ec
                    t_wait_bank(bank)
                    T.matmul(pb[bank][:, :], om_z[:, h % 2, ec * 128:(ec + 1) * 128],
                             qsb[:, cc, :], start=True, stop=False)
                    T.matmul(pb[bank][:, :], cm_z[:, h % 2, 0:128],
                             B_qsq[:, cc, :], start=False, stop=True)
                    mark("t")
                    bank_done(bank)
                    use_bank(bank, "s")
                    wait("s", "t")
                    Sc.activation(qp_d[h % 2][:, ec, :], pb[bank][:, :], AF.Exp,
                                  bias=negln16[:, 0:1])
                    mark("s")
                    bank_consumed(bank, "s")
                qsnap[h] = ctx.snap("s")
                if h > 0:
                    b_post(h - 1)
                if tj >= 0:
                    if h == 4:
                        t_stats_T(tj)
                    elif h == 6:
                        t_stats_VS(tj)
                    elif h == 9:
                        t_ffn_T(tj)
                    elif h == 11:
                        t_ffn_SV(tj)
            b_post(NH - 1)
            b_transpose_pair(NH // 2 - 1)
            if tj >= 0:
                t_tree(tj)
            wait("t", "s")
            wait("t", "gsu")
            for c in range(CH):
                bank = 4 + c % 2
                t_wait_bank(bank)
                for dc in range(CH):
                    T.matmul(pb[bank][:, :], owb[:, dc, c, :], B_ofm[:, dc, :],
                             start=(dc == 0), stop=False)
                T.matmul(pb[bank][:, :], ob_row[:, c * 128:(c + 1) * 128], ones512[:, :],
                         start=False, stop=True)
                mark("t")
                bank_done(bank)
                use_bank(bank, "v")
                wait("v", "t")
                wait("v", f"x1l{i % 2}")
                wait("v", "s")
                V.scalar_tensor_tensor(out=x2b[:, c, :], in0=pb[bank][:, :], scalar=1.0,
                                       in1=x1b[:, c, :], op0=ALU.mult, op1=ALU.add)
                mark("v")
                bank_consumed(bank, "v")
            if tj >= 0:
                t_out(tj)

        b_loads(0)
        for i in range(NT):
            b_attn(i)
            if i + 1 < NT:
                b_loads(i + 1)
        j = NT - 1
        t_cast(j)
        t_stats_T(j)
        t_stats_VS(j)
        t_ffn_T(j)
        t_ffn_SV(j)
        t_tree(j)
        t_out(j)

        if ctx.em == "q":
            Q.real.wait_ge(sems["ol"], ctx.cnt["ol"])

    with nc.Block() as block:
        def make_body(name):
            def body(eng):
                engines = {}
                for n in Ctx.ENGS:
                    real = eng if (n == name or (name == "g" and n == "p")) else None
                    engines[n] = Eng(real)
                walk(Ctx(name, engines, sems))
            return body
        block.sync(make_body("q"))
        block.gpsimd(make_body("g"))
        block.tensor(make_body("t"))
        block.vector(make_body("v"))
        block.scalar(make_body("s"))
    return nc


_NC_CACHE = None


def kernel(**inputs):
    global _NC_CACHE
    xs = np.ascontiguousarray(np.asarray(inputs["L0_attn_x"], dtype=np.float32))

    t = np.arange(S, dtype=np.float64)
    inv = 1.0 / (10000.0 ** (np.arange(0, 64, 2, dtype=np.float64) / 64.0))
    f = t[:, None] * inv[None, :]
    sin64 = np.sin(f)
    cos64 = np.cos(f)
    cos_t = np.empty((128, S), dtype=np.float32)
    sin_t = np.empty((128, S), dtype=np.float32)
    for r in range(128):
        cos_t[r] = cos64[:, (r % 64) % 32]
        sin_t[r] = sin64[:, (r % 64) % 32]
    P64 = np.zeros((64, 64), dtype=np.float32)
    for j in range(32):
        P64[j + 32, j] = -1.0
        P64[j, j + 32] = 1.0
    pmat = np.zeros((128, 128), dtype=np.float32)
    pmat[0:64, 0:64] = P64
    pmat[64:128, 64:128] = P64
    ident = np.eye(128, dtype=np.float32)

    common = {
        "conv_k": inputs["L0_attn_conv_k"], "conv_b": inputs["L0_attn_conv_b"],
        "rff_w": inputs["L0_attn_rff_w"], "rff_b": inputs["L0_attn_rff_b"],
        "proj_w": inputs["L0_attn_proj_w"], "proj_b": inputs["L0_attn_proj_b"],
        "omega": inputs["L0_attn_omega"], "landmarks": inputs["L0_attn_landmarks"],
        "out_w": inputs["L0_attn_out_w"], "out_b": inputs["L0_attn_out_b"],
        "ffn_win": inputs["L0_ffn_Win"], "ffn_bin": inputs["L0_ffn_bin_"],
        "ffn_w1": inputs["L0_ffn_W1"], "ffn_b1": inputs["L0_ffn_b1"],
        "ffn_w2": inputs["L0_ffn_W2"], "ffn_b2": inputs["L0_ffn_b2"],
        "ffn_wout": inputs["L0_ffn_Wout"], "ffn_bout": inputs["L0_ffn_bout"],
        "rope_c": cos_t, "rope_s": sin_t, "pmat": pmat, "ident": ident,
    }
    common = {k: np.ascontiguousarray(np.asarray(v, dtype=np.float32)) for k, v in common.items()}

    if _NC_CACHE is None:
        _NC_CACHE = build_program()
    nc = _NC_CACHE

    in_maps = []
    for b in range(NCORES):
        m = dict(common)
        m["x"] = np.ascontiguousarray(xs[b].T.reshape(CH, 128, S))
        in_maps.append(m)
    res = run_bass_kernel_spmd(nc, in_maps, core_ids=list(range(NCORES)))
    outs = []
    for b in range(NCORES):
        o = np.asarray(res.results[b]["out"])
        outs.append(o.reshape(D, S).T.astype(np.float32))
    return np.stack(outs, axis=0)


# version 33
# speedup vs baseline: 1.0077x; 1.0066x over previous
import math
import contextlib
import numpy as np

import concourse.bass as bass
import concourse.mybir as mybir
from concourse.bass_utils import run_bass_kernel_spmd

F32 = mybir.dt.float32
BF16 = mybir.dt.bfloat16
I32 = mybir.dt.int32
AF = mybir.ActivationFunctionType
ALU = mybir.AluOpType

NCORES = 8
S = 4096
D = 1024
CH = 8
TILE = 512
NT = S // TILE
NH = 16
EPS_LN = 1e-5
EPS_ATT = 1e-6
LN16 = math.log(16.0)
C2PI = 1.0 / (2.0 * math.pi)
SQ2M = math.sqrt(2.0 / 256.0)
GSC = [math.sqrt(2.0 * g) for g in (0.5, 1.0, 2.0)]

ACT_SET_LNEXP = 6
ACT_SET_TRIG = 9


class Eng:
    def __init__(self, real):
        self.real = real
        self.last = None
        self._ctx = None
        self._name = None

    def __getattr__(self, name):
        if self.real is None:
            def dummy(*a, **k):
                self.last = "dummy"
            return dummy
        f = getattr(self.real, name)

        def wrap(*a, **k):
            self.last = f(*a, **k)
            return self.last
        return wrap


class Ctx:
    ENGS = ("q", "g", "t", "v", "s", "p")
    STREAMS = ("su", "gsu", "wc0", "wc1", "gc0", "gc1", "gs0", "gs1", "xl",
               "x1sp", "ysp", "yl", "wl", "qksp", "vsp", "kl", "vl", "x1l",
               "ql", "x1l0", "x1l1", "ql0", "ql1", "ol", "rt") \
        + tuple(f"wlc{i}" for i in range(8)) \
        + tuple(f"wlr{i}" for i in range(6)) \
        + tuple(f"wlp{i}" for i in range(8))

    def __init__(self, em, engines, sems):
        self.em = em
        self.e = engines
        self.sems = sems
        self.cnt = {n: 0 for n in self.ENGS + self.STREAMS}
        self.lastw = {}
        for n, e in engines.items():
            e._ctx = self
            e._name = n

    def dma(self, eng, stream, **kw):
        self.cnt[stream] += 16
        e = self.e[eng]
        if e.real is not None:
            ins = e.real.dma_start(**kw)
            ins.then_inc(self.sems[stream], 16)

    def mark(self, name, inc=1):
        eng = self.e[name]
        if eng.last is None:
            return
        self.cnt[name] += inc
        if eng.real is not None:
            eng.last.then_inc(self.sems[name], inc)
        eng.last = None

    def wait(self, waiter, producer, val=None):
        if val is None:
            val = self.cnt[producer]
        if val <= 0:
            return
        key = (waiter, producer)
        if self.lastw.get(key, 0) >= val:
            return
        self.lastw[key] = val
        em_name = "g" if waiter == "p" else waiter
        if self.em == em_name:
            self.e[waiter].real.wait_ge(self.sems[producer], val)

    def snap(self, name):
        return self.cnt[name]


def build_program(upto=3, debug_dump=False):
    nc = bass.Bass("TRN2", target_bir_lowering=False, debug=False)
    dt_ = nc.dram_tensor

    x_in = dt_("x", [CH, 128, S], F32, kind="ExternalInput").ap()
    conv_k = dt_("conv_k", [3, 3, 256, 1024], F32, kind="ExternalInput").ap()
    conv_b = dt_("conv_b", [3, 1024], F32, kind="ExternalInput").ap()
    rff_w = dt_("rff_w", [3, 3, 1024, 256], F32, kind="ExternalInput").ap()
    rff_b = dt_("rff_b", [3, 3, 256], F32, kind="ExternalInput").ap()
    proj_w = dt_("proj_w", [3, 768, 1024], F32, kind="ExternalInput").ap()
    proj_b = dt_("proj_b", [3, 1024], F32, kind="ExternalInput").ap()
    omega = dt_("omega", [64, 256], F32, kind="ExternalInput").ap()
    landm = dt_("landmarks", [64, 64], F32, kind="ExternalInput").ap()
    out_w = dt_("out_w", [1024, 1024], F32, kind="ExternalInput").ap()
    out_b = dt_("out_b", [1024], F32, kind="ExternalInput").ap()
    f_win = dt_("ffn_win", [1024, 64], F32, kind="ExternalInput").ap()
    f_bin = dt_("ffn_bin", [64], F32, kind="ExternalInput").ap()
    f_w1 = dt_("ffn_w1", [64, 16], F32, kind="ExternalInput").ap()
    f_b1 = dt_("ffn_b1", [64, 16], F32, kind="ExternalInput").ap()
    f_w2 = dt_("ffn_w2", [64, 16], F32, kind="ExternalInput").ap()
    f_b2 = dt_("ffn_b2", [64], F32, kind="ExternalInput").ap()
    f_wout = dt_("ffn_wout", [64, 1024], F32, kind="ExternalInput").ap()
    f_bout = dt_("ffn_bout", [1024], F32, kind="ExternalInput").ap()
    rope_c = dt_("rope_c", [128, S], F32, kind="ExternalInput").ap()
    rope_s = dt_("rope_s", [128, S], F32, kind="ExternalInput").ap()
    pmat = dt_("pmat", [128, 128], F32, kind="ExternalInput").ap()
    ident_in = dt_("ident", [128, 128], F32, kind="ExternalInput").ap()

    out = dt_("out", [CH, 128, S], F32, kind="ExternalOutput").ap()
    kvdbg = dt_("kvdbg", [128, 2, NH, 65], BF16, kind="ExternalOutput").ap() if debug_dump else None
    khdbg = dt_("khdbg", [128, CH, 512], BF16, kind="ExternalOutput").ap() if debug_dump else None
    ofdbg = dt_("ofdbg", [128, CH, 512], BF16, kind="ExternalOutput").ap() if debug_dump else None
    x2dbg = dt_("x2dbg", [128, CH, 512], F32, kind="ExternalOutput").ap() if debug_dump else None
    owdbg = dt_("owdbg", [128, 8, 8, 128], BF16, kind="ExternalOutput").ap() if debug_dump else None
    x1bdbg = dt_("x1bdbg", [128, CH, 512], F32, kind="ExternalOutput").ap() if debug_dump else None
    lndbg = dt_("lndbg", [1, 4, 512], F32, kind="ExternalOutput").ap() if debug_dump else None

    x1s = dt_("x1s", [CH, 128, S], F32, kind="Internal").ap()
    wbd = dt_("wbd", [3, 128, 144, 128], BF16, kind="Internal").ap()
    ys = dt_("ys", [CH, 128, S + 2], BF16, kind="Internal").ap()
    qs = dt_("qs", [CH, 128, S], BF16, kind="Internal").ap()

    al = lambda n, sh, dty: nc.alloc_sbuf_tensor(n, sh, dty).ap()

    winb = al("winb", [128, 8, 64], BF16)
    woutd = al("woutd", [128, 8, 128], BF16)
    om_z = al("om_z", [128, 2, 256], BF16)
    cm_z = al("cm_z", [128, 2, 256], BF16)
    lm_bd = al("lm_bd", [128, 128], BF16)
    cm_bd = al("cm_bd", [128, 128], BF16)
    onesm = al("onesm", [128, 128], BF16)
    l2d = al("l2d", [128, 64], BF16)
    pmat_b = al("pmat_b", [128, 128], BF16)
    ident_b = al("ident_b", [128, 128], BF16)
    ones_row = al("ones_row", [1, 128], BF16)
    ones_pad = al("ones_pad", [128, 128], BF16)
    ones512 = al("ones512", [128, 512], BF16)
    negln16 = al("negln16", [128, 1], F32)
    epsl = al("epsl", [128, 1], F32)
    nybias = al("nybias", [128, 1], F32)
    w1p = al("w1p", [128, 16], F32)
    b1p = al("b1p", [128, 16], F32)
    w2p = al("w2p", [128, 16], F32)
    b2c = al("b2c", [128, 1], F32)
    binc = al("binc", [128, 1], F32)
    winsum = al("winsum", [1, 64], BF16)
    bout_row = al("bout_row", [128, 1024], BF16)
    ob_row = al("ob_row", [128, 1024], BF16)
    pbv_row = al("pbv_row", [128, 1024], BF16)
    pbv_f = al("pbv_f", [1, 1024], F32)
    cbb = al("cbb", [128, 3, 8], F32)
    pbb = al("pbb", [128, 3, 8], F32)
    rbbr = al("rbbr", [128, 3, 6], F32)
    rbb = al("rbb", [128, 3, 6], F32)
    lm_f = al("lm_f", [128, 64], F32)
    lm_T = al("lm_T", [128, 64], F32)
    lsq_sb = al("lsq_sb", [128, 64], F32)
    kvbfT = al("kvbfT", [65, 256], BF16)
    kvp = al("kvp", [128, 2, NH, 65], BF16)
    stat_m = al("stat_m", [1, 512], F32)
    stat_e2 = al("stat_e2", [1, 512], F32)
    stat_m2 = al("stat_m2", [1, 512], F32)
    stat_var = al("stat_var", [1, 512], F32)
    stat_ln = al("stat_ln", [1, 512], F32)
    arow_f = al("arow_f", [1, 512], F32)
    arow_b = al("arow_b", [1, 512], BF16)
    b0row_b = al("b0row_b", [1, 512], BF16)
    dcol = al("dcol", [128, 8], F32)
    rcol = al("rcol", [128, 8], F32)
    zpad = al("zpad", [128, 8], BF16)

    ARN = 166 * 1024 + 2048
    arena = nc.alloc_sbuf_tensor("arena", [128, ARN], mybir.dt.uint8).ap()

    def carve(off, dty, shape):
        n = int(np.prod(shape))
        sz = {F32: 4, BF16: 2, I32: 4}[dty]
        sl = arena[:, off:off + n * sz].bitcast(dty)
        if len(shape) == 1:
            return sl
        spec = " ".join(f"d{i}" for i in range(len(shape)))
        kw = {f"d{i}": shape[i] for i in range(len(shape))}
        return sl.rearrange(f"p ({spec}) -> p {spec}", **kw)

    P0_xt = carve(0, F32, [CH, 512])
    P0_x1 = carve(16384, F32, [CH, 512])
    P0_ytmp = carve(32768, F32, [CH, 512])
    P0_xbf = carve(49152, BF16, [CH, 512])
    P0_xsq = carve(57344, BF16, [CH, 512])
    P0_abc = carve(65536, BF16, [512])
    P0_bbc = carve(66560, BF16, [512])
    P0_s = carve(67584, F32, [256])
    P0_h = carve(68608, BF16, [16, 256])
    P0_pp = carve(76800, BF16, [16, 256])
    P0_tr1 = carve(84992, F32, [8, 256])
    P0_tr2 = carve(93184, F32, [4, 256])
    P0_tr3 = carve(97280, F32, [2, 256])
    P0_g = carve(99328, BF16, [256])
    P0_y = carve(99840, BF16, [CH, 512])
    P0_apk = carve(108032, BF16, [256])

    def p0set(base):
        xbf = carve(base + 16384, BF16, [CH, 512])
        return dict(
            xt=carve(base + 0, F32, [CH, 512]),
            xbf=xbf,
            y=xbf,
            xsq=carve(base + 24576, BF16, [CH, 512]),
            s=carve(base + 32768, F32, [256]),
            h=carve(base + 33792, BF16, [16, 256]),
            apk=carve(base + 41984, BF16, [256]),
            abc=carve(base + 42496, BF16, [512]),
            bbc=carve(base + 43520, BF16, [512]),
            fr=carve(base + 44544, F32, [6, 512]),
            br=carve(base + 56832, BF16, [4, 512]),
        )
    P0S = [p0set(0), p0set(60928)]
    P0_pp2 = carve(121856, BF16, [16, 256])
    P0_tr1b = carve(130048, F32, [8, 256])
    P0_tr2b = carve(138240, F32, [4, 256])
    P0_tr3b = carve(142336, F32, [2, 256])
    P0_g2 = carve(144384, BF16, [256])
    P0_xts = [P0S[0]["xt"], P0S[1]["xt"], carve(144896, F32, [CH, 512])]

    A1_y = carve(0, BF16, [CH, 514])
    A1_h2 = [carve(8224, BF16, [CH, 512]), carve(147648, BF16, [CH, 512])]
    A1_z2 = [carve(16416, BF16, [6, 512]), carve(155840, BF16, [6, 512])]
    A1_preq = carve(22560, BF16, [CH, 512])
    A1_prek = carve(30752, BF16, [CH, 512])
    A1_qsb = carve(38944, BF16, [CH, 512])
    A1_ksb = carve(47136, BF16, [CH, 512])
    A1_vtm = carve(55328, BF16, [4, NH, 65])
    A1_ct = carve(63680, BF16, [512])
    A1_st = carve(64704, BF16, [512])
    A1_rot = carve(65728, BF16, [CH, 512])
    A1_u6 = carve(73920, F32, [3, 512])
    A1_n6 = carve(80064, I32, [3, 512])
    A1_n6f = carve(80064, F32, [3, 512])
    A1_mg6 = carve(86208, BF16, [3, 512])
    A1_ml6 = carve(89280, BF16, [3, 512])
    A1_wb = carve(110784, BF16, [144, 128])
    A1_ct2 = [A1_ct, carve(161984, BF16, [512])]
    A1_st2 = [A1_st, carve(163008, BF16, [512])]
    A2_ksq = carve(65728, BF16, [CH, 512])
    A2_khsq = A2_ksq
    A2_khat = carve(147648, BF16, [CH, 512])
    kvacc = carve(92352, F32, [NH, 256])[0:65]
    A2_kp = carve(108736, BF16, [4, 256])
    A2_kp2 = carve(166976, BF16, [4, 256])


    B_x1 = carve(0, F32, [CH, 512])
    B_qsb = carve(16384, BF16, [CH, 512])
    B_qsq = carve(24576, BF16, [CH, 512])
    B_qp = carve(32768, BF16, [2, 512])
    B_otm = carve(34816, BF16, [4, 128])
    B_ofm = carve(35840, BF16, [CH, 512])
    B_x2 = carve(44032, F32, [CH, 512])
    B_xbf = carve(60416, BF16, [CH, 512])
    B_xsq = carve(68608, BF16, [CH, 512])
    B_s = carve(76800, F32, [256])
    B_h = carve(77824, BF16, [16, 256])
    B_pp = carve(86016, BF16, [16, 256])
    B_tr1 = carve(94208, F32, [8, 256])
    B_tr2 = carve(102400, F32, [4, 256])
    B_tr3 = carve(106496, F32, [2, 256])
    B_g = carve(108544, BF16, [256])
    B_apk = carve(109056, BF16, [256])
    owb = carve(109632, BF16, [8, 8, 128])
    B_x1b = carve(126016, F32, [CH, 512])
    B_qsbb = carve(142400, BF16, [CH, 512])
    B_x2b = carve(150592, F32, [CH, 512])
    B_otmb = carve(166976, BF16, [4, 128])
    B_qp2 = carve(168000, BF16, [2, 512])

    pb = [nc.alloc_psum_tensor(f"pb{i}", [128, 512], F32).ap() for i in range(8)]
    pbh = [p.bitcast(BF16) for p in pb]

    es = contextlib.ExitStack()
    sems = {n: es.enter_context(nc.semaphore(f"sem_{n}")) for n in Ctx.ENGS + Ctx.STREAMS}

    def walk(ctx: Ctx):
        Q, G, T, V, Sc, P = (ctx.e[n] for n in ("q", "g", "t", "v", "s", "p"))
        mark, wait = ctx.mark, ctx.wait
        prod, cons = {}, {}

        def t_wait_bank(i):
            if i in cons:
                wait("t", cons[i][0], cons[i][1])

        def bank_done(i):
            prod[i] = ("t", ctx.cnt["t"])

        def use_bank(i, eng):
            if i in prod:
                wait(eng, prod[i][0], prod[i][1])

        def bank_consumed(i, eng):
            cons[i] = (eng, ctx.cnt[eng])

        def vfence():
            mark("v")
            wait("v", "v")

        def sfence():
            mark("s")
            wait("s", "s")

        def load_table(set_id):
            if ctx.em == "s":
                Sc.real.add_instruction(mybir.InstLoadActFuncSet(
                    name=nc.get_next_instruction_name(), act_func_set_id=set_id,
                    ins=[], outs=[]))

        load_table(ACT_SET_LNEXP)

        V.memset(cm_z.rearrange("p a b -> p (a b)"), 0.0)
        V.memset(cm_bd[:], 0.0)
        vfence()
        V.memset(cm_z[0:64, 0, :], -0.5)
        V.memset(cm_z[64:128, 1, :], -0.5)
        V.memset(cm_bd[0:64, 0:64], -1.0 / 64.0)
        V.memset(cm_bd[64:128, 64:128], -1.0 / 64.0)
        V.memset(onesm[:], 1.0)
        V.memset(om_z[64:128, 0, :], 0.0)
        V.memset(om_z[0:64, 1, :], 0.0)
        V.memset(ones_row[:], 1.0)
        V.memset(ones_pad[:], 0.0)
        V.memset(ones512[:], 0.0)
        V.memset(bout_row[:], 0.0)
        V.memset(ob_row[:], 0.0)
        V.memset(pbv_row[:], 0.0)
        for _k in (0, 1):
            V.memset(P0S[_k]["br"].rearrange("p a b -> p (a b)"), 0.0)
        vfence()
        V.memset(ones_pad[0:1, :], 1.0)
        V.memset(ones512[0:1, :], 1.0)
        V.memset(epsl[:], EPS_LN)
        V.memset(negln16[:], -LN16)
        V.memset(zpad[:], 0.0)
        mark("v")

        D_ = ctx.dma
        D_("q", "su", out=lm_f[0:64, :], in_=landm)
        D_("q", "su", out=lm_f[64:128, :], in_=landm)
        with nc.allow_non_contiguous_dma(reason="tiny transposed landmark load"):
            D_("q", "su", out=lm_T[0:64, :], in_=landm.transpose([1, 0]))
            D_("q", "su", out=lm_T[64:128, :], in_=landm.transpose([1, 0]))
        with nc.allow_non_contiguous_dma(reason="tiny bias loads"):
            D_("q", "su", out=cbb[:], in_=conv_b.rearrange("i (c p) -> p i c", p=128))
            D_("q", "su", out=pbb[:], in_=proj_b.rearrange("i (c p) -> p i c", p=128))
            D_("q", "su", out=rbbr[:], in_=rff_b.rearrange("i g (mm p) -> p i (g mm)", p=128))
        D_("q", "su", out=w1p[0:64, :], in_=f_w1)
        D_("q", "su", out=w1p[64:128, :], in_=f_w1)
        D_("q", "su", out=b1p[0:64, :], in_=f_b1)
        D_("q", "su", out=b1p[64:128, :], in_=f_b1)
        D_("q", "su", out=w2p[0:64, :], in_=f_w2)
        D_("q", "su", out=w2p[64:128, :], in_=f_w2)
        D_("q", "su", out=b2c[0:64, 0], in_=f_b2)
        D_("q", "su", out=b2c[64:128, 0], in_=f_b2)
        D_("q", "su", out=binc[0:64, 0], in_=f_bin)
        D_("q", "su", out=binc[64:128, 0], in_=f_bin)

        D_("g", "gsu", out=winb[:], in_=f_win.rearrange("(c p) m -> p c m", p=128))
        D_("g", "gsu", out=woutd[0:64, :, :], in_=f_wout.rearrange("m (c j) -> m c j", j=128))
        D_("g", "gsu", out=woutd[64:128, :, :], in_=f_wout.rearrange("m (c j) -> m c j", j=128))
        D_("g", "gsu", out=om_z[0:64, 0, :], in_=omega)
        D_("g", "gsu", out=om_z[64:128, 1, :], in_=omega)
        D_("g", "gsu", out=pmat_b[:], in_=pmat)
        D_("g", "gsu", out=ident_b[:], in_=ident_in)
        wait("g", "v")
        D_("g", "gsu", out=bout_row[0:1, :], in_=f_bout[None, :])
        D_("g", "gsu", out=ob_row[0:1, :], in_=out_b[None, :])
        for prj in range(3):
            for oc in range(CH):
                g4 = oc // 2
                c0 = g4 * 256 + (oc % 2) * 128
                D_("g", "wc0",
                   out=wbd[prj][:, oc:48 + oc:8, :].rearrange("p (w a) j -> p w a j", w=3),
                   in_=conv_k[prj, :, :, c0:c0 + 128].rearrange("w (a p) j -> p w a j", p=128))
            for rc in range(6):
                gam, mm = rc // 2, rc % 2
                D_("g", "wc0",
                   out=wbd[prj][:, 48 + gam * 16 + mm:48 + gam * 16 + 16 + mm:2, :],
                   in_=rff_w[prj, gam, :, mm * 128:(mm + 1) * 128].rearrange("(a p) j -> p a j", p=128))
            if prj < 2:
                for cc in range(CH):
                    D_("g", "wc0",
                       out=wbd[prj][:, 96 + cc:137 + cc:8, :],
                       in_=proj_w[prj, :, cc * 128:(cc + 1) * 128].rearrange("(f p) j -> p f j", p=128))
            else:
                for cg in range(2):
                    D_("g", "wc0",
                       out=wbd[prj][:, 96:144, :].rearrange("p (f c) j -> p f c j", f=6)[:, :, cg * 4:(cg + 1) * 4, :],
                       in_=proj_w[prj, :, cg * 512:(cg + 1) * 512].rearrange("(f p) (c j) -> p f c j", p=128, j=128))
        D_("q", "su", out=pbv_f[0:1, :], in_=proj_b[2:3, :])

        wait("v", "su")
        vfence()
        V.tensor_scalar_mul(pbv_row[0:1, :], pbv_f[0:1, :], 1.0 / SQ2M)
        V.scalar_tensor_tensor(out=b1p[:], in0=w1p[:], scalar=binc[:, 0:1], in1=b1p[:],
                               op0=ALU.mult, op1=ALU.add)
        V.tensor_scalar(out=rbb[:], in0=rbbr[:], scalar1=-C2PI, scalar2=0.25,
                        op0=ALU.mult, op1=ALU.add)
        V.tensor_mul(lsq_sb[:], lm_f[:], lm_f[:])
        vfence()
        V.reduce_sum(nybias[:, 0:1], lsq_sb[:], axis=mybir.AxisListType.X)
        vfence()
        V.tensor_scalar_mul(nybias[:, 0:1], nybias[:, 0:1], -1.0 / 64.0)
        mark("v")

        V.tensor_scalar_mul(l2d[:], lm_T[:], 2.0 / 64.0)
        V.memset(lm_bd[:], 0.0)
        vfence()
        V.tensor_copy(lm_bd[0:64, 0:64], l2d[0:64, :])
        V.tensor_copy(lm_bd[64:128, 64:128], l2d[64:128, :])
        mark("v")
        wait("t", "gsu")
        wait("t", "v")
        t_wait_bank(1)
        for c in range(CH):
            T.matmul(pb[1][0:64, 0:64], onesm[:, 0:64], winb[:, c, :],
                     start=(c == 0), stop=(c == CH - 1))
        mark("t")
        bank_done(1)
        use_bank(1, "s")
        wait("s", "t")
        Sc.copy(winsum[:], pb[1][0:1, 0:64])
        mark("s")
        bank_consumed(1, "s")

        wait("q", "v")
        with nc.allow_non_contiguous_dma(reason="edge pad columns"):
            D_("q", "ysp", out=ys[:, :, 0:1].transpose([1, 0, 2]),
               in_=zpad.rearrange("p (c e) -> p c e", e=1))
            D_("q", "ysp", out=ys[:, :, S + 1:S + 2].transpose([1, 0, 2]),
               in_=zpad.rearrange("p (c e) -> p c e", e=1))

        def ln_stats(xbf, xsq, arow_bf_dst, b0row_bf_dst, bank_m=0, bank_s=1):
            t_wait_bank(bank_m)
            t_wait_bank(bank_s)
            for c in range(CH):
                T.matmul(pb[bank_m][:, :], onesm[:, :], xbf[:, c, :],
                         start=(c == 0), stop=(c == CH - 1))
            for c in range(CH):
                T.matmul(pb[bank_s][:, :], onesm[:, :], xsq[:, c, :],
                         start=(c == 0), stop=(c == CH - 1))
            mark("t")
            bank_done(bank_m)
            bank_done(bank_s)
            use_bank(bank_m, "v")
            use_bank(bank_s, "v")
            wait("v", "t")
            V.tensor_scalar_mul(stat_m[:], pb[bank_m][0:1, :], 1.0 / float(D))
            V.tensor_scalar(out=stat_e2[:], in0=pb[bank_s][0:1, :], scalar1=1.0 / float(D),
                            scalar2=EPS_LN, op0=ALU.mult, op1=ALU.add)
            vfence()
            V.tensor_mul(stat_m2[:], stat_m[:], stat_m[:])
            vfence()
            V.tensor_sub(stat_var[:], stat_e2[:], stat_m2[:])
            mark("v")
            bank_consumed(bank_m, "v")
            bank_consumed(bank_s, "v")
            wait("s", "v")
            Sc.activation(stat_ln[:], stat_var[:], AF.Ln)
            sfence()
            Sc.activation(arow_f[:], stat_ln[:], AF.Exp, scale=-0.5)
            sfence()
            Sc.activation(arow_bf_dst[:], arow_f[:], AF.Copy)
            mark("s")
            wait("v", "s")
            V.scalar_tensor_tensor(out=b0row_bf_dst[:], in0=stat_m[:], scalar=-1.0,
                                   in1=arow_f[:], op0=ALU.mult, op1=ALU.mult)
            mark("v")

        def ffn(xbf, arow_bf, b0row_bf, xin_f32, xout_f32,
                sb_s, sb_h, sb_pp, tr1, tr2, tr3, sb_g, sb_apk,
                bank_s=2, bank_a=3, bank_o0=4, bank_o1=5):
            t_wait_bank(bank_s)
            t_wait_bank(bank_a)
            for half in range(2):
                cols = slice(half * 256, half * 256 + 256)
                po = slice(half * 64, half * 64 + 64)
                for c in range(CH):
                    T.matmul(pb[bank_s][po, 0:256], winb[:, c, :], xbf[:, c, cols],
                             start=(c == 0), stop=False)
                T.matmul(pb[bank_s][po, 0:256], winsum[:], b0row_bf[0:1, cols],
                         start=False, stop=True)
                T.matmul(pb[bank_a][po, 0:256], ones_row[0:1, 0:64], arow_bf[0:1, cols],
                         start=True, stop=True)
            mark("t")
            bank_done(bank_s)
            bank_done(bank_a)
            use_bank(bank_a, "s")
            wait("s", "t")
            Sc.copy(sb_apk, pb[bank_a][:, 0:256])
            mark("s")
            bank_consumed(bank_a, "s")
            use_bank(bank_s, "v")
            wait("v", "s")
            V.tensor_mul(sb_s, pb[bank_s][:, 0:256], sb_apk)
            mark("v")
            bank_consumed(bank_s, "v")
            wait("s", "v")
            for k in range(16):
                Sc.activation(sb_h[:, k, :], sb_s, AF.Relu,
                              bias=b1p[:, k:k + 1], scale=w1p[:, k:k + 1])
            mark("s")
            wait("v", "s")
            V.tensor_scalar(out=sb_pp[:, 0, :], in0=sb_h[:, 0, :], scalar1=w2p[:, 0:1],
                            scalar2=b2c[:, 0:1], op0=ALU.mult, op1=ALU.add)
            for k in range(1, 16):
                V.tensor_scalar_mul(sb_pp[:, k, :], sb_h[:, k, :], w2p[:, k:k + 1])
            vfence()
            for j in range(8):
                V.tensor_add(tr1[:, j, :], sb_pp[:, 2 * j, :], sb_pp[:, 2 * j + 1, :])
            vfence()
            for j in range(4):
                V.tensor_add(tr2[:, j, :], tr1[:, 2 * j, :], tr1[:, 2 * j + 1, :])
            vfence()
            for j in range(2):
                V.tensor_add(tr3[:, j, :], tr2[:, 2 * j, :], tr2[:, 2 * j + 1, :])
            vfence()
            V.tensor_add(sb_g, tr3[:, 0, :], tr3[:, 1, :])
            mark("v")
            wait("t", "v")
            for c in range(CH):
                bank = bank_o0 if (c % 2 == 0) else bank_o1
                t_wait_bank(bank)
                T.matmul(pb[bank][:, 0:256], woutd[0:64, c, :], sb_g[0:64, :],
                         start=True, stop=False)
                T.matmul(pb[bank][:, 0:256], bout_row[:, c * 128:(c + 1) * 128],
                         ones512[:, 0:256], start=False, stop=True)
                T.matmul(pb[bank][:, 256:512], woutd[64:128, c, :], sb_g[64:128, :],
                         start=True, stop=False)
                T.matmul(pb[bank][:, 256:512], bout_row[:, c * 128:(c + 1) * 128],
                         ones512[:, 0:256], start=False, stop=True)
                mark("t")
                bank_done(bank)
                use_bank(bank, "v")
                wait("v", "t")
                V.scalar_tensor_tensor(out=xout_f32[:, c, :], in0=pb[bank][:, :], scalar=0.5,
                                       in1=xin_f32[:, c, :], op0=ALU.mult, op1=ALU.add)
                mark("v")
                bank_consumed(bank, "v")

        def p0_stats_vs(banks, ra, rb):
            wait("v", "t")
            for k in (0, 1):
                bm, bs = banks[k]
                fr = P0S[k]["fr"]
                V.tensor_scalar_mul(fr[0:1, 0, :], pb[bm][0:1, :], 1.0 / float(D))
                V.tensor_scalar(out=fr[0:1, 1, :], in0=pb[bs][0:1, :],
                                scalar1=1.0 / float(D), scalar2=EPS_LN,
                                op0=ALU.mult, op1=ALU.add)
            vfence()
            for k in (0, 1):
                fr = P0S[k]["fr"]
                V.tensor_mul(fr[0:1, 2, :], fr[0:1, 0, :], fr[0:1, 0, :])
            vfence()
            for k in (0, 1):
                fr = P0S[k]["fr"]
                V.tensor_sub(fr[0:1, 3, :], fr[0:1, 1, :], fr[0:1, 2, :])
            mark("v")
            for k in (0, 1):
                bank_consumed(banks[k][0], "v")
                bank_consumed(banks[k][1], "v")
            wait("s", "v")
            for k in (0, 1):
                fr = P0S[k]["fr"]
                Sc.activation(fr[0:1, 4, :], fr[0:1, 3, :], AF.Ln)
            sfence()
            for k in (0, 1):
                fr = P0S[k]["fr"]
                Sc.activation(fr[0:1, 5, :], fr[0:1, 4, :], AF.Exp, scale=-0.5)
            sfence()
            for k in (0, 1):
                Sc.activation(P0S[k]["br"][0:1, ra, :], P0S[k]["fr"][0:1, 5, :], AF.Copy)
            mark("s")
            wait("v", "s")
            for k in (0, 1):
                V.scalar_tensor_tensor(out=P0S[k]["br"][0:1, rb, :],
                                       in0=P0S[k]["fr"][0:1, 0, :], scalar=-1.0,
                                       in1=P0S[k]["fr"][0:1, 5, :],
                                       op0=ALU.mult, op1=ALU.mult)
            mark("v")

        def p0_stats_t(banks):
            wait("t", "s")
            for k in (0, 1):
                bm, bs = banks[k]
                st_ = P0S[k]
                t_wait_bank(bm)
                t_wait_bank(bs)
                for c in range(CH):
                    T.matmul(pb[bm][:, :], onesm[:, :], st_["xbf"][:, c, :],
                             start=(c == 0), stop=(c == CH - 1))
                for c in range(CH):
                    T.matmul(pb[bs][:, :], onesm[:, :], st_["xsq"][:, c, :],
                             start=(c == 0), stop=(c == CH - 1))
                mark("t")
                bank_done(bm)
                bank_done(bs)
                use_bank(bm, "v")
                use_bank(bs, "v")

        def p0_cast(k, xt_):
            st_ = P0S[k]
            Sc.activation(st_["xbf"].rearrange("p c t -> p (c t)"),
                          xt_.rearrange("p c t -> p (c t)"), AF.Copy)
            Sc.activation(st_["xsq"].rearrange("p c t -> p (c t)"),
                          xt_.rearrange("p c t -> p (c t)"), AF.Square)

        NP = NT // 2
        for pr in range(NP):
            tA, tB = (2 * pr) * TILE, (2 * pr + 1) * TILE
            tt_of = {0: tA, 1: tB}
            slot_of = {0: (2 * pr) % 3, 1: (2 * pr + 1) % 3}
            if pr == 0:
                for k in (0, 1):
                    D_("q", "xl", out=P0_xts[k],
                       in_=x_in[:, :, tt_of[k]:tt_of[k] + TILE].transpose([1, 0, 2]))
            wait("s", "xl")
            wait("s", "t")
            wait("s", "ysp")
            for k in (0, 1):
                p0_cast(k, P0_xts[slot_of[k]])
            mark("s")
            p0_stats_t({0: (0, 1), 1: (2, 3)})
            p0_stats_vs({0: (0, 1), 1: (2, 3)}, 0, 1)
            if 2 * pr + 2 < NT:
                wait("q", "v")
                wait("q", "x1sp")
                tn = (2 * pr + 2) * TILE
                D_("q", "xl", out=P0_xts[(2 * pr + 2) % 3],
                   in_=x_in[:, :, tn:tn + TILE].transpose([1, 0, 2]))
            fbank = {0: (4, 5, 6), 1: (0, 1, 2)}
            for k in (0, 1):
                s0_, s1_, ba_ = fbank[k]
                st_ = P0S[k]
                t_wait_bank(s0_)
                t_wait_bank(s1_)
                t_wait_bank(ba_)
                for half, sb_ in ((0, s0_), (1, s1_)):
                    cols = slice(half * 256, half * 256 + 256)
                    po = slice(half * 64, half * 64 + 64)
                    for c in range(CH):
                        T.matmul(pb[sb_][po, 0:256], winb[:, c, :], st_["xbf"][:, c, cols],
                                 start=(c == 0), stop=False)
                mark("t")
            wait("t", "v")
            wait("t", "s")
            for k in (0, 1):
                s0_, s1_, ba_ = fbank[k]
                st_ = P0S[k]
                for half, sb_ in ((0, s0_), (1, s1_)):
                    cols = slice(half * 256, half * 256 + 256)
                    po = slice(half * 64, half * 64 + 64)
                    T.matmul(pb[sb_][po, 0:256], winsum[:], st_["br"][0:1, 1, cols],
                             start=False, stop=True)
                    T.matmul(pb[ba_][po, 0:256], ones_row[0:1, 0:64], st_["br"][0:1, 0, cols],
                             start=True, stop=True)
                mark("t")
                bank_done(s0_)
                bank_done(s1_)
                bank_done(ba_)
            for k in (0, 1):
                s0_, s1_, ba_ = fbank[k]
                st_ = P0S[k]
                use_bank(ba_, "s")
                wait("s", "t")
                Sc.copy(st_["apk"], pb[ba_][:, 0:256])
                mark("s")
                bank_consumed(ba_, "s")
                use_bank(s0_, "v")
                use_bank(s1_, "v")
                wait("v", "s")
                V.tensor_mul(st_["s"][0:64, :], pb[s0_][0:64, 0:256], st_["apk"][0:64, :])
                V.tensor_mul(st_["s"][64:128, :], pb[s1_][64:128, 0:256], st_["apk"][64:128, :])
                mark("v")
                bank_consumed(s0_, "v")
                bank_consumed(s1_, "v")
                wait("s", "v")
                for kk in range(16):
                    Sc.activation(st_["h"][:, kk, :], st_["s"], AF.Relu,
                                  bias=b1p[:, kk:kk + 1], scale=w1p[:, kk:kk + 1])
                mark("s")
            obank = {0: (3, 7), 1: (4, 5)}
            for k in (0, 1):
                st_ = P0S[k]
                wait("v", "s")
                V.tensor_scalar(out=P0_pp2[:, 0, :], in0=st_["h"][:, 0, :],
                                scalar1=w2p[:, 0:1], scalar2=b2c[:, 0:1],
                                op0=ALU.mult, op1=ALU.add)
                for kk in range(1, 16):
                    V.tensor_scalar_mul(P0_pp2[:, kk, :], st_["h"][:, kk, :], w2p[:, kk:kk + 1])
                vfence()
                for j in range(8):
                    V.tensor_add(P0_tr1b[:, j, :], P0_pp2[:, 2 * j, :], P0_pp2[:, 2 * j + 1, :])
                vfence()
                for j in range(4):
                    V.tensor_add(P0_tr2b[:, j, :], P0_tr1b[:, 2 * j, :], P0_tr1b[:, 2 * j + 1, :])
                vfence()
                for j in range(2):
                    V.tensor_add(P0_tr3b[:, j, :], P0_tr2b[:, 2 * j, :], P0_tr2b[:, 2 * j + 1, :])
                vfence()
                V.tensor_add(P0_g2, P0_tr3b[:, 0, :], P0_tr3b[:, 1, :])
                mark("v")
                wait("t", "v")
                b0_, b1_ = obank[k]
                for c in range(CH):
                    bank = b0_ if c % 2 == 0 else b1_
                    t_wait_bank(bank)
                    T.matmul(pb[bank][:, 0:256], woutd[0:64, c, :], P0_g2[0:64, :],
                             start=True, stop=False)
                    T.matmul(pb[bank][:, 0:256], bout_row[:, c * 128:(c + 1) * 128],
                             ones512[:, 0:256], start=False, stop=True)
                    T.matmul(pb[bank][:, 256:512], woutd[64:128, c, :], P0_g2[64:128, :],
                             start=True, stop=False)
                    T.matmul(pb[bank][:, 256:512], bout_row[:, c * 128:(c + 1) * 128],
                             ones512[:, 0:256], start=False, stop=True)
                    mark("t")
                    bank_done(bank)
                    use_bank(bank, "v")
                    wait("v", "t")
                    V.scalar_tensor_tensor(out=P0_xts[slot_of[k]][:, c, :], in0=pb[bank][:, :],
                                           scalar=0.5, in1=P0_xts[slot_of[k]][:, c, :],
                                           op0=ALU.mult, op1=ALU.add)
                    mark("v")
                    bank_consumed(bank, "v")
                wait("q", "v")
                D_("q", "x1sp", out=x1s[:, :, tt_of[k]:tt_of[k] + TILE].transpose([1, 0, 2]),
                   in_=P0_xts[slot_of[k]])
                wait("s", "v")
                p0_cast(k, P0_xts[slot_of[k]])
                mark("s")
            p0_stats_t({0: (2, 3), 1: (0, 1)})
            p0_stats_vs({0: (2, 3), 1: (0, 1)}, 2, 3)
            ybank = {0: (6, 7), 1: (2, 3)}
            wait("t", "v")
            wait("t", "s")
            for k in (0, 1):
                ba_, bb_ = ybank[k]
                st_ = P0S[k]
                t_wait_bank(ba_)
                t_wait_bank(bb_)
                T.matmul(pb[ba_][:, :], ones_pad[:, :], st_["br"][:, 2, :], start=True, stop=True)
                T.matmul(pb[bb_][:, :], ones_pad[:, :], st_["br"][:, 3, :], start=True, stop=True)
                mark("t")
                bank_done(ba_)
                bank_done(bb_)
                use_bank(ba_, "s")
                use_bank(bb_, "s")
            wait("s", "t")
            wait("s", "v")
            wait("s", "p")
            for k in (0, 1):
                ba_, bb_ = ybank[k]
                st_ = P0S[k]
                Sc.copy(st_["abc"], pb[ba_][:, :])
                Sc.copy(st_["bbc"], pb[bb_][:, :])
                mark("s")
                bank_consumed(ba_, "s")
                bank_consumed(bb_, "s")
            for k in (0, 1):
                st_ = P0S[k]
                wait("v", "s")
                for c in range(CH):
                    V.tensor_mul(st_["y"][:, c, :], P0_xts[slot_of[k]][:, c, :], st_["abc"])
                mark("v")
                wait("p", "v")
                for c in range(CH):
                    P.tensor_add(st_["y"][:, c, :], st_["y"][:, c, :], st_["bbc"])
                mark("p")
                wait("q", "p")
                D_("q", "ysp", out=ys[:, :, tt_of[k] + 1:tt_of[k] + 1 + TILE].transpose([1, 0, 2]),
                   in_=st_["y"])
                if k == 0 and 2 * pr + 3 < NT:
                    tn = (2 * pr + 3) * TILE
                    wait("q", "v")
                    wait("q", "x1sp")
                    D_("q", "xl", out=P0_xts[(2 * pr + 3) % 3],
                       in_=x_in[:, :, tn:tn + TILE].transpose([1, 0, 2]))

        if upto < 1:
            wait("q", "x1sp")
            for i in range(NT):
                t0 = i * TILE
                D_("q", "ol", out=out[:, :, t0:t0 + TILE], in_=x1s[:, :, t0:t0 + TILE])
            if ctx.em == "q":
                Q.real.wait_ge(sems["ol"], ctx.cnt["ol"])
            return

        load_table(ACT_SET_TRIG)
        wait("v", "x1sp")
        wait("v", "ysp")
        V.memset(A1_vtm[:, :, :, 64:65], 1.0)
        mark("v")

        wb_free = {"conv": 0, "rff": 0, "proj": 0}
        wl_ready = {}

        wait("g", "wc0")

        def a1_wl_convrff(prj):
            wait("g", "t", wb_free["conv"])
            D_("g", "wlc0", out=A1_wb[:, 0:48, :], in_=wbd[prj][:, 0:48, :])
            for oc in range(CH):
                wl_ready[("conv", prj, oc)] = ctx.snap("wlc0")
            wait("g", "t", wb_free["rff"])
            D_("g", "wlr0", out=A1_wb[:, 48:96, :], in_=wbd[prj][:, 48:96, :])
            for rc in range(6):
                wl_ready[("rff", prj, rc)] = ctx.snap("wlr0")
        def a1_wl_proj(prj):
            wait("g", "t", wb_free["proj"])
            D_("g", "wlp0", out=A1_wb[:, 96:144, :], in_=wbd[prj][:, 96:144, :])
            for cc in range(CH):
                wl_ready[("proj", prj, cc)] = ctx.snap("wlp0")

        def a1_tile_loads(i):
            t0 = i * TILE
            wait("q", "t")
            wait("q", "ysp")
            wait("q", "x1sp")
            wait("q", "qksp")
            D_("q", "yl", out=A1_y, in_=ys[:, :, t0:t0 + TILE + 2].transpose([1, 0, 2]))
            wait("g", "v")
            D_("g", "rt", out=A1_ct2[i % 2], in_=rope_c[:, t0:t0 + TILE])
            D_("g", "rt", out=A1_st2[i % 2], in_=rope_s[:, t0:t0 + TILE])

        a1_tile_loads(0)
        a1_wl_convrff(0)
        a1_wl_proj(0)
        h_snap = {}
        sin_snap = {}

        def a1_conv(i, prj):
            A1_h = A1_h2[prj % 2]
            for oc in range(CH):
                bank = oc % 2
                grp = oc // 2
                t_wait_bank(bank)
                wait("t", "wlc0", wl_ready[("conv", prj, oc)])
                wait("t", "yl", yl_gate[i])
                first = True
                for w in range(3):
                    for a in range(2):
                        widx = (w * 2 + a) * 8 + oc
                        T.matmul(pb[bank][:, :], A1_wb[:, widx, :],
                                 A1_y[:, grp * 2 + a, w:w + TILE],
                                 start=first, stop=(w == 2 and a == 1))
                        first = False
                mark("t")
                bank_done(bank)
                use_bank(bank, "s")
                wait("s", "t")
                wait("s", "v")
                Sc.activation(A1_h[:, oc, :], pb[bank][:, :], AF.Identity,
                              bias=cbb[:, prj, oc:oc + 1])
                mark("s")
                bank_consumed(bank, "s")
            wb_free["conv"] = ctx.snap("t")
            h_snap[prj] = ctx.snap("s")

        def a1_rff(prj):
            A1_h = A1_h2[prj % 2]
            for rc in range(6):
                bank = 2 + rc % 2
                gam = rc // 2
                t_wait_bank(bank)
                wait("t", "wlr0", wl_ready[("rff", prj, rc)])
                wait("t", "s", h_snap[prj])
                for a in range(CH):
                    widx = 48 + gam * 16 + a * 2 + (rc % 2)
                    T.matmul(pb[bank][:, :], A1_wb[:, widx, :],
                             A1_h[:, a, :], start=(a == 0), stop=(a == CH - 1))
                mark("t")
                bank_done(bank)
                use_bank(bank, "v")
                wait("v", "t")
                if rc in (0, 3):
                    wait("v", "s")
                V.tensor_scalar(out=A1_u6[:, rc % 3, :], in0=pb[bank][:, :],
                                scalar1=-GSC[gam] * C2PI,
                                scalar2=rbb[:, prj, rc:rc + 1],
                                op0=ALU.mult, op1=ALU.add)
                mark("v")
                bank_consumed(bank, "v")
                if rc == 2 or rc == 5:
                    a1_vchain_half(prj, rc // 3)
            wb_free["rff"] = ctx.snap("t")

        def a1_vchain_half(prj, half):
            A1_z = A1_z2[prj % 2]
            u6f = A1_u6.rearrange("p a b -> p (a b)")
            n6f = A1_n6f.rearrange("p a b -> p (a b)")
            mg6 = A1_mg6.rearrange("p a b -> p (a b)")
            ml6 = A1_ml6.rearrange("p a b -> p (a b)")
            vfence()
            V.tensor_copy(A1_n6.rearrange("p a b -> p (a b)"), u6f)
            vfence()
            V.tensor_copy(n6f, A1_n6.rearrange("p a b -> p (a b)"))
            vfence()
            V.tensor_sub(u6f, u6f, n6f)
            vfence()
            V.tensor_single_scalar(out=mg6, in_=u6f, scalar=0.5, op=ALU.is_gt)
            V.tensor_single_scalar(out=ml6, in_=u6f, scalar=-0.5, op=ALU.is_lt)
            vfence()
            V.tensor_sub(u6f, u6f, mg6)
            vfence()
            V.tensor_add(u6f, u6f, ml6)
            mark("v")
            wait("s", "v")
            Sc.activation(A1_z[:, half * 3:half * 3 + 3, :].rearrange("p a b -> p (a b)"),
                          u6f, AF.Sin, scale=float(2 * math.pi))
            mark("s")
            sin_snap[prj] = ctx.snap("s")

        def a1_proj(i, prj):
            A1_z = A1_z2[prj % 2]
            if prj < 2:
                pre = A1_preq if prj == 0 else A1_prek
                for cc in range(CH):
                    bank = 4 + cc % 2
                    t_wait_bank(bank)
                    wait("t", "wlp0", wl_ready[("proj", prj, cc)])
                    wait("t", "s", sin_snap[prj])
                    for f in range(6):
                        T.matmul(pb[bank][:, :], A1_wb[:, 96 + f * 8 + cc, :],
                                 A1_z[:, f, :], start=(f == 0), stop=(f == 5))
                    mark("t")
                    bank_done(bank)
                    use_bank(bank, "s")
                    wait("s", "t")
                    Sc.activation(pre[:, cc, :], pb[bank][:, :], AF.Identity,
                                  scale=SQ2M, bias=pbb[:, prj, cc:cc + 1])
                    mark("s")
                    bank_consumed(bank, "s")
            else:
                for sub in range(4):
                    for cg in range(2):
                        bank = 4 + (sub * 2 + cg) % 2
                        t_wait_bank(bank)
                        wait("t", "wlp0", wl_ready[("proj", prj, cg)])
                        wait("t", "s", sin_snap[prj])
                        for f in range(6):
                            i0 = 96 + f * 8 + cg * 4
                            T.matmul(pb[bank][:, :],
                                     A1_z[:, f, sub * 128:(sub + 1) * 128],
                                     A1_wb[:, i0:i0 + 4, :].rearrange("p n j -> p (n j)"),
                                     start=(f == 0), stop=False)
                        T.matmul(pb[bank][:, :], ones_pad[:, :],
                                 pbv_row[:, cg * 512:(cg + 1) * 512],
                                 start=False, stop=True)
                        mark("t")
                        bank_done(bank)
                        use_bank(bank, "s")
                        wait("s", "t")
                        Sc.activation(A1_vtm[:, sub, cg * 8:(cg + 1) * 8, 0:64],
                                      pb[bank][:, :].rearrange("p (h d) -> p h d", d=64),
                                      AF.Identity, scale=SQ2M)
                        mark("s")
                        bank_consumed(bank, "s")
            wb_free["proj"] = ctx.snap("t")

        yl_gate = {}
        yl_gate[0] = ctx.snap("yl")
        for i in range(NT):
            t0 = i * TILE
            if i == 0:
                a1_conv(i, 0)
            for prj in range(3):
                a1_rff(prj)
                if prj < 2 or i + 1 < NT:
                    a1_wl_convrff(prj + 1 if prj < 2 else 0)
                if prj < 2:
                    a1_conv(i, prj + 1)
                    if prj == 1 and i + 1 < NT:
                        a1_tile_loads(i + 1)
                        yl_gate[i + 1] = ctx.snap("yl")
                a1_proj(i, prj)
                if prj < 2 or i + 1 < NT:
                    a1_wl_proj(prj + 1 if prj < 2 else 0)

            for which in range(2):
                pre = A1_preq if which == 0 else A1_prek
                dst = A1_qsb if which == 0 else A1_ksb
                rtmp = A1_h2[1]
                wait("s", "v")
                for cc in range(CH):
                    bank = 6 + cc % 2
                    t_wait_bank(bank)
                    wait("t", "s")
                    T.matmul(pb[bank][:, :], pmat_b[:], pre[:, cc, :], start=True, stop=True)
                    mark("t")
                    bank_done(bank)
                    use_bank(bank, "s")
                    wait("s", "t")
                    Sc.copy(A1_rot[:, cc, :], pb[bank][:, :])
                    mark("s")
                    bank_consumed(bank, "s")
                wait("v", "s")
                wait("v", "rt")
                wait("v", "qksp")
                for cc in range(CH):
                    V.tensor_mul(rtmp[:, cc, :], pre[:, cc, :], A1_ct2[i % 2])
                for cc in range(CH):
                    V.tensor_mul(A1_rot[:, cc, :], A1_rot[:, cc, :], A1_st2[i % 2])
                vfence()
                for cc in range(CH):
                    V.tensor_add(dst[:, cc, :], rtmp[:, cc, :], A1_rot[:, cc, :])
                mark("v")
                if which == 0:
                    wait("q", "v")
                    D_("q", "qksp", out=qs[:, :, t0:t0 + TILE].transpose([1, 0, 2]),
                       in_=dst)
            load_table(ACT_SET_LNEXP)
            vfence()
            V.tensor_mul(A2_ksq.rearrange("p c t -> p (c t)"),
                         A1_ksb.rearrange("p c t -> p (c t)"),
                         A1_ksb.rearrange("p c t -> p (c t)"))
            mark("v")
            wait("t", "v")
            for cc in range(CH):
                bank = cc % 2
                t_wait_bank(bank)
                T.matmul(pb[bank][:, :], lm_bd[:, :],
                         A1_ksb[:, cc, :], start=True, stop=False)
                T.matmul(pb[bank][:, :], cm_bd[:, :],
                         A2_ksq[:, cc, :], start=False, stop=True)
                mark("t")
                bank_done(bank)
                use_bank(bank, "s")
                wait("s", "t")
                wait("s", "v")
                Sc.activation(A2_khat[:, cc, :], pb[bank][:, :],
                              AF.Exp, bias=nybias[:, 0:1])
                mark("s")
                bank_consumed(bank, "s")
            wait("v", "s")
            V.tensor_mul(A2_khsq.rearrange("p c t -> p (c t)"),
                         A2_khat.rearrange("p c t -> p (c t)"),
                         A2_khat.rearrange("p c t -> p (c t)"))
            mark("v")
            if debug_dump and i == NT - 1:
                wait("q", "s")
                D_("q", "ol", out=khdbg, in_=A2_khat[:])
            if i + 1 < NT:
                a1_conv(i + 1, 0)
            wait("t", "v")
            kp_d = [A2_kp, A2_kp2]
            s_snap = {}

            def kv_emit(hh):
                kbank = 4 + hh % 2
                t_wait_bank(kbank)
                wait("t", "s", s_snap[hh])
                for sub in range(4):
                    T.matmul(pb[kbank][0:65, 0:256], A1_vtm[:, sub, hh, :],
                             kp_d[hh % 2][:, sub, :], start=(sub == 0), stop=(sub == 3))
                mark("t")
                bank_done(kbank)
                use_bank(kbank, "v")
                wait("v", "t")
                if i == 0:
                    V.tensor_copy(kvacc[:, hh, :], pb[kbank][0:65, 0:256])
                else:
                    V.tensor_add(kvacc[:, hh, :], kvacc[:, hh, :], pb[kbank][0:65, 0:256])
                mark("v")
                bank_consumed(kbank, "v")

            for h in range(NH):
                base, cc = (h % 2) * 64, h // 2
                for sub in range(4):
                    bank = 2 + sub % 2
                    t_wait_bank(bank)
                    T.matmul(pb[bank][:, 0:256],
                             A2_khat[:, cc, sub * 128:(sub + 1) * 128],
                             om_z[:, h % 2, :], start=True, stop=False)
                    T.matmul(pb[bank][:, 0:256],
                             A2_khsq[:, cc, sub * 128:(sub + 1) * 128],
                             cm_z[:, h % 2, :], start=False, stop=True)
                    mark("t")
                    bank_done(bank)
                    use_bank(bank, "s")
                    wait("s", "t")
                    Sc.activation(kp_d[h % 2][:, sub, :], pb[bank][:, 0:256], AF.Exp,
                                  bias=negln16[:, 0:1])
                    mark("s")
                    bank_consumed(bank, "s")
                s_snap[h] = ctx.snap("s")
                if h > 0:
                    kv_emit(h - 1)
            kv_emit(NH - 1)
            if i + 1 < NT:
                load_table(ACT_SET_TRIG)

        if upto < 2:
            wait("q", "x1sp")
            wait("q", "qksp")
            for i in range(NT):
                t0 = i * TILE
                D_("q", "ol", out=out[:, :, t0:t0 + TILE], in_=x1s[:, :, t0:t0 + TILE])
            if ctx.em == "q":
                Q.real.wait_ge(sems["ol"], ctx.cnt["ol"])
            return

        fin_t, fin_v, fin_s = ctx.snap("t"), ctx.snap("v"), ctx.snap("s")
        for h in range(NH):
            wait("v", "t")
            wait("v", "s")
            if h == 0:
                vfence()
            V.tensor_copy(kvbfT[:], kvacc[:, h, :])
            mark("v")
            wait("t", "v")
            t_wait_bank(6)
            T.transpose(pbh[6][0:128, 0:65], kvbfT[0:65, 0:128], ident_b[0:65, 0:65])
            T.transpose(pbh[6][0:128, 66:131], kvbfT[0:65, 128:256], ident_b[0:65, 0:65])
            mark("t")
            bank_done(6)
            use_bank(6, "s")
            wait("s", "t")
            Sc.copy(kvp[:, 0, h, :], pbh[6][0:128, 0:65])
            Sc.copy(kvp[:, 1, h, :], pbh[6][0:128, 66:131])
            mark("s")
            bank_consumed(6, "s")

        if upto < 3:
            wait("q", "x1sp")
            for i in range(NT):
                t0 = i * TILE
                D_("q", "ol", out=out[:, :, t0:t0 + TILE], in_=x1s[:, :, t0:t0 + TILE])
            if ctx.em == "q":
                Q.real.wait_ge(sems["ol"], ctx.cnt["ol"])
            return


        if debug_dump:
            wait("q", "s")
            D_("q", "ol", out=kvdbg, in_=kvp[:])
        load_table(ACT_SET_LNEXP)
        wait("g", "t")
        wait("g", "v")
        wait("g", "s")
        for _ws in range(8):
            wait("g", f"wlc{_ws}")
            wait("g", f"wlp{_ws}")
        for _ws in range(6):
            wait("g", f"wlr{_ws}")
        D_("g", "gsu", out=owb[:],
           in_=out_w.rearrange("(dc p) (cc j) -> p dc cc j", p=128, j=128))
        Bx1d = [B_x1, B_x1b]
        Bqsd = [B_qsb, B_qsbb]
        Bx2d = [B_x2, B_x2b]
        Botm = [B_otm, B_otmb]

        def b_loads(i):
            t0 = i * TILE
            if i == 0:
                wait("q", "t", fin_t)
                wait("q", "v", fin_v)
                wait("q", "s", fin_s)
            wait("q", "x1sp")
            wait("q", "qksp")
            wait("q", "ol")
            D_("q", f"x1l{i % 2}", out=Bx1d[i % 2],
               in_=x1s[:, :, t0:t0 + TILE].transpose([1, 0, 2]))
            D_("q", f"ql{i % 2}", out=Bqsd[i % 2],
               in_=qs[:, :, t0:t0 + TILE].transpose([1, 0, 2]))

        def t_cast(j):
            x2b = Bx2d[j % 2]
            wait("p", "v")
            wait("p", "t")
            P.tensor_copy(B_xbf.rearrange("p c t -> p (c t)"),
                          x2b.rearrange("p c t -> p (c t)"))
            P.tensor_mul(B_xsq.rearrange("p c t -> p (c t)"),
                         x2b.rearrange("p c t -> p (c t)"),
                         x2b.rearrange("p c t -> p (c t)"))
            mark("p")

        def t_stats_T(j):
            wait("t", "p")
            t_wait_bank(6)
            t_wait_bank(7)
            for c in range(CH):
                T.matmul(pb[6][:, :], onesm[:, :], B_xbf[:, c, :],
                         start=(c == 0), stop=(c == CH - 1))
            for c in range(CH):
                T.matmul(pb[7][:, :], onesm[:, :], B_xsq[:, c, :],
                         start=(c == 0), stop=(c == CH - 1))
            mark("t")
            bank_done(6)
            bank_done(7)
            use_bank(6, "v")
            use_bank(7, "v")

        def t_stats_VS(j):
            use_bank(6, "s")
            use_bank(7, "s")
            wait("s", "t")
            Sc.activation(stat_m[:], pb[6][0:1, :], AF.Copy, scale=1.0 / float(D))
            Sc.activation(stat_e2[:], pb[7][0:1, :], AF.Copy, scale=1.0 / float(D),
                          bias=float(EPS_LN))
            sfence()
            Sc.activation(stat_m2[:], stat_m[:], AF.Square)
            mark("s")
            bank_consumed(6, "s")
            bank_consumed(7, "s")
            wait("v", "s")
            V.tensor_sub(stat_var[:], stat_e2[:], stat_m2[:])
            mark("v")
            wait("s", "v")
            Sc.activation(stat_ln[:], stat_var[:], AF.Ln)
            sfence()
            Sc.activation(arow_f[:], stat_ln[:], AF.Exp, scale=-0.5)
            sfence()
            Sc.activation(arow_b[:], arow_f[:], AF.Copy)
            mark("s")
            wait("v", "s")
            V.scalar_tensor_tensor(out=b0row_b[:], in0=stat_m[:], scalar=-1.0,
                                   in1=arow_f[:], op0=ALU.mult, op1=ALU.mult)
            mark("v")

        def t_ffn_T(j):
            t_wait_bank(6)
            t_wait_bank(7)
            wait("t", "v")
            wait("t", "s")
            for half in range(2):
                cols = slice(half * 256, half * 256 + 256)
                po = slice(half * 64, half * 64 + 64)
                for c in range(CH):
                    T.matmul(pb[6][po, 0:256], winb[:, c, :], B_xbf[:, c, cols],
                             start=(c == 0), stop=False)
                T.matmul(pb[6][po, 0:256], winsum[:], b0row_b[0:1, cols],
                         start=False, stop=True)
                T.matmul(pb[7][po, 0:256], ones_row[0:1, 0:64], arow_b[0:1, cols],
                         start=True, stop=True)
            mark("t")
            bank_done(6)
            bank_done(7)

        def t_ffn_SV(j):
            use_bank(7, "s")
            wait("s", "t")
            Sc.copy(B_apk, pb[7][:, 0:256])
            mark("s")
            bank_consumed(7, "s")
            use_bank(6, "v")
            wait("v", "s")
            V.tensor_mul(B_s, pb[6][:, 0:256], B_apk)
            mark("v")
            bank_consumed(6, "v")
            wait("s", "v")
            for kk in range(16):
                Sc.activation(B_h[:, kk, :], B_s, AF.Relu,
                              bias=b1p[:, kk:kk + 1], scale=w1p[:, kk:kk + 1])
            mark("s")

        def t_tree(j):
            wait("v", "s")
            V.tensor_scalar(out=B_pp[:, 0, :], in0=B_h[:, 0, :], scalar1=w2p[:, 0:1],
                            scalar2=b2c[:, 0:1], op0=ALU.mult, op1=ALU.add)
            for kk in range(1, 16):
                V.tensor_scalar_mul(B_pp[:, kk, :], B_h[:, kk, :], w2p[:, kk:kk + 1])
            vfence()
            for jj in range(8):
                V.tensor_add(B_tr1[:, jj, :], B_pp[:, 2 * jj, :], B_pp[:, 2 * jj + 1, :])
            vfence()
            for jj in range(4):
                V.tensor_add(B_tr2[:, jj, :], B_tr1[:, 2 * jj, :], B_tr1[:, 2 * jj + 1, :])
            vfence()
            for jj in range(2):
                V.tensor_add(B_tr3[:, jj, :], B_tr2[:, 2 * jj, :], B_tr2[:, 2 * jj + 1, :])
            vfence()
            V.tensor_add(B_g, B_tr3[:, 0, :], B_tr3[:, 1, :])
            mark("v")

        def t_out(j):
            x1b, x2b = Bx1d[j % 2], Bx2d[j % 2]
            wait("t", "v")
            for c in range(CH):
                bank = 6 + c % 2
                t_wait_bank(bank)
                T.matmul(pb[bank][:, 0:256], woutd[0:64, c, :], B_g[0:64, :],
                         start=True, stop=False)
                T.matmul(pb[bank][:, 0:256], bout_row[:, c * 128:(c + 1) * 128],
                         ones512[:, 0:256], start=False, stop=True)
                T.matmul(pb[bank][:, 256:512], woutd[64:128, c, :], B_g[64:128, :],
                         start=True, stop=False)
                T.matmul(pb[bank][:, 256:512], bout_row[:, c * 128:(c + 1) * 128],
                         ones512[:, 0:256], start=False, stop=True)
                mark("t")
                bank_done(bank)
                use_bank(bank, "v")
                wait("v", "t")
                V.scalar_tensor_tensor(out=x1b[:, c, :], in0=pb[bank][:, :],
                                       scalar=0.5, in1=x2b[:, c, :],
                                       op0=ALU.mult, op1=ALU.add)
                mark("v")
                bank_consumed(bank, "v")
            wait("q", "v")
            t0 = j * TILE
            D_("q", "ol", out=out[:, :, t0:t0 + TILE].transpose([1, 0, 2]), in_=x1b)

        def b_transpose_pair(p):
            obank = 2 + p % 2
            cc = p
            t_wait_bank(obank)
            wait("t", "v")
            for sub in range(4):
                T.transpose(pbh[obank][0:128, sub * 128:(sub + 1) * 128],
                            Botm[p % 2][:, sub, :], ident_b[:, :])
            mark("t")
            bank_done(obank)
            use_bank(obank, "s")
            wait("s", "t")
            Sc.copy(B_ofm[:, cc, :], pbh[obank][:, 0:512])
            mark("s")
            bank_consumed(obank, "s")

        def b_attn(i):
            x1b, qsb, x2b = Bx1d[i % 2], Bqsd[i % 2], Bx2d[i % 2]
            tj = i - 1
            if tj >= 0:
                t_cast(tj)
            wait("v", f"ql{i % 2}")
            wait("v", "t", fin_t if i == 0 else None)
            V.tensor_mul(B_qsq.rearrange("p c t -> p (c t)"),
                         qsb.rearrange("p c t -> p (c t)"),
                         qsb.rearrange("p c t -> p (c t)"))
            mark("v")
            wait("t", "v")
            qp_d = [B_qp, B_qp2]
            qsnap = {}

            def b_post(hh):
                nbank = 4 + hh % 2
                t_wait_bank(nbank)
                if i == 0 and hh == 0:
                    wait("t", "s")
                wait("t", "s", qsnap[hh])
                for sub in range(4):
                    for ec in range(2):
                        T.matmul(pb[nbank][:, sub * 65:sub * 65 + 65],
                                 qp_d[hh % 2][:, ec, sub * 128:(sub + 1) * 128],
                                 kvp[:, ec, hh, :], start=(ec == 0), stop=(ec == 1))
                mark("t")
                bank_done(nbank)
                if hh % 2 == 1:
                    if hh >= 3:
                        b_transpose_pair((hh - 1) // 2 - 1)
                    use_bank(4, "v")
                    use_bank(5, "v")
                    wait("v", "t")
                    for hx in (hh - 1, hh):
                        nb = 4 + hx % 2
                        for sub in range(4):
                            cc8 = (hx % 2) * 4 + sub
                            V.tensor_single_scalar(out=dcol[:, cc8:cc8 + 1],
                                                   in_=pb[nb][:, sub * 65 + 64:sub * 65 + 65],
                                                   scalar=EPS_ATT, op=ALU.add)
                    vfence()
                    V.reciprocal(rcol[:, 0:8], dcol[:, 0:8])
                    vfence()
                    for hx in (hh - 1, hh):
                        nb = 4 + hx % 2
                        bh = (hx % 2) * 64
                        for sub in range(4):
                            cc8 = (hx % 2) * 4 + sub
                            V.tensor_scalar_mul(Botm[(hx // 2) % 2][:, sub, bh:bh + 64],
                                                pb[nb][:, sub * 65:sub * 65 + 64],
                                                rcol[:, cc8:cc8 + 1])
                    mark("v")
                    bank_consumed(4, "v")
                    bank_consumed(5, "v")

            for h in range(NH):
                base, cc = (h % 2) * 64, h // 2
                for ec in range(2):
                    bank = (0 if h % 2 == 0 else 2) + ec
                    t_wait_bank(bank)
                    T.matmul(pb[bank][:, :], om_z[:, h % 2, ec * 128:(ec + 1) * 128],
                             qsb[:, cc, :], start=True, stop=False)
                    T.matmul(pb[bank][:, :], cm_z[:, h % 2, 0:128],
                             B_qsq[:, cc, :], start=False, stop=True)
                    mark("t")
                    bank_done(bank)
                    use_bank(bank, "s")
                    wait("s", "t")
                    Sc.activation(qp_d[h % 2][:, ec, :], pb[bank][:, :], AF.Exp,
                                  bias=negln16[:, 0:1])
                    mark("s")
                    bank_consumed(bank, "s")
                qsnap[h] = ctx.snap("s")
                if h > 0:
                    b_post(h - 1)
                if tj >= 0:
                    if h == 4:
                        t_stats_T(tj)
                    elif h == 6:
                        t_stats_VS(tj)
                    elif h == 9:
                        t_ffn_T(tj)
                    elif h == 11:
                        t_ffn_SV(tj)
            b_post(NH - 1)
            b_transpose_pair(NH // 2 - 1)
            if tj >= 0:
                t_tree(tj)
            wait("t", "s")
            wait("t", "gsu")
            for c in range(CH):
                bank = 4 + c % 2
                t_wait_bank(bank)
                for dc in range(CH):
                    T.matmul(pb[bank][:, :], owb[:, dc, c, :], B_ofm[:, dc, :],
                             start=(dc == 0), stop=False)
                T.matmul(pb[bank][:, :], ob_row[:, c * 128:(c + 1) * 128], ones512[:, :],
                         start=False, stop=True)
                mark("t")
                bank_done(bank)
                use_bank(bank, "v")
                wait("v", "t")
                wait("v", f"x1l{i % 2}")
                wait("v", "s")
                V.scalar_tensor_tensor(out=x2b[:, c, :], in0=pb[bank][:, :], scalar=1.0,
                                       in1=x1b[:, c, :], op0=ALU.mult, op1=ALU.add)
                mark("v")
                bank_consumed(bank, "v")
            if tj >= 0:
                t_out(tj)

        b_loads(0)
        for i in range(NT):
            b_attn(i)
            if i + 1 < NT:
                b_loads(i + 1)
        j = NT - 1
        t_cast(j)
        t_stats_T(j)
        t_stats_VS(j)
        t_ffn_T(j)
        t_ffn_SV(j)
        t_tree(j)
        t_out(j)

        if ctx.em == "q":
            Q.real.wait_ge(sems["ol"], ctx.cnt["ol"])

    with nc.Block() as block:
        def make_body(name):
            def body(eng):
                engines = {}
                for n in Ctx.ENGS:
                    real = eng if (n == name or (name == "g" and n == "p")) else None
                    engines[n] = Eng(real)
                walk(Ctx(name, engines, sems))
            return body
        block.sync(make_body("q"))
        block.gpsimd(make_body("g"))
        block.tensor(make_body("t"))
        block.vector(make_body("v"))
        block.scalar(make_body("s"))
    return nc


_NC_CACHE = None


def kernel(**inputs):
    global _NC_CACHE
    xs = np.ascontiguousarray(np.asarray(inputs["L0_attn_x"], dtype=np.float32))

    t = np.arange(S, dtype=np.float64)
    inv = 1.0 / (10000.0 ** (np.arange(0, 64, 2, dtype=np.float64) / 64.0))
    f = t[:, None] * inv[None, :]
    sin64 = np.sin(f)
    cos64 = np.cos(f)
    cos_t = np.empty((128, S), dtype=np.float32)
    sin_t = np.empty((128, S), dtype=np.float32)
    for r in range(128):
        cos_t[r] = cos64[:, (r % 64) % 32]
        sin_t[r] = sin64[:, (r % 64) % 32]
    P64 = np.zeros((64, 64), dtype=np.float32)
    for j in range(32):
        P64[j + 32, j] = -1.0
        P64[j, j + 32] = 1.0
    pmat = np.zeros((128, 128), dtype=np.float32)
    pmat[0:64, 0:64] = P64
    pmat[64:128, 64:128] = P64
    ident = np.eye(128, dtype=np.float32)

    common = {
        "conv_k": inputs["L0_attn_conv_k"], "conv_b": inputs["L0_attn_conv_b"],
        "rff_w": inputs["L0_attn_rff_w"], "rff_b": inputs["L0_attn_rff_b"],
        "proj_w": inputs["L0_attn_proj_w"], "proj_b": inputs["L0_attn_proj_b"],
        "omega": inputs["L0_attn_omega"], "landmarks": inputs["L0_attn_landmarks"],
        "out_w": inputs["L0_attn_out_w"], "out_b": inputs["L0_attn_out_b"],
        "ffn_win": inputs["L0_ffn_Win"], "ffn_bin": inputs["L0_ffn_bin_"],
        "ffn_w1": inputs["L0_ffn_W1"], "ffn_b1": inputs["L0_ffn_b1"],
        "ffn_w2": inputs["L0_ffn_W2"], "ffn_b2": inputs["L0_ffn_b2"],
        "ffn_wout": inputs["L0_ffn_Wout"], "ffn_bout": inputs["L0_ffn_bout"],
        "rope_c": cos_t, "rope_s": sin_t, "pmat": pmat, "ident": ident,
    }
    common = {k: np.ascontiguousarray(np.asarray(v, dtype=np.float32)) for k, v in common.items()}

    if _NC_CACHE is None:
        _NC_CACHE = build_program()
    nc = _NC_CACHE

    in_maps = []
    for b in range(NCORES):
        m = dict(common)
        m["x"] = np.ascontiguousarray(xs[b].T.reshape(CH, 128, S))
        in_maps.append(m)
    res = run_bass_kernel_spmd(nc, in_maps, core_ids=list(range(NCORES)))
    outs = []
    for b in range(NCORES):
        o = np.asarray(res.results[b]["out"])
        outs.append(o.reshape(D, S).T.astype(np.float32))
    return np.stack(outs, axis=0)
